# version 10
# baseline (speedup 1.0000x reference)
import contextlib
import numpy as np
import ml_dtypes
import concourse.bass as bass
import concourse.mybir as mybir
from concourse.bass_types import AP as _AP
from concourse.bass_utils import run_bass_kernel_spmd

F32 = mybir.dt.float32
BF16 = mybir.dt.bfloat16
AF = mybir.ActivationFunctionType
ALU = mybir.AluOpType
AX = mybir.AxisListType.X
bfdt = ml_dtypes.bfloat16

D = 4096
DIN = 8192
H = 128
PP = 64
GG = 8
CH = 256
NCH = 8
TPC = 2048
NT = 16
DIP = 18560
NOC = DIP // 128
CONVD = 10240
EPS = 1e-5
RSQD = float(1.0 / np.sqrt(128.0))

ENGS = ("sync", "scalar", "vector", "tensor", "gpsimd")
CSEM = {"sync": "c_sy", "scalar": "ac", "vector": "dv", "tensor": "pe", "gpsimd": "gp"}
QSEM = {"sync": "d_sy", "scalar": "d_sc", "gpsimd": "d_gp"}
NSLOT = 16

SCRATCH = {
    "zt":     ([TPC, DIN], F32),
    "xnt":    ([TPC, D], F32),
    "xbc":    ([CONVD, 2051], F32),
    "xdt":    ([TPC, DIN], BF16),
    "dxt":    ([TPC, DIN], F32),
    "bfm":    ([GG * 128, TPC], BF16),
    "cfm":    ([GG * 128, TPC], BF16),
    "yt":     ([TPC, DIN], F32),
    "expacs": ([TPC, H], F32),
    "cums":   ([NCH, H], F32),
    "gin":    ([128, DIN + 1], F32),
    "gout":   ([512, DIN + 1], F32),
    "goutx":  ([512, DIN + 1], F32),
    "gffn":   ([TPC, D], F32),
    "yfm":    ([NT * 64 * 128, 128], BF16),
    "x1":     ([TPC, D], F32),
    "wqe":    ([8, D, 128], BF16),
    "qf":     ([1024, TPC], BF16),
    "qkst":   ([1, 16], F32),
    "kvloc":  ([4097, 1024], BF16),
    "kvg":    ([4 * 4097, 1024], BF16),
    "kvgx":   ([4 * 4097, 1024], BF16),
    "g1":     ([TPC, D], F32),
    "attn":   ([TPC, 1024], F32),
    "out":    ([TPC, D], F32),
}

WSPEC = {
    "xh":      ([2051, D], F32),
    "wip":     ([NOC, 128, 32, 128], F32),
    "convw":   ([80, 128, 4], F32),
    "convb":   ([80, 128], F32),
    "dtbias":  ([H, 1], F32),
    "alog":    ([H, 1], F32),
    "dvec":    ([H, 1], F32),
    "normw":   ([1, DIN], F32),
    "wout":    ([DIN, D], F32),
    "woutt":   ([4, 64, 128, 1024], F32),
    "nwt":     ([128, 64], F32),
    "wb0":     ([D, D], F32),
    "wb0t":    ([32, 128, 32, 128], F32),
    "wg0":     ([D, 28], F32),
    "wla0":    ([D, 224], F32),
    "wlb0a":   ([128, D], F32),
    "wlb0b":   ([96, D], F32),
    "wqr":     ([4, D, 1024], F32),
    "wk":      ([D, 1024], F32),
    "wkt":     ([8, 128, 32, 128], F32),
    "wv":      ([D, 1024], F32),
    "wvt":     ([8, 128, 32, 128], F32),
    "wo":      ([1024, D], F32),
    "wb1":     ([D, D], F32),
    "wb1t":    ([32, 128, 32, 128], F32),
    "bb1":     ([1, D], F32),
    "wr1":     ([D, 8], F32),
    "br1":     ([1, 8], F32),
    "wla1":    ([D, 128], F32),
    "wlb1":    ([128, D], F32),
    "c_ident": ([128, 128], F32),
    "c_ones":  ([128, 128], F32),
    "c_mask":  ([2, 128, CH], F32),
    "c_trineg": ([128, 128], F32),
    "c_misc":  ([128, 4], F32),
    "relmask": ([128, 8], F32),
}


def ap_of(t):
    if isinstance(t, _AP):
        return t
    return t.ap() if callable(getattr(t, "ap", None)) else t[:, :]


def view(t, offset, dims):
    a = ap_of(t)
    return _AP(a.tensor, offset, [list(d) for d in dims])


class Op:
    __slots__ = ("eng", "fn", "deps", "sem", "need", "inc_k", "val", "seq")

    def __init__(self, eng, fn, sem, inc_k, seq):
        self.eng, self.fn, self.sem, self.inc_k, self.seq = eng, fn, sem, inc_k, seq
        self.deps = []
        self.need = False
        self.val = None


class Prog:
    def __init__(self):
        self.nc = bass.Bass("TRN2", target_bir_lowering=False, debug=False)
        self.ops = {e: [] for e in ENGS}
        self.sb = {}
        self.dramd = {}
        self._stack = None
        self.ext_in = set()
        self.ext_out = set()
        self.last_writer = {}
        self.readers = {}
        self._seq = 0
        self._slot = {q: 0 for q in QSEM}
        self._slot_last = {q: [None] * NSLOT for q in QSEM}

    def sbuf(self, name, shape, dtype):
        if name not in self.sb:
            self.sb[name] = self._stack.enter_context(self.nc.sbuf_tensor(name, shape, dtype))
        return self.sb[name]

    def psum(self, name, shape, dtype=F32):
        if name not in self.sb:
            self.sb[name] = self._stack.enter_context(self.nc.psum_tensor(name, shape, dtype))
        return self.sb[name]

    def dram(self, name):
        if name not in self.dramd:
            shape, dtype = SCRATCH[name] if name in SCRATCH else WSPEC[name]
            kind = ("ExternalInput" if name in self.ext_in else
                    "ExternalOutput" if name in self.ext_out else "Internal")
            self.dramd[name] = self.nc.dram_tensor("t_" + name, list(shape), dtype, kind=kind)
        return self.dramd[name]

    def _plan(self, eng, fn, reads, writes, sem, inc_k, extra_deps=()):
        self._seq += 1
        o = Op(eng, fn, sem, inc_k, self._seq)
        deps = {}

        def add(d):
            if d is None:
                return
            if eng == "tensor" and d.eng == "tensor":
                return
            cur = deps.get(d.sem)
            if cur is None or cur.seq < d.seq:
                deps[d.sem] = d

        rn = [ap.tensor.name for ap in reads]
        wn = [ap.tensor.name for ap in writes]
        for t in rn:
            add(self.last_writer.get(t))
        for t in wn:
            add(self.last_writer.get(t))
            for d in self.readers.get(t, {}).values():
                add(d)
        for d in extra_deps:
            add(d)
        for d in deps.values():
            d.need = True
        o.deps = list(deps.values())
        self.ops[eng].append(o)
        for t in wn:
            self.last_writer[t] = o
            self.readers[t] = {}
        for t in rn:
            self.readers.setdefault(t, {})[eng] = o
        return o

    def op(self, eng, meth, *args, extra_deps=(), **kw):
        writes, reads = [], []
        seen_w = False
        for a in args:
            if isinstance(a, _AP):
                if not seen_w:
                    writes.append(a)
                    seen_w = True
                else:
                    reads.append(a)
        for k, v in kw.items():
            if isinstance(v, _AP):
                (writes if k in ("accum_out", "out") else reads).append(v)

        def fn(e, _m=meth, _a=args, _k=kw):
            return getattr(e, _m)(*_a, **_k)
        return self._plan(eng, fn, reads, writes, CSEM[eng], 1, extra_deps)

    def dma(self, q, out, in_, extra_deps=(), **kw):
        def fn(e, _o=out, _i=in_, _k=kw):
            return e.dma_start(out=_o, in_=_i, **_k)
        slot = self._slot[q] % NSLOT
        self._slot[q] += 1
        prev = self._slot_last[q][slot]
        deps = list(extra_deps) + ([prev] if prev is not None else [])
        o = self._plan(q, fn, [in_], [out], QSEM[q] + str(slot), 16, deps)
        o.need = True
        self._slot_last[q][slot] = o
        return o

    def collective(self, kind, alu, groups, in_ap, out_ap, extra_deps=()):
        def fn(e, _i=in_ap, _o=out_ap):
            return e.collective_compute(kind, alu, replica_groups=groups, ins=[_i.opt()], outs=[_o.opt()])
        return self._plan("gpsimd", fn, [in_ap], [out_ap], "gp", 1, extra_deps)

    def lower(self):
        semnames = sorted(set(CSEM.values()) |
                          {q + str(i) for q in QSEM.values() for i in range(NSLOT)})
        cnt = {s: 0 for s in semnames}
        for eng in ENGS:
            for o in self.ops[eng]:
                if o.need:
                    cnt[o.sem] += o.inc_k
                    o.val = cnt[o.sem]
        bar = [(s, cnt[s]) for s in semnames if cnt[s] > 0]
        sems = {n: self._stack.enter_context(self.nc.semaphore(n)) for n in semnames}
        block = self._stack.enter_context(self.nc.Block())
        for eng in ENGS:
            ops = self.ops[eng]
            if eng == "sync":
                pass
            if not ops and eng != "sync":
                continue

            def mk(ops, fin):
                def f(e):
                    last = {}
                    for o in ops:
                        for d in o.deps:
                            if last.get(d.sem, 0) < d.val:
                                e.wait_ge(sems[d.sem], d.val)
                                last[d.sem] = d.val
                        instr = o.fn(e)
                        if o.need:
                            instr.then_inc(sems[o.sem], o.inc_k)
                    for s, v in fin:
                        if last.get(s, 0) < v:
                            e.wait_ge(sems[s], v)
                return f

            getattr(block, eng)(mk(ops, bar if eng == "sync" else []))


def build_launch(builders, ext_in, ext_out, dbg=()):
    P = Prog()
    P.ext_in = set(ext_in)
    P.ext_out = set(ext_out)
    with contextlib.ExitStack() as st:
        P._stack = st
        consts(P)
        for b in builders:
            b(P)
        for name, get_ap in dbg:
            ap = get_ap(P)
            t = P.nc.dram_tensor("t_dbg_" + name, list(ap.shape), ap.dtype, kind="ExternalOutput")
            P.dramd["dbg_" + name] = t
            P.dma("scalar", t.ap(), ap)
        P.lower()
    return P.nc


def consts(P):
    P.sbuf("c_idb", [128, 128], BF16)
    P.sbuf("c_idf", [128, 128], F32)
    P.sbuf("c_onec", [128, 1], BF16)
    P.sbuf("c_oner", [1, 128], BF16)
    P.sbuf("c_onerf", [1, 128], F32)
    P.sbuf("c_miscs", [128, 4], F32)
    for j in range(8):
        P.sbuf(f"ar{j}", [128, 8192], BF16)
    P.sbuf("sc_a", [128, 4096], F32)
    P.sbuf("sc_b", [128, 4096], F32)
    P.sbuf("sc_c", [128, 2051], F32)
    P.sbuf("sc_d", [128, 2051], F32)
    P.sbuf("wb0", [128, 2048], BF16)
    P.sbuf("wb1", [128, 2048], BF16)
    for i in range(4):
        P.sbuf(f"ip_ev{i}", [128, 512], F32)
    for i in range(7):
        P.psum(f"PS{i}", [128, 512], F32)
    P.psum("PT0", [128, 1024], BF16)
    ci = P.dram("c_ident").ap()
    co = P.dram("c_ones").ap()
    P.dma("sync", P.sb["c_idf"][:, :], ci)
    P.dma("gpsimd", P.sb["c_idb"][:, :], ci)
    P.dma("gpsimd", P.sb["c_onec"][:, :], co[:, 0:1], allow_slow_non_contiguous=True)
    P.dma("gpsimd", P.sb["c_oner"][:, :], co[0:1, :])
    P.dma("sync", P.sb["c_onerf"][:, :], co[0:1, :])
    P.dma("sync", P.sb["c_miscs"][:, :], P.dram("c_misc").ap())


def EPSC(P, nr=128):
    return P.sb["c_miscs"][0:nr, 0:1]


def ONEC(P, nr=128):
    return P.sb["c_miscs"][0:nr, 1:2]


def bfv(P, name):
    return P.sb[name][:, :].bitcast(BF16)


def xn_rhs(P, kc, c0, n):
    ar = P.sb[f"ar{kc // 4}"]
    return view(ar, (kc % 4) * 2048 + c0, [[8192, 128], [1, n]])


def b_l0_ln(P):
    xnh = P.sbuf("xnh", [128, 128], BF16)
    xh = P.dram("xh").ap()
    xt = P.sb["sc_a"]
    sqd = P.sb["sc_b"]
    xb = bfv(P, "sc_b")
    s_mu = P.sbuf("ln_mu", [128, 1], F32)
    s_nm = P.sbuf("ln_nm", [128, 1], F32)
    s_s2 = P.sbuf("ln_s2", [128, 1], F32)
    s_rs = P.sbuf("ln_rs", [128, 1], F32)
    ptr = P.psum("PT0", [128, 1024], BF16)

    for tt in range(-1, NT):
        r0, nr = (0, 3) if tt < 0 else (3 + tt * 128, 128)
        P.dma("sync", xt[0:nr, :], xh[r0:r0 + nr, :])
        P.op("vector", "reduce_sum", s_mu[0:nr, :], xt[0:nr, :], axis=AX)
        P.op("vector", "tensor_scalar", s_mu[0:nr, :], s_mu[0:nr, :], 1.0 / D, None, op0=ALU.mult)
        P.op("vector", "tensor_scalar", s_nm[0:nr, :], s_mu[0:nr, :], -1.0, None, op0=ALU.mult)
        P.op("scalar", "activation", sqd[0:nr, :], xt[0:nr, :], AF.Square, bias=s_nm[0:nr, :],
             accum_out=s_s2[0:nr, :])
        P.op("vector", "tensor_scalar", s_s2[0:nr, :], s_s2[0:nr, :], 1.0 / D, None, op0=ALU.mult)
        P.op("scalar", "activation", s_rs[0:nr, :], s_s2[0:nr, :], AF.Sqrt, bias=EPSC(P, nr))
        P.op("vector", "reciprocal", s_rs[0:nr, :], s_rs[0:nr, :])
        P.op("vector", "tensor_scalar", xb[0:nr, 0:4096], xt[0:nr, :], s_mu[0:nr, :], s_rs[0:nr, :],
             op0=ALU.subtract, op1=ALU.mult)
        if tt >= 0:
            P.dma("gpsimd", P.dram("xnt").ap()[tt * 128:(tt + 1) * 128, :], xb[0:nr, 0:4096])
        for half in range(8):
            for j in range(4):
                kc = half * 4 + j
                P.op("tensor", "transpose", ptr[:, j * 128:j * 128 + nr],
                     xb[0:nr, kc * 128:(kc + 1) * 128], P.sb["c_idb"][0:nr, 0:nr])
            if tt < 0:
                for j in range(4):
                    kc = half * 4 + j
                    P.op("vector", "tensor_copy", xnh[:, kc * 4:kc * 4 + 3], ptr[:, j * 128:j * 128 + 3])
            else:
                ar = P.sb[f"ar{half // 1 // 1}"]
                ar = P.sb[f"ar{half}"]
                dst = view(ar, tt * 128, [[8192, 128], [2048, 4], [1, 128]])
                src = view(ptr, 0, [[1024, 128], [128, 4], [1, 128]])
                P.op("vector", "tensor_copy", dst, src)


def b_l0_inproj(P):
    xnh = P.sb["xnh"]
    wip = P.dram("wip").ap()
    ztd = P.dram("zt").ap()
    xbcd = P.dram("xbc").ap()
    wb = [P.sb["wb0"], P.sb["wb1"]]
    ev = [P.sb[f"ip_ev{i}"] for i in range(4)]
    ztr = P.sbuf("ip_ztr", [128, 512], F32)
    hev = P.sbuf("ip_hev", [128, 4], F32)
    dtr = P.sb["sc_c"]
    pm = [P.psum(f"PS{i}", [128, 512], F32) for i in range(4)]
    ptf = P.psum("PS4", [128, 512], F32)

    for oc in range(NOC):
        P.dma("gpsimd", wb[0][:, :], wip[oc, :, 0:16, :])
        P.dma("gpsimd", wb[1][:, :], wip[oc, :, 16:32, :])
        is_z, is_dt = oc < 64, oc == NOC - 1
        for ts4 in range(4):
            pi = ts4
            for kc in range(32):
                P.op("tensor", "matmul", pm[pi][:, :],
                     wb[kc // 16][:, (kc % 16) * 128:(kc % 16 + 1) * 128],
                     xn_rhs(P, kc, ts4 * 512, 512),
                     start=(kc == 0), stop=(kc == 31))
            if is_dt:
                P.op("vector", "tensor_copy", dtr[:, ts4 * 512:(ts4 + 1) * 512], pm[pi][:, :])
            elif is_z:
                P.op("vector", "tensor_copy", ev[pi][:, :], pm[pi][:, :])
                for j in range(4):
                    P.op("tensor", "transpose", ptf[:, j * 128:(j + 1) * 128],
                         ev[pi][:, j * 128:(j + 1) * 128], P.sb["c_idf"][:, :])
                P.op("vector", "tensor_copy", ztr[:, :], ptf[:, :])
                dst = view(ztd, (ts4 * 512) * DIN + oc * 128, [[DIN, 128], [DIN * 128, 4], [1, 128]])
                src = view(ztr, 0, [[512, 128], [128, 4], [1, 128]])
                P.dma("scalar", dst, src)
            else:
                P.op("vector", "tensor_copy", ev[pi][:, :], pm[pi][:, :])
                r0 = (oc - 64) * 128
                P.dma("scalar", xbcd[r0:r0 + 128, 3 + ts4 * 512:3 + (ts4 + 1) * 512], ev[pi][:, :])
        if (not is_z) and (not is_dt):
            for kc in range(32):
                P.op("tensor", "matmul", pm[0][:, 0:3],
                     wb[kc // 16][:, (kc % 16) * 128:(kc % 16 + 1) * 128],
                     xnh[:, kc * 4:kc * 4 + 3], start=(kc == 0), stop=(kc == 31))
            P.op("vector", "tensor_copy", hev[:, 0:3], pm[0][:, 0:3])
            r0 = (oc - 64) * 128
            P.dma("scalar", xbcd[r0:r0 + 128, 0:3], hev[:, 0:3])


def b_l0_dt(P):
    cumd = P.dram("cums").ap()
    dtb = P.sbuf("dtb", [128, 1], F32)
    alog = P.sbuf("alogv", [128, 1], F32)
    dvec = P.sbuf("dvecv", [128, 1], F32)
    dtr = P.sb["sc_c"]
    tmp = P.sb["sc_a"]
    acs2 = P.sb["sc_a"]
    acs = P.sbuf("acs_fm", [128, TPC], F32)
    dtt = P.sb["sc_b"]
    dbc = P.sbuf("d_bc", [128, 128], F32)
    cdec = P.sbuf("cdec", [128, 9], F32)
    rowt = P.sbuf("dt_rowt", [1, 128], F32)
    ptf = P.sb["PS4"]
    ptg = P.sb["PS3"]
    P.dma("sync", dtb[:, :], P.dram("dtbias").ap())
    P.dma("sync", alog[:, :], P.dram("alog").ap())
    P.dma("sync", dvec[:, :], P.dram("dvec").ap())

    T = lambda c0, n: tmp[:, c0:c0 + n]
    P.op("scalar", "activation", T(0, 2048), dtr[:, 0:2048], AF.Abs, bias=dtb[:, :])
    P.op("scalar", "activation", T(0, 2048), T(0, 2048), AF.Exp, scale=-1.0)
    P.op("scalar", "activation", T(0, 2048), T(0, 2048), AF.Ln, bias=ONEC(P))
    P.op("vector", "tensor_scalar", dtr[:, 0:2048], dtr[:, 0:2048], dtb[:, :], 0.0, op0=ALU.add, op1=ALU.max)
    P.op("vector", "tensor_add", dtr[:, 0:2048], dtr[:, 0:2048], T(0, 2048))
    P.op("scalar", "activation", alog[:, :], alog[:, :], AF.Exp)
    P.op("vector", "tensor_scalar", alog[:, :], alog[:, :], -1.0, None, op0=ALU.mult)
    P.op("vector", "tensor_scalar", acs[:, :], dtr[:, 0:2048], alog[:, :], None, op0=ALU.mult)
    for c in range(NCH):
        o = c * CH
        in_acs = True
        for sh in (1, 2, 4, 8, 16, 32, 64, 128):
            if in_acs:
                s_t, s_o, d_t, d_o = acs, o, acs2, 2048
            else:
                s_t, s_o, d_t, d_o = acs2, 2048, acs, o
            P.op("vector", "tensor_copy", d_t[:, d_o:d_o + sh], s_t[:, s_o:s_o + sh])
            P.op("vector", "tensor_add", d_t[:, d_o + sh:d_o + CH], s_t[:, s_o + sh:s_o + CH],
                 s_t[:, s_o:s_o + CH - sh])
            in_acs = not in_acs
        assert in_acs
    P.op("vector", "tensor_copy", cdec[:, 0:8], view(acs, CH - 1, [[TPC, 128], [CH, 8]]))
    P.op("vector", "memset", T(0, 1), 0.0)
    for z in range(1, 9):
        P.op("vector", "tensor_add", T(z, 1), T(z - 1, 1), cdec[:, z - 1:z])
    P.op("vector", "tensor_copy", cdec[:, 8:9], T(8, 1))
    P.op("scalar", "activation", cdec[:, :], cdec[:, :], AF.Exp)
    P.op("scalar", "activation", T(0, 9), T(0, 9), AF.Exp)
    P.op("tensor", "transpose", ptf[0:8, 0:128], T(0, 8), P.sb["c_idf"][:, :])
    P.op("vector", "tensor_copy", P.sb["ip_ev0"][0:8, 0:128], ptf[0:8, 0:128])
    P.dma("scalar", cumd[:, :], P.sb["ip_ev0"][0:8, 0:128])
    P.op("tensor", "transpose", ptg[0:1, 0:128], dvec[:, 0:1], P.sb["c_idf"][:, :])
    P.op("vector", "tensor_copy", rowt[:, :], ptg[0:1, 0:128])
    P.op("tensor", "matmul", ptg[:, 128:256], P.sb["c_onerf"][:, :], rowt[:, :], start=True, stop=True)
    P.op("vector", "tensor_copy", dbc[:, :], ptg[:, 128:256])
    for tt in range(NT):
        P.op("tensor", "transpose", ptf[:, 0:128], dtr[:, tt * 128:(tt + 1) * 128], P.sb["c_idf"][:, :])
        P.op("vector", "tensor_copy", dtt[:, tt * 128:(tt + 1) * 128], ptf[:, 0:128])


def b_l0_conv(P):
    xbcd = P.dram("xbc").ap()
    xdtd = P.dram("xdt").ap()
    dxtd = P.dram("dxt").ap()
    bfmd = P.dram("bfm").ap()
    cfmd = P.dram("cfm").ap()
    cw = P.sbuf("cv_w", [128, 320], F32)
    cb = P.sbuf("cv_b", [128, 80], F32)
    cin = [P.sb["sc_c"], P.sb["sc_d"]]
    cacc = P.sb["sc_a"]
    cbf = bfv(P, "sc_a")
    xev = [P.sb["ip_ev0"][:, :].bitcast(BF16)[:, 0:128], P.sb["ip_ev1"][:, :].bitcast(BF16)[:, 0:128]]
    dev = [P.sb["ip_ev2"][:, 0:128], P.sb["ip_ev3"][:, 0:128]]
    pbf = P.sb["PT0"]
    dtt = P.sb["sc_b"]
    dbc = P.sb["d_bc"]
    P.dma("sync", cw[:, :], view(P.dram("convw"), 0, [[4, 128], [512, 80], [1, 4]]))
    P.dma("sync", cb[:, :], view(P.dram("convb"), 0, [[1, 128], [128, 80]]), allow_slow_non_contiguous=True)
    P.dma("sync", cin[0][:, 0:2051], xbcd[0:128, :])
    for fc in range(80):
        bb = fc % 2
        if fc + 1 < 80:
            P.dma("sync", cin[(fc + 1) % 2][:, 0:2051], xbcd[(fc + 1) * 128:(fc + 2) * 128, :])
        P.op("vector", "tensor_scalar", cacc[:, 0:2048], cin[bb][:, 0:TPC], cw[:, fc * 4:fc * 4 + 1],
             None, op0=ALU.mult)
        for k in range(1, 4):
            P.op("vector", "scalar_tensor_tensor", cacc[:, 0:2048], cin[bb][:, k:k + TPC],
                 cw[:, fc * 4 + k:fc * 4 + k + 1], cacc[:, 0:2048], op0=ALU.mult, op1=ALU.add)
        P.op("scalar", "activation", cacc[:, 0:2048], cacc[:, 0:2048], AF.Silu, bias=cb[:, fc:fc + 1])
        P.op("vector", "tensor_copy", cbf[:, 4096:6144], cacc[:, 0:2048])
        if fc < 64:
            h0, h1 = 2 * fc, 2 * fc + 1
            for tb8 in range(2):
                for j in range(8):
                    tt = tb8 * 8 + j
                    P.op("tensor", "transpose", pbf[:, j * 128:(j + 1) * 128],
                         cbf[:, 4096 + tt * 128:4096 + (tt + 1) * 128], P.sb["c_idb"][:, :])
                for j in range(8):
                    tt = tb8 * 8 + j
                    tb = tt % 2
                    P.op("vector", "tensor_scalar", xev[tb][:, 0:64], pbf[:, j * 128:j * 128 + 64],
                         dtt[:, tt * 128 + h0:tt * 128 + h0 + 1], None, op0=ALU.mult)
                    P.op("vector", "tensor_scalar", xev[tb][:, 64:128], pbf[:, j * 128 + 64:j * 128 + 128],
                         dtt[:, tt * 128 + h1:tt * 128 + h1 + 1], None, op0=ALU.mult)
                    P.op("vector", "tensor_scalar", dev[tb][:, 0:64], pbf[:, j * 128:j * 128 + 64],
                         dbc[:, h0:h0 + 1], None, op0=ALU.mult)
                    P.op("vector", "tensor_scalar", dev[tb][:, 64:128], pbf[:, j * 128 + 64:j * 128 + 128],
                         dbc[:, h1:h1 + 1], None, op0=ALU.mult)
                    P.dma("scalar", xdtd[tt * 128:(tt + 1) * 128, fc * 128:(fc + 1) * 128], xev[tb])
                    P.dma("scalar", dxtd[tt * 128:(tt + 1) * 128, fc * 128:(fc + 1) * 128], dev[tb])
        else:
            r0 = (fc - 64) * 128 if fc < 72 else (fc - 72) * 128
            dst = bfmd if fc < 72 else cfmd
            P.dma("scalar", dst[r0:r0 + 128, :], cbf[:, 4096:6144])


def tile4(Wt):
    K, M = Wt.shape
    return np.ascontiguousarray(Wt.reshape(K // 128, 128, M // 128, 128).transpose(2, 1, 0, 3))


def host_weights(inputs):
    W = {}
    wip = inputs["L0_attn_in_proj_w"]
    W["wip"] = np.ascontiguousarray(wip.reshape(NOC, 128, 32, 128).transpose(0, 3, 2, 1))
    W["convw"] = np.ascontiguousarray(inputs["L0_attn_conv_w"].reshape(80, 128, 4))
    W["convb"] = np.ascontiguousarray(inputs["L0_attn_conv_b"].reshape(80, 128))
    W["dtbias"] = np.ascontiguousarray(inputs["L0_attn_dt_bias"].reshape(H, 1))
    W["alog"] = np.ascontiguousarray(inputs["L0_attn_A_log"].reshape(H, 1))
    W["dvec"] = np.ascontiguousarray(inputs["L0_attn_D"].reshape(H, 1))
    W["normw"] = np.ascontiguousarray(inputs["L0_attn_norm_w"].reshape(1, DIN))
    W["wout"] = np.ascontiguousarray(inputs["L0_attn_out_proj_w"].T)
    W["woutt"] = np.ascontiguousarray(W["wout"].reshape(64, 128, 4, 1024).transpose(2, 0, 1, 3))
    W["nwt"] = np.ascontiguousarray(inputs["L0_attn_norm_w"].reshape(64, 128).T)
    W["wb0"] = np.ascontiguousarray(inputs["L0_ffn_base_W"].T)
    W["wb0t"] = tile4(W["wb0"])
    W["wg0"] = np.ascontiguousarray(inputs["L0_ffn_gate_W"].T)
    W["wla0"] = np.ascontiguousarray(inputs["L0_ffn_lora_A"].transpose(2, 0, 1).reshape(D, 224))
    lb0 = np.ascontiguousarray(inputs["L0_ffn_lora_B"].transpose(0, 2, 1).reshape(224, D))
    W["wlb0a"] = np.ascontiguousarray(lb0[0:128])
    W["wlb0b"] = np.ascontiguousarray(lb0[128:224])
    W["wqr"] = np.ascontiguousarray(
        inputs["L1_attn_wq"].reshape(D, 8, 4, 128).transpose(2, 0, 1, 3).reshape(4, D, 1024))
    W["wk"] = np.ascontiguousarray(inputs["L1_attn_wk"])
    W["wkt"] = tile4(W["wk"])
    W["wv"] = np.ascontiguousarray(inputs["L1_attn_wv"])
    W["wvt"] = tile4(W["wv"])
    W["wo"] = np.ascontiguousarray(inputs["L1_attn_wo"])
    W["wb1"] = np.ascontiguousarray(inputs["L1_ffn_W_base"].T)
    W["wb1t"] = tile4(W["wb1"])
    W["bb1"] = np.ascontiguousarray(inputs["L1_ffn_b_base"].reshape(1, D))
    W["wr1"] = np.ascontiguousarray(inputs["L1_ffn_W_router"].T)
    W["br1"] = np.ascontiguousarray(inputs["L1_ffn_b_router"].reshape(1, 8))
    W["wla1"] = np.ascontiguousarray(inputs["L1_ffn_lora_A"].transpose(1, 0, 2).reshape(D, 128))
    W["wlb1"] = np.ascontiguousarray(inputs["L1_ffn_lora_B"].reshape(128, D))
    W["c_ident"] = np.eye(128, dtype=np.float32)
    W["c_ones"] = np.ones((128, 128), dtype=np.float32)
    msk = np.zeros((2, 128, CH), dtype=np.float32)
    s = np.arange(128)[:, None]
    l = np.arange(CH)[None, :]
    msk[0] = (s <= l).astype(np.float32)
    msk[1] = ((s + 128) <= l).astype(np.float32)
    W["c_mask"] = msk
    tn = np.zeros((128, 128), dtype=np.float32)
    tn[np.arange(128)[:, None] > np.arange(128)[None, :]] = -1e30
    W["c_trineg"] = tn
    cmisc = np.zeros((128, 4), dtype=np.float32)
    cmisc[:, 0] = EPS
    cmisc[:, 1] = 1.0
    cmisc[:, 2] = RSQD
    W["c_misc"] = cmisc
    return W


def host_percore(inputs, core):
    seq, pos = core // 4, core % 4
    x = inputs["L0_attn_hidden_states"]
    xh = np.zeros((2051, D), dtype=np.float32)
    lo = pos * TPC
    if pos == 0:
        xh[3:] = x[seq, 0:TPC]
    else:
        xh[:] = x[seq, lo - 3:lo + TPC]
    rel = np.zeros((128, 8), dtype=np.float32)
    for r in range(4):
        rel[:, r] = 1.0 if r < pos else 0.0
        rel[:, 4 + r] = 1.0 - rel[:, r]
    return {"xh": np.ascontiguousarray(xh), "relmask": rel}


def f32v(P, name):
    return P.sb[name][:, :].bitcast(F32)


def arf(P, j):
    return P.sb[f"ar{j}"][:, :].bitcast(F32)


def bcast_ap(base_ap, inner_rep, n_outer, outer_stride, offset=0):
    return _AP(base_ap.tensor, base_ap.offset + offset,
               [list(base_ap.ap[0]), [outer_stride, n_outer], [0, inner_rep]])


def b_l0_ffn(P):
    gffnd = P.dram("gffn").ap()
    wla = P.sb["sc_a"][:, :].bitcast(BF16)
    wg = P.sb["sc_b"][:, :].bitcast(BF16)
    lba = P.sb["sc_a"][:, :].bitcast(BF16)
    ws = [P.sb["sc_c"][:, :].bitcast(BF16), P.sb["sc_d"][:, :].bitcast(BF16)]
    m1 = P.sbuf("f_m1", [128, 1], F32)
    nm1 = P.sbuf("f_nm1", [128, 1], F32)
    m2 = P.sbuf("f_m2", [128, 1], F32)
    ps = [P.sb["PS0"], P.sb["PS1"], P.sb["PS2"], P.sb["PS3"]]
    ps4 = P.sb["PS4"]
    ptr = P.sb["PT0"]
    hf = [P.sb["wb0"], P.sb["wb1"]]
    ev = [P.sb[f"ip_ev{i}"] for i in range(4)]
    hp = ev[2][:, 0:224]
    hb = ev[3][:, :].bitcast(BF16)[:, 0:224]
    lg = ev[0][:, 0:28]
    gt = ev[0][:, 32:60]

    P.dma("gpsimd", wla[:, 0:7168].rearrange("p (k r) -> p k r", k=32),
          view(P.dram("wla0"), 0, [[224, 128], [128 * 224, 32], [1, 224]]))
    P.dma("gpsimd", wg[:, 0:896].rearrange("p (k r) -> p k r", k=32),
          view(P.dram("wg0"), 0, [[28, 128], [128 * 28, 32], [1, 28]]))

    for tt in range(NT):
        for kc in range(32):
            P.op("tensor", "matmul", ps4[:, 0:28], xn_rhs(P, kc, tt * 128, 128),
                 wg[:, kc * 28:(kc + 1) * 28],
                 start=(kc == 0), stop=(kc == 31))
        for kc in range(32):
            P.op("tensor", "matmul", ps4[:, 32:256], xn_rhs(P, kc, tt * 128, 128),
                 wla[:, kc * 224:(kc + 1) * 224],
                 start=False, stop=(kc == 31))
        P.op("vector", "tensor_copy", lg, ps4[:, 0:28])
        P.op("vector", "tensor_copy", hp, ps4[:, 32:256])
        P.op("vector", "reduce_max", m1[:, :], lg, axis=AX)
        P.op("vector", "tensor_scalar", nm1[:, :], m1[:, :], -1.0, None, op0=ALU.mult)
        P.op("vector", "tensor_scalar", gt, lg, m1[:, :], None, op0=ALU.is_equal)
        P.op("vector", "scalar_tensor_tensor", gt, gt, -1e30, lg, op0=ALU.mult, op1=ALU.add)
        P.op("vector", "reduce_max", m2[:, :], gt, axis=AX)
        P.op("vector", "tensor_scalar", gt, lg, m2[:, :], None, op0=ALU.is_ge)
        P.op("scalar", "activation", lg, lg, AF.Exp, bias=nm1[:, :])
        P.op("vector", "tensor_mul", lg, lg, gt)
        P.op("vector", "reduce_sum", m2[:, :], lg, axis=AX)
        P.op("vector", "reciprocal", m2[:, :], m2[:, :])
        P.op("vector", "tensor_scalar", gt, lg, m2[:, :], 2.0, op0=ALU.mult, op1=ALU.mult)
        gv = _AP(gt.tensor, gt.offset, [list(gt.ap[0]), [1, 28], [0, 8]])
        P.op("vector", "tensor_tensor", hb.rearrange("p (e r) -> p e r", e=28),
             hp.rearrange("p (e r) -> p e r", e=28), gv, op=ALU.mult)
        P.op("tensor", "transpose", ptr[:, 0:128], hb[:, 0:128], P.sb["c_idb"][:, :])
        P.op("tensor", "transpose", ptr[0:96, 128:256], hb[:, 128:224], P.sb["c_idb"][:, :])
        P.op("vector", "tensor_copy", hf[0][:, tt * 128:(tt + 1) * 128], ptr[:, 0:128])
        P.op("vector", "tensor_copy", hf[1][0:96, tt * 128:(tt + 1) * 128], ptr[0:96, 128:256])

    P.dma("gpsimd", lba[:, 0:4096], P.dram("wlb0a").ap())
    P.dma("gpsimd", lba[0:96, 4096:8192], P.dram("wlb0b").ap())
    wbt = P.dram("wb0t").ap()
    for og in range(32):
        sb = og % 2
        P.dma("gpsimd", ws[sb][:, 0:4096], wbt[og])
        for tt in range(NT):
            pp = ps[tt % 4]
            for kc in range(32):
                P.op("tensor", "matmul", pp[:, 0:128], xn_rhs(P, kc, tt * 128, 128),
                     ws[sb][:, kc * 128:(kc + 1) * 128],
                     start=(kc == 0), stop=False)
            P.op("tensor", "matmul", pp[:, 0:128], hf[0][:, tt * 128:(tt + 1) * 128],
                 lba[:, og * 128:(og + 1) * 128], start=False, stop=False)
            P.op("tensor", "matmul", pp[:, 0:128], hf[1][0:96, tt * 128:(tt + 1) * 128],
                 lba[0:96, 4096 + og * 128:4096 + (og + 1) * 128], start=False, stop=True)
            gev = ev[tt % 4]
            P.op("scalar", "activation", gev[:, 0:128], pp[:, 0:128], AF.Silu)
            P.dma("scalar", gffnd[tt * 128:(tt + 1) * 128, og * 128:(og + 1) * 128], gev[:, 0:128])


def b_l0_ssd(P):
    ytd = P.dram("yt").ap()
    ead = P.dram("expacs").ap()
    bfmd = P.dram("bfm").ap()
    cfmd = P.dram("cfm").ap()
    xdtd = P.dram("xdt").ap()
    gind = P.dram("gin").ap()
    goutd = P.dram("gout").ap()
    acs = P.sb["acs_fm"]
    cdec = P.sb["cdec"]
    idb = P.sb["c_idb"]
    idf = P.sb["c_idf"]
    trn = P.sbuf("c_trn", [128, 128], BF16)
    P.dma("gpsimd", trn[:, :], P.dram("c_trineg").ap())
    ar = [P.sb[f"ar{j}"] for j in range(8)]
    ev = [P.sb[f"ip_ev{i}"] for i in range(4)]
    PS = [P.sb[f"PS{i}"] for i in range(7)]
    PT0 = P.sb["PT0"]
    A5 = dict(bch=0, cch=2048, bt0=4096, bt1=5120, mt0=6144, mt1a=6400, mt0b=6528, mt1b=6784,
              ah=6912, al=7168, ds=7424)
    A6 = dict(G=0, nacs=3072, eacs=3328, cdb=3584, st1=3712)

    def a5(c0, n):
        return view(ar[5], c0, [[8192, 128], [1, n]])

    def a6(c0, n, p=128):
        return _AP(arf(P, 6).tensor, c0, [[4096, p], [1, n]]) if False else arf(P, 6)[:, c0:c0 + n]

    for c in range(NCH):
        o = c * CH
        P.dma("sync", a5(A5["bch"], 2048).rearrange("p (g s) -> p g s", g=8),
              view(P.dram("bfm"), o, [[2048, 128], [262144, 8], [1, 256]]))
        P.dma("sync", a5(A5["cch"], 2048).rearrange("p (g s) -> p g s", g=8),
              view(P.dram("cfm"), o, [[2048, 128], [262144, 8], [1, 256]]))
        P.dma("sync", ar[3][:, :], xdtd[o:o + 128, :])
        P.dma("sync", ar[4][:, :], xdtd[o + 128:o + 256, :])
        P.op("vector", "tensor_copy", a5(A5["ah"], 256), acs[:, o:o + CH])
        P.op("vector", "tensor_copy", ev[0][:, 0:256], a5(A5["ah"], 256))
        P.op("vector", "tensor_sub", ev[0][:, 256:512], acs[:, o:o + CH], ev[0][:, 0:256])
        P.op("vector", "tensor_copy", a5(A5["al"], 256), ev[0][:, 256:512])
        for a in range(2):
            P.op("tensor", "transpose", PS[4][:, a * 128:(a + 1) * 128],
                 acs[:, o + a * 128:o + (a + 1) * 128], idf[:, :])
        P.op("vector", "tensor_scalar", a6(A6["nacs"], 256), PS[4][:, 0:256], -1.0, None, op0=ALU.mult)
        P.op("scalar", "activation", a6(A6["eacs"], 256), PS[4][:, 0:256], AF.Exp)
        for a in range(2):
            P.dma("scalar", ead[o + a * 128:o + (a + 1) * 128, :],
                  a6(A6["eacs"] + a * 128, 128))
        P.op("vector", "tensor_scalar", a6(A6["st1"], 256), acs[:, o:o + CH],
             acs[:, o + CH - 1:o + CH], None, op0=ALU.subtract)
        P.op("scalar", "activation", a6(A6["st1"], 256), a6(A6["st1"], 256), AF.Exp, scale=-1.0)
        for a in range(2):
            P.op("tensor", "transpose", PS[4][:, 256 + a * 128:256 + (a + 1) * 128],
                 a6(A6["st1"] + a * 128, 128), idf[:, :])
        P.op("vector", "tensor_copy", a5(A5["ds"], 256), PS[4][:, 256:512])
        P.op("tensor", "transpose", PS[4][0:1, 0:128], cdec[:, c:c + 1], idf[:, :])
        P.op("vector", "tensor_copy", P.sb["dt_rowt"][:, :], PS[4][0:1, 0:128])
        P.op("tensor", "matmul", PS[4][:, 128:256], P.sb["c_onerf"][:, :], P.sb["dt_rowt"][:, :],
             start=True, stop=True)
        P.op("vector", "tensor_copy", a6(A6["cdb"], 128), PS[4][:, 128:256])
        for a in range(2):
            for g in range(8):
                P.op("tensor", "transpose", PT0[:, g * 128:(g + 1) * 128],
                     a5(A5["bch"] + g * 256 + a * 128, 128), idb[:, :])
            P.op("vector", "tensor_copy", a5(A5["bt0"] if a == 0 else A5["bt1"], 1024), PT0[:, 0:1024])
        if c > 0:
            P.op("vector", "tensor_copy", ar[2][:, 0:4096], arf(P, 0)[:, :])
            P.op("vector", "tensor_copy", ar[2][:, 4096:8192], arf(P, 1)[:, :])
        for g in range(8):
            P.op("tensor", "matmul", PS[0][:, 0:256], a5(A5["bch"] + g * 256, 128),
                 a5(A5["cch"] + g * 256, 256), start=True, stop=True)
            P.op("tensor", "matmul", PS[0][:, 256:384], a5(A5["bch"] + g * 256 + 128, 128),
                 a5(A5["cch"] + g * 256 + 128, 128), start=False, stop=True)
            P.op("vector", "tensor_copy", a6(A6["G"] + g * 384, 256), PS[0][:, 0:256])
            P.op("vector", "tensor_copy", a6(A6["G"] + g * 384 + 256, 128), PS[0][:, 256:384])
        def emit_R(h, hb_):
            lhb = _AP(idb[:, h:h + 1].tensor, idb[:, h:h + 1].offset,
                      [list(idb[:, h:h + 1].ap[0]), [0, 128]])
            RB = PS[0] if hb_ == 0 else PS[5]
            P.op("tensor", "matmul", RB[:, 0:256], lhb, a5(A5["ah"], 256), start=True, stop=False)
            P.op("tensor", "matmul", RB[:, 0:256], lhb, a5(A5["al"], 256), start=False, stop=False)
            P.op("tensor", "matmul", RB[:, 256:384], lhb, a5(A5["ah"] + 128, 128),
                 start=False, stop=False)
            P.op("tensor", "matmul", RB[:, 256:384], lhb, a5(A5["al"] + 128, 128),
                 start=False, stop=False)
            P.op("tensor", "matmul", RB[:, 0:128], P.sb["c_idb"][:, :], trn[:, :],
                 start=False, stop=False)
            P.op("tensor", "matmul", RB[:, 256:384], P.sb["c_idb"][:, :], trn[:, :],
                 start=False, stop=True)

        def emit_Mt(h, hb_):
            g = h // 16
            RB = PS[0] if hb_ == 0 else PS[5]
            eb = ev[1] if hb_ == 0 else ev[0]
            mt_, mt1_ = (A5["mt0"], A5["mt1a"]) if hb_ == 0 else (A5["mt0b"], A5["mt1b"])
            P.op("scalar", "activation", eb[:, 0:256], RB[:, 0:256], AF.Exp,
                 bias=arf(P, 6)[:, A6["nacs"] + h:A6["nacs"] + h + 1])
            P.op("scalar", "activation", eb[:, 256:384], RB[:, 256:384], AF.Exp,
                 bias=arf(P, 6)[:, A6["nacs"] + 128 + h:A6["nacs"] + 128 + h + 1])
            P.op("vector", "tensor_mul", a5(mt_, 256), eb[:, 0:256], a6(A6["G"] + g * 384, 256))
            P.op("vector", "tensor_mul", a5(mt1_, 128), eb[:, 256:384],
                 a6(A6["G"] + g * 384 + 256, 128))

        emit_R(0, 0)
        emit_Mt(0, 0)
        emit_R(1, 1)
        for hg in range(16):
            for hh in range(8):
                h = hg * 8 + hh
                g = h // 16
                hb_ = h % 2
                if h + 1 < 128:
                    emit_Mt(h + 1, (h + 1) % 2)
                if h + 2 < 128:
                    emit_R(h + 2, h % 2)
                mt_, mt1_ = (A5["mt0"], A5["mt1a"]) if hb_ == 0 else (A5["mt0b"], A5["mt1b"])
                first = hh == 0
                last = hh == 7
                P.op("tensor", "matmul", PS[1][:, hh * 64:(hh + 1) * 64], a5(mt_, 128),
                     ar[3][:, h * 64:(h + 1) * 64], start=first, stop=last)
                P.op("tensor", "matmul", PS[2][:, hh * 64:(hh + 1) * 64], a5(mt_ + 128, 128),
                     ar[3][:, h * 64:(h + 1) * 64], start=first, stop=False)
                P.op("tensor", "matmul", PS[2][:, hh * 64:(hh + 1) * 64], a5(mt1_, 128),
                     ar[4][:, h * 64:(h + 1) * 64], start=False, stop=last)
                if c > 0:
                    sbf = view(ar[2], h * 64, [[8192, 128], [1, 64]])
                    P.op("tensor", "matmul", PS[3][:, hh * 64:(hh + 1) * 64],
                         a5(A5["cch"] + g * 256, 128), sbf, start=first, stop=last)
                    P.op("tensor", "matmul", PS[4][:, hh * 64:(hh + 1) * 64],
                         a5(A5["cch"] + g * 256 + 128, 128), sbf, start=first, stop=last)
            for lh in range(2):
                dps = PS[1] if lh == 0 else PS[2]
                ops_ = PS[3] if lh == 0 else PS[4]
                eav = _AP(arf(P, 6).tensor, A6["eacs"] + lh * 128 + hg * 8,
                          [[4096, 128], [1, 8], [0, 64]])
                if c > 0:
                    P.op("vector", "tensor_tensor", ev[2][:, 0:512].rearrange("p (a b) -> p a b", a=8),
                         ops_[:, 0:512].rearrange("p (a b) -> p a b", a=8), eav, op=ALU.mult)
                    P.op("vector", "scalar_tensor_tensor", ev[3][:, 0:512], dps[:, 0:512], 1.0,
                         ev[2][:, 0:512], op0=ALU.mult, op1=ALU.add)
                else:
                    P.op("vector", "tensor_copy", ev[3][:, 0:512], dps[:, 0:512])
                P.dma("scalar", ytd[o + lh * 128:o + (lh + 1) * 128, hg * 512:(hg + 1) * 512],
                      ev[3][:, 0:512])
        for a in range(2):
            dsv = _AP(a5(A5["ds"], 256).tensor, a5(0, 1).offset + A5["ds"] + a * 128,
                      [[8192, 128], [1, 128], [0, 64]])
            P.op("vector", "tensor_tensor", ar[3 + a][:, :].rearrange("p (h q) -> p h q", h=128),
                 ar[3 + a][:, :].rearrange("p (h q) -> p h q", h=128), dsv, op=ALU.mult)
        for hg in range(16):
            sps = PS[1] if hg % 2 == 0 else PS[2]
            for hh in range(8):
                h = hg * 8 + hh
                g = h // 16
                P.op("tensor", "matmul", sps[:, hh * 64:(hh + 1) * 64], a5(A5["bt0"] + g * 128, 128),
                     ar[3][:, h * 64:(h + 1) * 64], start=(hh == 0), stop=False)
                P.op("tensor", "matmul", sps[:, hh * 64:(hh + 1) * 64], a5(A5["bt1"] + g * 128, 128),
                     ar[4][:, h * 64:(h + 1) * 64], start=False, stop=(hh == 7))
            sr = arf(P, 0)[:, (hg % 8) * 512:(hg % 8 + 1) * 512] if hg < 8 else                  arf(P, 1)[:, (hg - 8) * 512:(hg - 8 + 1) * 512]
            if c > 0:
                cdv = _AP(arf(P, 6).tensor, A6["cdb"] + hg * 8, [[4096, 128], [1, 8], [0, 64]])
                P.op("vector", "tensor_tensor", ev[0][:, 0:512].rearrange("p (a b) -> p a b", a=8),
                     sr.rearrange("p (a b) -> p a b", a=8), cdv, op=ALU.mult)
                P.op("vector", "scalar_tensor_tensor", sr, sps[:, 0:512], 1.0, ev[0][:, 0:512],
                     op0=ALU.mult, op1=ALU.add)
            else:
                P.op("vector", "tensor_copy", sr, sps[:, 0:512])
    P.dma("scalar", gind[:, 0:4096], arf(P, 0)[:, :])
    P.dma("scalar", gind[:, 4096:8192], arf(P, 1)[:, :])
    P.dma("scalar", gind[:, 8192:8193], cdec[:, 8:9], allow_slow_non_contiguous=True)
    P.collective("AllGather", ALU.bypass, [[0, 1, 2, 3], [4, 5, 6, 7]], gind, goutd)
    if "goutx" in P.ext_out:
        P.dma("sync", P.dram("goutx").ap(), goutd)


def b_l0_post(P):
    gname = "goutx" if "goutx" in P.ext_in else "gout"
    goutd = P.dram(gname).ap()
    cumd = P.dram("cums").ap()
    ead = P.dram("expacs").ap()
    cfmd = P.dram("cfm").ap()
    ytd = P.dram("yt").ap()
    ztd = P.dram("zt").ap()
    dxtd = P.dram("dxt").ap()
    gffnd = P.dram("gffn").ap()
    xhd = P.dram("xh").ap()
    yfmd = P.dram("yfm").ap()
    x1d = P.dram("x1").ap()
    idb = P.sb["c_idb"]
    idf = P.sb["c_idf"]
    ar = [P.sb[f"ar{j}"] for j in range(8)]
    ev = [P.sb[f"ip_ev{i}"] for i in range(4)]
    PS = [P.sb[f"PS{i}"] for i in range(7)]
    PT0 = P.sb["PT0"]
    mrel = P.sbuf("mrel", [128, 8], F32)
    dmr = P.sbuf("dmr", [128, 8], F32)
    suf = P.sbuf("suf", [128, 1], F32)
    rrow = P.sbuf("dt_rowt", [1, 128], F32)
    nwt = P.sbuf("nwt", [128, 64], F32)
    rms = P.sbuf("rms_c", [128, 4], F32)
    P.dma("sync", mrel[:, :], P.dram("relmask").ap())
    P.dma("sync", nwt[:, :], P.dram("nwt").ap())

    for r in range(4):
        P.dma("sync", dmr[:, r:r + 1], goutd[r * 128:(r + 1) * 128, 8192:8193],
              allow_slow_non_contiguous=True)
        P.op("vector", "tensor_scalar", dmr[:, r:r + 1], dmr[:, r:r + 1], mrel[:, r:r + 1], None,
             op0=ALU.mult)
        P.op("vector", "tensor_scalar", dmr[:, r:r + 1], dmr[:, r:r + 1], mrel[:, 4 + r:5 + r], None,
             op0=ALU.add)
    P.op("vector", "memset", suf[:, :], 1.0)
    for r in (3, 2, 1, 0):
        P.op("vector", "tensor_mul", dmr[:, 4 + r:5 + r], mrel[:, r:r + 1], suf[:, :])
        if r > 0:
            P.op("vector", "tensor_mul", suf[:, :], suf[:, :], dmr[:, r:r + 1])
    for r in range(4):
        P.op("tensor", "transpose", PS[4][0:1, 0:128], dmr[:, 4 + r:5 + r], idf[:, :])
        P.op("vector", "tensor_copy", rrow[:, :], PS[4][0:1, 0:128])
        P.op("tensor", "matmul", PS[4][:, 128:256], P.sb["c_onerf"][:, :], rrow[:, :], start=True, stop=True)
        P.op("vector", "tensor_copy", arf(P, 6)[:, r * 128:(r + 1) * 128], PS[4][:, 128:256])
    for r in range(4):
        P.dma("sync", arf(P, 2)[:, :], goutd[r * 128:(r + 1) * 128, 0:4096])
        P.dma("sync", arf(P, 3)[:, :], goutd[r * 128:(r + 1) * 128, 4096:8192])
        for half in range(2):
            cbv = _AP(arf(P, 6).tensor, r * 128 + half * 64, [[4096, 128], [1, 64], [0, 64]])
            src_ = arf(P, 2) if half == 0 else arf(P, 3)
            if r == 0:
                P.op("vector", "tensor_tensor",
                     arf(P, half)[:, :].rearrange("p (a b) -> p a b", a=64),
                     src_[:, :].rearrange("p (a b) -> p a b", a=64), cbv, op=ALU.mult)
            else:
                P.op("vector", "tensor_tensor",
                     arf(P, 4)[:, :].rearrange("p (a b) -> p a b", a=64),
                     src_[:, :].rearrange("p (a b) -> p a b", a=64), cbv, op=ALU.mult)
                P.op("vector", "tensor_add", arf(P, half)[:, :], arf(P, half)[:, :], arf(P, 4)[:, :])

    for tt in range(NT):
        c, lh = tt // 2, tt % 2
        if lh == 0:
            P.dma("sync", view(ar[5], 0, [[8192, 128], [256, 8], [1, 256]]),
                  view(P.dram("cfm"), c * 256, [[2048, 128], [262144, 8], [1, 256]]))
            P.dma("sync", rrow[:, :], cumd[c:c + 1, :])
            P.op("tensor", "matmul", PS[4][:, 128:256], P.sb["c_onerf"][:, :], rrow[:, :],
                 start=True, stop=True)
            P.op("vector", "tensor_copy", arf(P, 6)[:, 512:640], PS[4][:, 128:256])
            for half in range(2):
                cmv = _AP(arf(P, 6).tensor, 512 + half * 64, [[4096, 128], [1, 64], [0, 64]])
                P.op("vector", "tensor_tensor",
                     ar[2][:, half * 4096:(half + 1) * 4096].rearrange("p (a b) -> p a b", a=64),
                     arf(P, half)[:, :].rearrange("p (a b) -> p a b", a=64), cmv, op=ALU.mult)
        P.dma("sync", arf(P, 3)[:, :], ytd[tt * 128:(tt + 1) * 128, 0:4096])
        P.dma("sync", arf(P, 4)[:, :], ytd[tt * 128:(tt + 1) * 128, 4096:8192])
        P.dma("sync", P.sb["sc_a"][:, :], ztd[tt * 128:(tt + 1) * 128, 0:4096])
        P.dma("sync", P.sb["sc_b"][:, :], dxtd[tt * 128:(tt + 1) * 128, 0:4096])
        P.dma("sync", arf(P, 7)[:, :], ztd[tt * 128:(tt + 1) * 128, 4096:8192])
        P.dma("sync", arf(P, 6)[:, 640:768], ead[tt * 128:(tt + 1) * 128, :])
        for hg in range(16):
            pp = PS[hg % 2]
            for hh in range(8):
                h = hg * 8 + hh
                g = h // 16
                P.op("tensor", "matmul", pp[:, hh * 64:(hh + 1) * 64],
                     view(ar[5], g * 256 + lh * 128, [[8192, 128], [1, 128]]),
                     view(ar[2], h * 64, [[8192, 128], [1, 64]]),
                     start=(hh == 0), stop=(hh == 7))
            eav = _AP(arf(P, 6).tensor, 640 + hg * 8, [[4096, 128], [1, 8], [0, 64]])
            P.op("vector", "tensor_tensor", ev[0][:, 0:512].rearrange("p (a b) -> p a b", a=8),
                 pp[:, 0:512].rearrange("p (a b) -> p a b", a=8), eav, op=ALU.mult)
            yh = arf(P, 3)[:, (hg % 8) * 512:(hg % 8 + 1) * 512] if hg < 8 else                  arf(P, 4)[:, (hg - 8) * 512:(hg - 8) * 512 + 512]
            P.op("vector", "tensor_add", yh, yh, ev[0][:, 0:512])
        for half in range(2):
            yh = arf(P, 3 + half)
            zt_h = P.sb["sc_a"][:, :] if half == 0 else arf(P, 7)[:, :]
            if half == 1:
                P.dma("sync", P.sb["sc_b"][:, :],
                      dxtd[tt * 128:(tt + 1) * 128, 4096:8192])
            P.op("vector", "tensor_add", yh[:, :], yh[:, :], P.sb["sc_b"][:, :])
            P.op("scalar", "activation", zt_h, zt_h, AF.Silu)
            P.op("vector", "tensor_mul", yh[:, :], yh[:, :], zt_h)
            P.op("scalar", "activation", P.sb["sc_b"][:, :], yh[:, :], AF.Square,
                 accum_out=rms[:, half:half + 1])
        P.op("vector", "tensor_add", rms[:, 2:3], rms[:, 0:1], rms[:, 1:2])
        P.op("vector", "tensor_scalar", rms[:, 2:3], rms[:, 2:3], 1.0 / DIN, None, op0=ALU.mult)
        P.op("scalar", "activation", rms[:, 2:3], rms[:, 2:3], AF.Sqrt, bias=EPSC(P))
        P.op("vector", "reciprocal", rms[:, 2:3], rms[:, 2:3])
        for half in range(2):
            yh = arf(P, 3 + half)
            yrb = ar[5][:, 4096:8192]
            P.op("vector", "tensor_scalar", yrb, yh[:, :], rms[:, 2:3], None, op0=ALU.mult)
            for b8 in range(4):
                for j in range(8):
                    kc = half * 32 + b8 * 8 + j
                    P.op("tensor", "transpose", PT0[:, j * 128:(j + 1) * 128],
                         yrb[:, (b8 * 8 + j) * 128:(b8 * 8 + j + 1) * 128], idb[:, :])
                for j in range(8):
                    kc = half * 32 + b8 * 8 + j
                    P.op("vector", "tensor_scalar", P.sb["wb0"][:, j * 256:j * 256 + 128],
                         PT0[:, j * 128:(j + 1) * 128], nwt[:, kc:kc + 1], None, op0=ALU.mult)
                dst = view(yfmd, (tt * 64 + half * 32 + b8 * 8) * 128 * 128,
                           [[128, 128], [16384, 8], [1, 128]])
                P.dma("scalar", dst, view(P.sb["wb0"], 0, [[2048, 128], [256, 8], [1, 128]]))

    woutd = P.dram("woutt").ap()
    ysl = [P.sb["sc_a"][:, :].bitcast(BF16), P.sb["sc_b"][:, :].bitcast(BF16)]
    for nb in range(4):
        for kc in range(64):
            P.dma("gpsimd", view(ar[kc // 8], (kc % 8) * 1024, [[8192, 128], [1, 1024]]),
                  woutd[nb, kc])
        for tt in range(NT):
            sb = tt % 2
            P.dma("sync", view(ysl[sb], 0, [[8192, 128], [128, 64], [1, 128]]),
                  view(yfmd, (tt * 64) * 128 * 128, [[128, 128], [16384, 64], [1, 128]]))
            for half in range(2):
                pp = PS[half]
                for kc in range(64):
                    P.op("tensor", "matmul", pp[:, 0:512],
                         ysl[sb][:, kc * 128:(kc + 1) * 128],
                         view(ar[kc // 8], (kc % 8) * 1024 + half * 512, [[8192, 128], [1, 512]]),
                         start=(kc == 0), stop=(kc == 63))
            P.dma("sync", P.sb["sc_c"][:, 0:1024], gffnd[tt * 128:(tt + 1) * 128, nb * 1024:(nb + 1) * 1024])
            P.dma("sync", P.sb["sc_d"][:, 0:1024], xhd[3 + tt * 128:3 + (tt + 1) * 128,
                                                       nb * 1024:(nb + 1) * 1024])
            P.dma("sync", P.sb["sc_d"][:, 1024:2048],
                  P.dram("xnt").ap()[tt * 128:(tt + 1) * 128, nb * 1024:(nb + 1) * 1024])
            for half in range(2):
                P.op("vector", "scalar_tensor_tensor", ev[half][:, 0:512], PS[half][:, 0:512], 1.0,
                     P.sb["sc_d"][:, 1024 + half * 512:1024 + (half + 1) * 512],
                     op0=ALU.mult, op1=ALU.add)
                P.op("vector", "tensor_mul", ev[half][:, 0:512], ev[half][:, 0:512],
                     P.sb["sc_c"][:, half * 512:(half + 1) * 512])
                P.op("vector", "tensor_add", ev[2 + half][:, 0:512], ev[half][:, 0:512],
                     P.sb["sc_d"][:, half * 512:(half + 1) * 512])
                P.dma("scalar", x1d[tt * 128:(tt + 1) * 128,
                                    nb * 1024 + half * 512:nb * 1024 + (half + 1) * 512],
                      ev[2 + half][:, 0:512])


def b_l1_ln(P):
    x1d = P.dram("x1").ap()
    xt = P.sb["sc_a"]
    sqd = P.sb["sc_b"]
    xb = bfv(P, "sc_b")
    s_mu = P.sbuf("ln_mu", [128, 1], F32)
    s_nm = P.sbuf("ln_nm", [128, 1], F32)
    s_s2 = P.sbuf("ln_s2", [128, 1], F32)
    s_rs = P.sbuf("ln_rs", [128, 1], F32)
    ptr = P.psum("PT0", [128, 1024], BF16)
    for tt in range(NT):
        P.dma("sync", xt[0:128, :], x1d[tt * 128:(tt + 1) * 128, :])
        P.op("vector", "reduce_sum", s_mu[:, :], xt[:, :], axis=AX)
        P.op("vector", "tensor_scalar", s_mu[:, :], s_mu[:, :], 1.0 / D, None, op0=ALU.mult)
        P.op("vector", "tensor_scalar", s_nm[:, :], s_mu[:, :], -1.0, None, op0=ALU.mult)
        P.op("scalar", "activation", sqd[:, :], xt[:, :], AF.Square, bias=s_nm[:, :],
             accum_out=s_s2[:, :])
        P.op("vector", "tensor_scalar", s_s2[:, :], s_s2[:, :], 1.0 / D, None, op0=ALU.mult)
        P.op("scalar", "activation", s_rs[:, :], s_s2[:, :], AF.Sqrt, bias=EPSC(P))
        P.op("vector", "reciprocal", s_rs[:, :], s_rs[:, :])
        P.op("vector", "tensor_scalar", xb[:, 0:4096], xt[:, :], s_mu[:, :], s_rs[:, :],
             op0=ALU.subtract, op1=ALU.mult)
        for half in range(8):
            for j in range(4):
                kc = half * 4 + j
                P.op("tensor", "transpose", ptr[:, j * 128:(j + 1) * 128],
                     xb[:, kc * 128:(kc + 1) * 128], P.sb["c_idb"][:, :])
            dst = view(P.sb[f"ar{half}"], tt * 128, [[8192, 128], [2048, 4], [1, 128]])
            src_ = view(ptr, 0, [[1024, 128], [128, 4], [1, 128]])
            P.op("vector", "tensor_copy", dst, src_)


def b_l1_qkv(P):
    wqrd = P.dram("wqr").ap()
    wqed = P.dram("wqe").ap()
    qfd = P.dram("qf").ap()
    kvd = P.dram("kvloc").ap()
    qkd = P.dram("qkst").ap()
    wkd = P.dram("wkt").ap()
    wvd = P.dram("wvt").ap()
    wb = [P.sb["wb0"], P.sb["wb1"]]
    ev = [P.sb[f"ip_ev{i}"] for i in range(4)]
    PS = [P.sb[f"PS{i}"] for i in range(7)]
    kst = P.sbuf("l1_kst", [1, 16], F32)
    kstb = P.sbuf("l1_kstb", [1, 8], BF16)
    qsq = P.sb["ip_ev2"][:, :].bitcast(BF16)[:, 0:512]
    nrow = P.sb["ip_ev3"][0:1, 0:512]

    for kc in range(32):
        for g in range(4):
            P.dma("sync", P.sb["sc_a"][:, g * 1024:(g + 1) * 1024], wqrd[g, kc * 128:(kc + 1) * 128, :])
        P.op("vector", "tensor_add", P.sb["sc_b"][:, 0:1024], P.sb["sc_a"][:, 0:1024],
             P.sb["sc_a"][:, 1024:2048])
        P.op("vector", "tensor_add", P.sb["sc_b"][:, 1024:2048], P.sb["sc_a"][:, 2048:3072],
             P.sb["sc_a"][:, 3072:4096])
        P.op("vector", "tensor_add", P.sb["sc_b"][:, 0:1024], P.sb["sc_b"][:, 0:1024],
             P.sb["sc_b"][:, 1024:2048])
        P.op("vector", "tensor_scalar", bfv(P, "sc_b")[:, 4096:5120], P.sb["sc_b"][:, 0:1024],
             RSQD, None, op0=ALU.mult)
        P.dma("scalar", view(wqed, kc * 128 * 128, [[128, 128], [524288, 8], [1, 128]]),
              view(bfv(P, "sc_b"), 4096, [[8192, 128], [128, 8], [1, 128]]))

    P.op("vector", "memset", kst[:, :], 0.0)
    for oc in range(8):
        P.dma("gpsimd", wb[0][:, :].rearrange("p (k o) -> p k o", k=16),
              view(wqed, oc * 524288, [[128, 128], [16384, 16], [1, 128]]))
        P.dma("gpsimd", wb[1][:, :].rearrange("p (k o) -> p k o", k=16),
              view(wqed, oc * 524288 + 16 * 16384, [[128, 128], [16384, 16], [1, 128]]))
        for ts4 in range(4):
            pp = PS[ts4 % 2]
            for kc in range(32):
                P.op("tensor", "matmul", pp[:, 0:512],
                     wb[kc // 16][:, (kc % 16) * 128:(kc % 16 + 1) * 128],
                     xn_rhs(P, kc, ts4 * 512, 512), start=(kc == 0), stop=(kc == 31))
            P.op("vector", "tensor_copy", ev[ts4 % 2][:, 0:512].bitcast(BF16)[:, 0:512], pp[:, 0:512])
            P.dma("scalar", qfd[oc * 128:(oc + 1) * 128, ts4 * 512:(ts4 + 1) * 512],
                  ev[ts4 % 2][:, 0:512].bitcast(BF16)[:, 0:512])
            P.op("scalar", "activation", qsq, pp[:, 0:512], AF.Square)
            P.op("tensor", "matmul", PS[4][0:1, 0:512], P.sb["c_onec"][:, :], qsq,
                 start=True, stop=True)
            P.op("vector", "tensor_copy", nrow, PS[4][0:1, 0:512])
            P.op("vector", "reduce_max", kst[0:1, 8 + oc:9 + oc], nrow, axis=AX)
            P.op("vector", "tensor_max", kst[0:1, oc:oc + 1], kst[0:1, oc:oc + 1],
                 kst[0:1, 8 + oc:9 + oc])
    P.dma("scalar", qkd[0:1, 0:8], kst[0:1, 0:8])

    P.op("vector", "memset", kst[:, :], 0.0)
    for oc in range(8):
        P.dma("gpsimd", wb[0][:, :], wkd[oc, :, 0:16, :])
        P.dma("gpsimd", wb[1][:, :], wkd[oc, :, 16:32, :])
        for ts4 in range(4):
            pp = PS[ts4 % 2]
            for kc in range(32):
                P.op("tensor", "matmul", pp[:, 0:512],
                     wb[kc // 16][:, (kc % 16) * 128:(kc % 16 + 1) * 128],
                     xn_rhs(P, kc, ts4 * 512, 512), start=(kc == 0), stop=(kc == 31))
            P.op("vector", "tensor_copy", ev[ts4 % 2][:, 0:512].bitcast(BF16)[:, 0:512], pp[:, 0:512])
            P.dma("scalar", view(kvd, 2048 * 1024 + oc * 128 * 2048 + ts4 * 512,
                                 [[2048, 128], [1, 512]]),
                  ev[ts4 % 2][:, 0:512].bitcast(BF16)[:, 0:512])
            P.op("scalar", "activation", qsq, pp[:, 0:512], AF.Square)
            P.op("tensor", "matmul", PS[4][0:1, 0:512], P.sb["c_onec"][:, :], qsq,
                 start=True, stop=True)
            P.op("vector", "tensor_copy", nrow, PS[4][0:1, 0:512])
            P.op("vector", "reduce_max", kst[0:1, 8 + oc:9 + oc], nrow, axis=AX)
            P.op("vector", "tensor_max", kst[0:1, oc:oc + 1], kst[0:1, oc:oc + 1],
                 kst[0:1, 8 + oc:9 + oc])
    P.op("vector", "tensor_scalar", kst[0:1, 0:8], kst[0:1, 0:8], 1.02, None, op0=ALU.mult)
    P.op("vector", "tensor_copy", kstb[:, :], kst[0:1, 0:8])
    P.dma("scalar", view(kvd, 4096 * 1024, [[1024, 1], [1, 8]]), kstb[:, :])

    for oc in range(8):
        P.dma("gpsimd", wb[0][:, :], wvd[oc, :, 0:16, :])
        P.dma("gpsimd", wb[1][:, :], wvd[oc, :, 16:32, :])
        for tt in range(NT):
            pp = PS[tt % 2]
            for kc in range(32):
                P.op("tensor", "matmul", pp[:, 0:128], xn_rhs(P, kc, tt * 128, 128),
                     wb[kc // 16][:, (kc % 16) * 128:(kc % 16 + 1) * 128],
                     start=(kc == 0), stop=(kc == 31))
            P.op("vector", "tensor_copy", ev[tt % 2][:, 0:128].bitcast(BF16)[:, 0:128], pp[:, 0:128])
            P.dma("scalar", view(kvd, tt * 128 * 1024 + oc * 128, [[1024, 128], [1, 128]]),
                  ev[tt % 2][:, 0:128].bitcast(BF16)[:, 0:128])
    P.collective("AllGather", ALU.bypass, [[0, 1, 2, 3], [4, 5, 6, 7]],
                 P.dram("kvloc").ap(), P.dram("kvg").ap())
    if "kvgx" in P.ext_out:
        P.dma("sync", P.dram("kvgx").ap(), P.dram("kvg").ap())


def b_l1_ffn(P):
    g1d = P.dram("g1").ap()
    wla = P.sb["sc_a"][:, :].bitcast(BF16)
    wr = P.sb["sc_b"][:, :].bitcast(BF16)
    ws = [P.sb["sc_c"][:, :].bitcast(BF16), P.sb["sc_d"][:, :].bitcast(BF16)]
    m1 = P.sb["f_m1"] if "f_m1" in P.sb else P.sbuf("f_m1", [128, 1], F32)
    m2 = P.sb["f_m2"] if "f_m2" in P.sb else P.sbuf("f_m2", [128, 1], F32)
    zt_ = P.sbuf("s_z", [128, 8], F32)
    ke = P.sbuf("s_ke", [128, 8], F32)
    se = P.sbuf("s_se", [128, 8], F32)
    msk = P.sbuf("s_msk", [128, 8], F32)
    tau = P.sbuf("s_tau", [128, 1], F32)
    brow = P.sbuf("dt_rowt", [1, 128], F32)[0:1, :].bitcast(BF16)[0:1, 0:128]
    rrow2 = P.sbuf("s_rrow", [1, 8], BF16)
    PS = [P.sb[f"PS{i}"] for i in range(7)]
    ptr = P.sb["PT0"]
    hf = P.sb["wb0"]
    ev = [P.sb[f"ip_ev{i}"] for i in range(4)]
    hp = ev[2][:, 0:128]
    hb = ev[3][:, :].bitcast(BF16)[:, 0:128]

    P.dma("gpsimd", view(wla, 0, [[8192, 128], [128, 32], [1, 128]]),
          view(P.dram("wla1"), 0, [[128, 128], [128 * 128, 32], [1, 128]]))
    P.dma("gpsimd", view(wr, 0, [[8192, 128], [8, 32], [1, 8]]),
          view(P.dram("wr1"), 0, [[8, 128], [128 * 8, 32], [1, 8]]))

    P.dma("gpsimd", rrow2[:, :], P.dram("br1").ap())
    rb8 = P.sbuf("f_rb8", [128, 8], F32)
    bbt = P.sbuf("d_bc", [128, 128], F32)

    for tt in range(NT):
        if tt == 0:
            P.op("tensor", "matmul", PS[4][:, 16:24], P.sb["c_oner"][:, :], rrow2[:, :],
                 start=True, stop=True)
            P.op("vector", "tensor_copy", rb8[:, :], PS[4][:, 16:24])
        for kc in range(32):
            P.op("tensor", "matmul", PS[4][:, 0:8], xn_rhs(P, kc, tt * 128, 128),
                 wr[:, kc * 8:(kc + 1) * 8], start=(kc == 0), stop=(kc == 31))
        for kc in range(32):
            P.op("tensor", "matmul", PS[4][:, 32:160], xn_rhs(P, kc, tt * 128, 128),
                 wla[:, kc * 128:(kc + 1) * 128], start=False, stop=(kc == 31))
        P.op("vector", "tensor_add", zt_[:, :], PS[4][:, 0:8], rb8[:, :])
        P.op("vector", "tensor_copy", hp, PS[4][:, 32:160])
        for e in range(8):
            P.op("vector", "tensor_scalar", msk[:, :], zt_[:, :], zt_[:, e:e + 1], None, op0=ALU.is_ge)
            P.op("vector", "reduce_sum", ke[:, e:e + 1], msk[:, :], axis=AX)
            P.op("vector", "tensor_mul", msk[:, :], msk[:, :], zt_[:, :])
            P.op("vector", "reduce_sum", se[:, e:e + 1], msk[:, :], axis=AX)
        P.op("vector", "tensor_mul", ke[:, :], ke[:, :], zt_[:, :])
        P.op("vector", "scalar_tensor_tensor", msk[:, :], ke[:, :], 1.0, se[:, :],
             op0=ALU.add, op1=ALU.is_gt)
        P.op("vector", "reduce_sum", m1[:, :], msk[:, :], axis=AX)
        P.op("vector", "tensor_mul", se[:, :], msk[:, :], zt_[:, :])
        P.op("vector", "reduce_sum", m2[:, :], se[:, :], axis=AX)
        P.op("vector", "reciprocal", m1[:, :], m1[:, :])
        P.op("vector", "tensor_scalar", tau[:, :], m2[:, :], -1.0, m1[:, :], op0=ALU.add, op1=ALU.mult)
        P.op("vector", "tensor_scalar", zt_[:, :], zt_[:, :], tau[:, :], 0.0, op0=ALU.subtract, op1=ALU.max)
        wv_ = _AP(zt_[:, :].tensor, zt_[:, :].offset, [list(zt_[:, :].ap[0]), [1, 8], [0, 16]])
        P.op("vector", "tensor_tensor", hb.rearrange("p (e r) -> p e r", e=8),
             hp.rearrange("p (e r) -> p e r", e=8), wv_, op=ALU.mult)
        P.op("tensor", "transpose", ptr[:, 0:128], hb, P.sb["c_idb"][:, :])
        P.op("vector", "tensor_copy", hf[:, tt * 128:(tt + 1) * 128], ptr[:, 0:128])

    P.dma("gpsimd", view(wr, 1024, [[8192, 128], [1, 4096]]), P.dram("wlb1").ap())
    wbt = P.dram("wb1t").ap()
    for og in range(32):
        sb = og % 2
        P.dma("gpsimd", ws[sb][:, 0:4096], wbt[og])
        P.dma("gpsimd", brow, P.dram("bb1").ap()[0:1, og * 128:(og + 1) * 128])
        P.op("tensor", "matmul", PS[4][:, 256:384], P.sb["c_oner"][:, :], brow,
             start=True, stop=True)
        P.op("vector", "tensor_copy", bbt[:, :], PS[4][:, 256:384])
        for tt in range(NT):
            pp = PS[tt % 4]
            for kc in range(32):
                P.op("tensor", "matmul", pp[:, 0:128], xn_rhs(P, kc, tt * 128, 128),
                     ws[sb][:, kc * 128:(kc + 1) * 128], start=(kc == 0), stop=False)
            P.op("tensor", "matmul", pp[:, 0:128], hf[:, tt * 128:(tt + 1) * 128],
                 wr[:, 1024 + og * 128:1024 + (og + 1) * 128], start=False, stop=True)
            gev = ev[tt % 4]
            P.op("vector", "scalar_tensor_tensor", gev[:, 0:128], pp[:, 0:128], 1.0, bbt[:, :],
                 op0=ALU.mult, op1=ALU.add)
            P.op("scalar", "activation", gev[:, 0:128], gev[:, 0:128], AF.Silu)
            P.dma("scalar", g1d[tt * 128:(tt + 1) * 128, og * 128:(og + 1) * 128], gev[:, 0:128])


def b_l1_attn(P):
    gname = "kvgx" if "kvgx" in P.ext_in else "kvg"
    kvgd = P.dram(gname).ap()
    qfd = P.dram("qf").ap()
    qkd = P.dram("qkst").ap()
    attnd = P.dram("attn").ap()
    ar = [P.sb[f"ar{j}"] for j in range(8)]
    ev = [P.sb[f"ip_ev{i}"] for i in range(4)]
    PS = [P.sb[f"PS{i}"] for i in range(7)] if "PS5" in P.sb else [P.sb[f"PS{i}"] for i in range(5)]
    sa = P.sb["sc_a"]
    sab = sa[:, :].bitcast(BF16)
    sbb = P.sb["sc_b"][:, :].bitcast(BF16)
    st8 = sa[0:1, 0:48]
    nb8 = sa[:, 64:72]
    rc = sa[:, 72:73]
    rrow = P.sb["dt_rowt"] if "dt_rowt" in P.sb else P.sbuf("dt_rowt", [1, 128], F32)
    P.dma("sync", st8[0:1, 0:8], qkd[0:1, 0:8])
    for r in range(4):
        P.dma("gpsimd", st8[0:1, 16 + r * 8:24 + r * 8],
              view(kvgd, (r * 4097 + 4096) * 1024, [[8, 1], [1, 8]]))
    P.op("vector", "tensor_copy", st8[0:1, 8:16], st8[0:1, 16:24])
    for r in range(1, 4):
        P.op("vector", "tensor_max", st8[0:1, 8:16], st8[0:1, 8:16], st8[0:1, 16 + r * 8:24 + r * 8])
    P.op("vector", "tensor_mul", st8[0:1, 8:16], st8[0:1, 8:16], st8[0:1, 0:8])
    P.op("scalar", "activation", st8[0:1, 8:16], st8[0:1, 8:16], AF.Sqrt)
    P.op("vector", "tensor_scalar", st8[0:1, 8:16], st8[0:1, 8:16], -1.0, None, op0=ALU.mult)
    for h in range(8):
        P.op("tensor", "matmul", PS[4][:, 0:1], P.sb["c_onerf"][:, :], st8[0:1, 8 + h:9 + h],
             start=True, stop=True)
        P.op("vector", "tensor_copy", nb8[:, h:h + 1], PS[4][:, 0:1])
    for h in range(8):
        P.dma("sync", view(ar[h // 4], (h % 4) * 2048, [[8192, 128], [1, 2048]]),
              qfd[h * 128:(h + 1) * 128, :])
    kt = [sab[:, 256:384], sab[:, 384:512], sab[:, 776:904]]
    va = [sab[:, 512:644], sab[:, 644:776], sab[:, 904:1036]]
    et = [sbb[:, 0:512], sbb[:, 512:1024], sbb[:, 1024:1536]]
    for i in range(3):
        P.op("vector", "memset", va[i][:, 128:129], 1.0)
    SCB = [0, 1, 5]
    for h in range(8):
        for qt in range(4):
            qv = view(ar[h // 4], (h % 4) * 2048 + qt * 512, [[8192, 128], [1, 512]])

            def ld(kc):
                b = kc % 3
                r, lt = kc // 16, kc % 16
                P.dma("sync", kt[b],
                      view(kvgd, (r * 4097 + 2048) * 1024 + h * 128 * 2048 + lt * 128,
                           [[2048, 128], [1, 128]]))
                P.dma("sync", va[b][:, 0:128],
                      view(kvgd, (r * 4097 + lt * 128) * 1024 + h * 128, [[1024, 128], [1, 128]]))

            def sc(kc):
                b = kc % 3
                P.op("tensor", "matmul", PS[SCB[b]][:, 0:512], kt[b], qv, start=True, stop=True)
                P.op("scalar", "activation", et[b], PS[SCB[b]][:, 0:512], AF.Exp, bias=nb8[:, h:h + 1])

            ld(0)
            ld(1)
            sc(0)
            for kc in range(64):
                if kc + 2 < 64:
                    ld(kc + 2)
                if kc + 1 < 64:
                    sc(kc + 1)
                b = kc % 3
                for qi in range(4):
                    P.op("tensor", "matmul", PS[2 + (qi % 2)][:, (qi // 2) * 256:(qi // 2) * 256 + 129],
                         et[b][:, qi * 128:(qi + 1) * 128], va[b][:, 0:129],
                         start=(kc == 0 and qi < 2), stop=(kc == 63))
            for qi in range(4):
                pp = PS[2 + (qi % 2)]
                c0 = (qi // 2) * 256
                P.op("vector", "reciprocal", rc, pp[:, c0 + 128:c0 + 129])
                P.op("vector", "tensor_scalar", ev[qi % 4][:, 0:128], pp[:, c0:c0 + 128], rc,
                     None, op0=ALU.mult)
                P.dma("scalar", attnd[qt * 512 + qi * 128:qt * 512 + (qi + 1) * 128,
                                      h * 128:(h + 1) * 128], ev[qi % 4][:, 0:128])


def b_l1_wo(P):
    attnd = P.dram("attn").ap()
    g1d = P.dram("g1").ap()
    x1d = P.dram("x1").ap()
    outd = P.dram("out").ap()
    wod = P.dram("wo").ap()
    ar = [P.sb[f"ar{j}"] for j in range(8)]
    ev = [P.sb[f"ip_ev{i}"] for i in range(4)]
    PS = [P.sb[f"PS{i}"] for i in range(7)]
    ptr = P.sb["PT0"]
    for kc in range(8):
        P.dma("gpsimd", view(ar[4 + kc // 2], (kc % 2) * 4096, [[8192, 128], [1, 4096]]),
              wod[kc * 128:(kc + 1) * 128, :])
    for tt in range(NT):
        P.dma("sync", P.sb["sc_a"][:, 0:1024], attnd[tt * 128:(tt + 1) * 128, :])
        ab = bfv(P, "sc_b")
        P.op("vector", "tensor_copy", ab[:, 0:1024], P.sb["sc_a"][:, 0:1024])
        for kc in range(8):
            P.op("tensor", "transpose", ptr[:, (kc % 8) * 128:(kc % 8 + 1) * 128],
                 ab[:, kc * 128:(kc + 1) * 128], P.sb["c_idb"][:, :])
        P.op("vector", "tensor_copy", ab[:, 1024:2048], ptr[:, 0:1024])
        for og in range(8):
            pp = PS[og % 2]
            for kc in range(8):
                P.op("tensor", "matmul", pp[:, 0:512], ab[:, 1024 + kc * 128:1024 + (kc + 1) * 128],
                     view(ar[4 + kc // 2], (kc % 2) * 4096 + og * 512, [[8192, 128], [1, 512]]),
                     start=(kc == 0), stop=(kc == 7))
            P.dma("sync", P.sb["sc_c"][:, 0:512], g1d[tt * 128:(tt + 1) * 128, og * 512:(og + 1) * 512])
            P.dma("sync", P.sb["sc_c"][:, 512:1024], x1d[tt * 128:(tt + 1) * 128, og * 512:(og + 1) * 512])
            P.op("vector", "tensor_mul", ev[og % 2][:, 0:512], pp[:, 0:512], P.sb["sc_c"][:, 0:512])
            P.op("vector", "tensor_add", ev[2 + og % 2][:, 0:512], ev[og % 2][:, 0:512],
                 P.sb["sc_c"][:, 512:1024])
            P.dma("scalar", outd[tt * 128:(tt + 1) * 128, og * 512:(og + 1) * 512],
                  ev[2 + og % 2][:, 0:512])


L0_BUILDERS = None
L1_BUILDERS = None
L0_IN = ["xh", "relmask", "c_ident", "c_ones", "c_misc", "c_trineg", "wip", "convw", "convb",
         "dtbias", "alog", "dvec", "wg0", "wla0", "wlb0a", "wlb0b", "wb0t", "woutt", "nwt"]
L1_IN = ["c_ident", "c_ones", "c_misc", "wqr", "wkt", "wvt", "wo", "wb1t", "bb1",
         "wr1", "br1", "wla1", "wlb1"]
FUSED = True


def kernel(**inputs):
    l0b = [b_l0_ln, b_l0_inproj, b_l0_dt, b_l0_conv, b_l0_ffn, b_l0_ssd, b_l0_post]
    l1b = [b_l1_ln, b_l1_qkv, b_l1_ffn, b_l1_attn, b_l1_wo]
    W = host_weights(inputs)
    pc = [host_percore(inputs, c) for c in range(8)]
    out = np.zeros((2, SEQL if False else 8192, D), dtype=np.float32)
    if FUSED:
        nc = build_launch(l0b + l1b, ext_in=L0_IN + L1_IN, ext_out=["out"])
        maps = [{("t_" + n): (W[n] if n in W else pc[c][n]) for n in L0_IN + L1_IN}
                for c in range(8)]
        res = run_bass_kernel_spmd(nc, maps, core_ids=list(range(8)))
        for c in range(8):
            out[c // 4, (c % 4) * TPC:(c % 4 + 1) * TPC] = res.results[c]["t_out"]
        return out
    nc0 = build_launch(l0b, ext_in=L0_IN, ext_out=["x1"])
    maps0 = [{("t_" + n): (W[n] if n in W else pc[c][n]) for n in L0_IN} for c in range(8)]
    res0 = run_bass_kernel_spmd(nc0, maps0, core_ids=list(range(8)))
    nc1 = build_launch(l1b, ext_in=["x1"] + L1_IN, ext_out=["out"])
    maps1 = []
    for c in range(8):
        m = {"t_x1": np.ascontiguousarray(res0.results[c]["t_x1"])}
        for n in L1_IN:
            m["t_" + n] = W[n]
        maps1.append(m)
    res1 = run_bass_kernel_spmd(nc1, maps1, core_ids=list(range(8)))
    for c in range(8):
        out[c // 4, (c % 4) * TPC:(c % 4 + 1) * TPC] = res1.results[c]["t_out"]
    return out


# version 12 (final)
# speedup vs baseline: 1.3798x; 1.3798x over previous
import contextlib
import numpy as np
import ml_dtypes
import concourse.bass as bass
import concourse.mybir as mybir
from concourse.bass_types import AP as _AP
from concourse.bass_utils import run_bass_kernel_spmd

F32 = mybir.dt.float32
BF16 = mybir.dt.bfloat16
AF = mybir.ActivationFunctionType
ALU = mybir.AluOpType
AX = mybir.AxisListType.X
bfdt = ml_dtypes.bfloat16

D = 4096
DIN = 8192
H = 128
PP = 64
GG = 8
CH = 256
NCH = 8
TPC = 2048
NT = 16
DIP = 18560
NOC = DIP // 128
CONVD = 10240
EPS = 1e-5
RSQD = float(1.0 / np.sqrt(128.0))

ENGS = ("sync", "scalar", "vector", "tensor", "gpsimd")
CSEM = {"sync": "c_sy", "scalar": "ac", "vector": "dv", "tensor": "pe", "gpsimd": "gp"}
QSEM = {"sync": "d_sy", "scalar": "d_sc", "gpsimd": "d_gp"}
NSLOT = 16

SCRATCH = {
    "zt":     ([TPC, DIN], F32),
    "xnt":    ([TPC, D], F32),
    "xbc":    ([CONVD, 2051], F32),
    "xdt":    ([TPC, DIN], BF16),
    "dxt":    ([TPC, DIN], BF16),
    "bfm":    ([GG * 128, TPC], BF16),
    "cfm":    ([GG * 128, TPC], BF16),
    "yt":     ([TPC, DIN], F32),
    "expacs": ([TPC, H], F32),
    "cums":   ([NCH, H], F32),
    "gin":    ([128, DIN + 1], F32),
    "gout":   ([512, DIN + 1], F32),
    "goutx":  ([512, DIN + 1], F32),
    "gffn":   ([TPC, D], F32),
    "yfm":    ([NT * 64 * 128, 128], BF16),
    "x1":     ([TPC, D], F32),
    "wqe":    ([8, D, 128], BF16),
    "qf":     ([1024, TPC], BF16),
    "qkst":   ([1, 16], F32),
    "kvloc":  ([4097, 1024], BF16),
    "kvg":    ([4 * 4097, 1024], BF16),
    "kvgx":   ([4 * 4097, 1024], BF16),
    "g1":     ([TPC, D], F32),
    "attn":   ([TPC, 1024], F32),
    "out":    ([TPC, D], F32),
}

WSPEC = {
    "xh":      ([2051, D], F32),
    "wip":     ([NOC, 128, 32, 128], F32),
    "convw":   ([80, 128, 4], F32),
    "convb":   ([80, 128], F32),
    "dtbias":  ([H, 1], F32),
    "alog":    ([H, 1], F32),
    "dvec":    ([H, 1], F32),
    "normw":   ([1, DIN], F32),
    "wout":    ([DIN, D], F32),
    "woutt":   ([4, 64, 128, 1024], F32),
    "nwt":     ([128, 64], F32),
    "wb0":     ([D, D], F32),
    "wb0t":    ([32, 128, 32, 128], F32),
    "wg0":     ([D, 28], F32),
    "wla0":    ([D, 224], F32),
    "wlb0a":   ([128, D], F32),
    "wlb0b":   ([96, D], F32),
    "wqr":     ([4, D, 1024], F32),
    "wk":      ([D, 1024], F32),
    "wkt":     ([8, 128, 32, 128], F32),
    "wv":      ([D, 1024], F32),
    "wvt":     ([8, 128, 32, 128], F32),
    "wo":      ([1024, D], F32),
    "wb1":     ([D, D], F32),
    "wb1t":    ([32, 128, 32, 128], F32),
    "bb1":     ([1, D], F32),
    "wr1":     ([D, 8], F32),
    "br1":     ([1, 8], F32),
    "wla1":    ([D, 128], F32),
    "wlb1":    ([128, D], F32),
    "c_ident": ([128, 128], F32),
    "c_ones":  ([128, 128], F32),
    "c_mask":  ([2, 128, CH], F32),
    "c_trineg": ([128, 128], F32),
    "c_misc":  ([128, 4], F32),
    "relmask": ([128, 8], F32),
}


def ap_of(t):
    if isinstance(t, _AP):
        return t
    return t.ap() if callable(getattr(t, "ap", None)) else t[:, :]


def view(t, offset, dims):
    a = ap_of(t)
    return _AP(a.tensor, offset, [list(d) for d in dims])


class Op:
    __slots__ = ("eng", "fn", "deps", "sem", "need", "inc_k", "val", "seq")

    def __init__(self, eng, fn, sem, inc_k, seq):
        self.eng, self.fn, self.sem, self.inc_k, self.seq = eng, fn, sem, inc_k, seq
        self.deps = []
        self.need = False
        self.val = None


class Prog:
    def __init__(self):
        self.nc = bass.Bass("TRN2", target_bir_lowering=False, debug=False)
        self.ops = {e: [] for e in ENGS}
        self.sb = {}
        self.dramd = {}
        self._stack = None
        self.ext_in = set()
        self.ext_out = set()
        self.last_writer = {}
        self.readers = {}
        self._seq = 0
        self._slot = {q: 0 for q in QSEM}
        self._slot_last = {q: [None] * NSLOT for q in QSEM}

    def sbuf(self, name, shape, dtype):
        if name not in self.sb:
            self.sb[name] = self._stack.enter_context(self.nc.sbuf_tensor(name, shape, dtype))
        return self.sb[name]

    def psum(self, name, shape, dtype=F32):
        if name not in self.sb:
            self.sb[name] = self._stack.enter_context(self.nc.psum_tensor(name, shape, dtype))
        return self.sb[name]

    def dram(self, name):
        if name not in self.dramd:
            shape, dtype = SCRATCH[name] if name in SCRATCH else WSPEC[name]
            kind = ("ExternalInput" if name in self.ext_in else
                    "ExternalOutput" if name in self.ext_out else "Internal")
            self.dramd[name] = self.nc.dram_tensor("t_" + name, list(shape), dtype, kind=kind)
        return self.dramd[name]

    def _plan(self, eng, fn, reads, writes, sem, inc_k, extra_deps=()):
        self._seq += 1
        o = Op(eng, fn, sem, inc_k, self._seq)
        deps = {}

        def add(d):
            if d is None:
                return
            if eng == "tensor" and d.eng == "tensor":
                return
            cur = deps.get(d.sem)
            if cur is None or cur.seq < d.seq:
                deps[d.sem] = d

        rn = [ap.tensor.name for ap in reads]
        wn = [ap.tensor.name for ap in writes]
        for t in rn:
            add(self.last_writer.get(t))
        for t in wn:
            add(self.last_writer.get(t))
            for d in self.readers.get(t, {}).values():
                add(d)
        for d in extra_deps:
            add(d)
        for d in deps.values():
            d.need = True
        o.deps = list(deps.values())
        self.ops[eng].append(o)
        for t in wn:
            self.last_writer[t] = o
            self.readers[t] = {}
        for t in rn:
            self.readers.setdefault(t, {})[eng] = o
        return o

    def op(self, eng, meth, *args, extra_deps=(), **kw):
        writes, reads = [], []
        seen_w = False
        for a in args:
            if isinstance(a, _AP):
                if not seen_w:
                    writes.append(a)
                    seen_w = True
                else:
                    reads.append(a)
        for k, v in kw.items():
            if isinstance(v, _AP):
                (writes if k in ("accum_out", "out") else reads).append(v)

        def fn(e, _m=meth, _a=args, _k=kw):
            return getattr(e, _m)(*_a, **_k)
        return self._plan(eng, fn, reads, writes, CSEM[eng], 1, extra_deps)

    def dma(self, q, out, in_, extra_deps=(), **kw):
        def fn(e, _o=out, _i=in_, _k=kw):
            return e.dma_start(out=_o, in_=_i, **_k)
        slot = self._slot[q] % NSLOT
        self._slot[q] += 1
        prev = self._slot_last[q][slot]
        deps = list(extra_deps) + ([prev] if prev is not None else [])
        o = self._plan(q, fn, [in_], [out], QSEM[q] + str(slot), 16, deps)
        o.need = True
        self._slot_last[q][slot] = o
        return o

    def collective(self, kind, alu, groups, in_ap, out_ap, extra_deps=()):
        def fn(e, _i=in_ap, _o=out_ap):
            return e.collective_compute(kind, alu, replica_groups=groups, ins=[_i.opt()], outs=[_o.opt()])
        return self._plan("gpsimd", fn, [in_ap], [out_ap], "gp", 1, extra_deps)

    def lower(self):
        semnames = sorted(set(CSEM.values()) |
                          {q + str(i) for q in QSEM.values() for i in range(NSLOT)})
        cnt = {s: 0 for s in semnames}
        for eng in ENGS:
            for o in self.ops[eng]:
                if o.need:
                    cnt[o.sem] += o.inc_k
                    o.val = cnt[o.sem]
        bar = [(s, cnt[s]) for s in semnames if cnt[s] > 0]
        sems = {n: self._stack.enter_context(self.nc.semaphore(n)) for n in semnames}
        block = self._stack.enter_context(self.nc.Block())
        for eng in ENGS:
            ops = self.ops[eng]
            if eng == "sync":
                pass
            if not ops and eng != "sync":
                continue

            def mk(ops, fin):
                def f(e):
                    last = {}
                    for o in ops:
                        for d in o.deps:
                            if last.get(d.sem, 0) < d.val:
                                e.wait_ge(sems[d.sem], d.val)
                                last[d.sem] = d.val
                        instr = o.fn(e)
                        if o.need:
                            instr.then_inc(sems[o.sem], o.inc_k)
                    for s, v in fin:
                        if last.get(s, 0) < v:
                            e.wait_ge(sems[s], v)
                return f

            getattr(block, eng)(mk(ops, bar if eng == "sync" else []))


def build_launch(builders, ext_in, ext_out, dbg=()):
    P = Prog()
    P.ext_in = set(ext_in)
    P.ext_out = set(ext_out)
    with contextlib.ExitStack() as st:
        P._stack = st
        consts(P)
        for b in builders:
            b(P)
        for name, get_ap in dbg:
            ap = get_ap(P)
            t = P.nc.dram_tensor("t_dbg_" + name, list(ap.shape), ap.dtype, kind="ExternalOutput")
            P.dramd["dbg_" + name] = t
            P.dma("scalar", t.ap(), ap)
        P.lower()
    return P.nc


def consts(P):
    P.sbuf("c_idb", [128, 128], BF16)
    P.sbuf("c_idf", [128, 128], F32)
    P.sbuf("c_onec", [128, 1], BF16)
    P.sbuf("c_oner", [1, 128], BF16)
    P.sbuf("c_onerf", [1, 128], F32)
    P.sbuf("c_miscs", [128, 4], F32)
    for j in range(8):
        P.sbuf(f"ar{j}", [128, 8192], BF16)
    P.sbuf("sc_a", [128, 4096], F32)
    P.sbuf("sc_b", [128, 4096], F32)
    P.sbuf("sc_c", [128, 2051], F32)
    P.sbuf("sc_d", [128, 2051], F32)
    P.sbuf("wb0", [128, 2048], BF16)
    P.sbuf("wb1", [128, 2048], BF16)
    for i in range(4):
        P.sbuf(f"ip_ev{i}", [128, 512], F32)
    for i in range(7):
        P.psum(f"PS{i}", [128, 512], F32)
    P.psum("PT0", [128, 1024], BF16)
    ci = P.dram("c_ident").ap()
    co = P.dram("c_ones").ap()
    P.dma("sync", P.sb["c_idf"][:, :], ci)
    P.dma("gpsimd", P.sb["c_idb"][:, :], ci)
    P.dma("gpsimd", P.sb["c_onec"][:, :], co[:, 0:1], allow_slow_non_contiguous=True)
    P.dma("gpsimd", P.sb["c_oner"][:, :], co[0:1, :])
    P.dma("sync", P.sb["c_onerf"][:, :], co[0:1, :])
    P.dma("sync", P.sb["c_miscs"][:, :], P.dram("c_misc").ap())


def EPSC(P, nr=128):
    return P.sb["c_miscs"][0:nr, 0:1]


def ONEC(P, nr=128):
    return P.sb["c_miscs"][0:nr, 1:2]


def bfv(P, name):
    return P.sb[name][:, :].bitcast(BF16)


def xn_rhs(P, kc, c0, n):
    ar = P.sb[f"ar{kc // 4}"]
    return view(ar, (kc % 4) * 2048 + c0, [[8192, 128], [1, n]])


def b_l0_ln(P):
    xnh = P.sbuf("xnh", [128, 128], BF16)
    xh = P.dram("xh").ap()
    xt = P.sb["sc_a"]
    sqd = P.sb["sc_b"]
    xb = bfv(P, "sc_b")
    s_mu = P.sbuf("ln_mu", [128, 1], F32)
    s_nm = P.sbuf("ln_nm", [128, 1], F32)
    s_s2 = P.sbuf("ln_s2", [128, 1], F32)
    s_rs = P.sbuf("ln_rs", [128, 1], F32)
    ptr = P.psum("PT0", [128, 1024], BF16)

    for tt in range(-1, NT):
        r0, nr = (0, 3) if tt < 0 else (3 + tt * 128, 128)
        P.dma("sync", xt[0:nr, :], xh[r0:r0 + nr, :])
        P.op("vector", "reduce_sum", s_mu[0:nr, :], xt[0:nr, :], axis=AX)
        P.op("vector", "tensor_scalar", s_mu[0:nr, :], s_mu[0:nr, :], 1.0 / D, None, op0=ALU.mult)
        P.op("vector", "tensor_scalar", s_nm[0:nr, :], s_mu[0:nr, :], -1.0, None, op0=ALU.mult)
        P.op("scalar", "activation", sqd[0:nr, :], xt[0:nr, :], AF.Square, bias=s_nm[0:nr, :],
             accum_out=s_s2[0:nr, :])
        P.op("vector", "tensor_scalar", s_s2[0:nr, :], s_s2[0:nr, :], 1.0 / D, None, op0=ALU.mult)
        P.op("scalar", "activation", s_rs[0:nr, :], s_s2[0:nr, :], AF.Sqrt, bias=EPSC(P, nr))
        P.op("vector", "reciprocal", s_rs[0:nr, :], s_rs[0:nr, :])
        P.op("vector", "tensor_scalar", xb[0:nr, 0:4096], xt[0:nr, :], s_mu[0:nr, :], s_rs[0:nr, :],
             op0=ALU.subtract, op1=ALU.mult)
        if tt >= 0:
            P.dma("gpsimd", P.dram("xnt").ap()[tt * 128:(tt + 1) * 128, :], xb[0:nr, 0:4096])
        for half in range(8):
            for j in range(4):
                kc = half * 4 + j
                P.op("tensor", "transpose", ptr[:, j * 128:j * 128 + nr],
                     xb[0:nr, kc * 128:(kc + 1) * 128], P.sb["c_idb"][0:nr, 0:nr])
            if tt < 0:
                for j in range(4):
                    kc = half * 4 + j
                    P.op("vector", "tensor_copy", xnh[:, kc * 4:kc * 4 + 3], ptr[:, j * 128:j * 128 + 3])
            else:
                ar = P.sb[f"ar{half // 1 // 1}"]
                ar = P.sb[f"ar{half}"]
                dst = view(ar, tt * 128, [[8192, 128], [2048, 4], [1, 128]])
                src = view(ptr, 0, [[1024, 128], [128, 4], [1, 128]])
                P.op("vector", "tensor_copy", dst, src)


def b_l0_inproj(P):
    xnh = P.sb["xnh"]
    wip = P.dram("wip").ap()
    ztd = P.dram("zt").ap()
    xbcd = P.dram("xbc").ap()
    wb = [P.sb["wb0"], P.sb["wb1"]]
    ev = [P.sb[f"ip_ev{i}"] for i in range(4)]
    ztr = P.sbuf("ip_ztr", [128, 512], F32)
    hev = P.sbuf("ip_hev", [128, 4], F32)
    dtr = P.sb["sc_c"]
    pm = [P.psum(f"PS{i}", [128, 512], F32) for i in range(4)]
    ptf = P.psum("PS4", [128, 512], F32)

    for oc in range(NOC):
        P.dma("gpsimd", wb[0][:, :], wip[oc, :, 0:16, :])
        P.dma("gpsimd", wb[1][:, :], wip[oc, :, 16:32, :])
        is_z, is_dt = oc < 64, oc == NOC - 1
        for ts4 in range(4):
            pi = ts4
            for kc in range(32):
                P.op("tensor", "matmul", pm[pi][:, :],
                     wb[kc // 16][:, (kc % 16) * 128:(kc % 16 + 1) * 128],
                     xn_rhs(P, kc, ts4 * 512, 512),
                     start=(kc == 0), stop=(kc == 31))
            if is_dt:
                P.op("vector", "tensor_copy", dtr[:, ts4 * 512:(ts4 + 1) * 512], pm[pi][:, :])
            elif is_z:
                P.op("vector", "tensor_copy", ev[pi][:, :], pm[pi][:, :])
                for j in range(4):
                    P.op("tensor", "transpose", ptf[:, j * 128:(j + 1) * 128],
                         ev[pi][:, j * 128:(j + 1) * 128], P.sb["c_idf"][:, :])
                P.op("vector", "tensor_copy", ztr[:, :], ptf[:, :])
                dst = view(ztd, (ts4 * 512) * DIN + oc * 128, [[DIN, 128], [DIN * 128, 4], [1, 128]])
                src = view(ztr, 0, [[512, 128], [128, 4], [1, 128]])
                P.dma("scalar", dst, src)
            else:
                P.op("vector", "tensor_copy", ev[pi][:, :], pm[pi][:, :])
                r0 = (oc - 64) * 128
                P.dma("scalar", xbcd[r0:r0 + 128, 3 + ts4 * 512:3 + (ts4 + 1) * 512], ev[pi][:, :])
        if (not is_z) and (not is_dt):
            for kc in range(32):
                P.op("tensor", "matmul", pm[0][:, 0:3],
                     wb[kc // 16][:, (kc % 16) * 128:(kc % 16 + 1) * 128],
                     xnh[:, kc * 4:kc * 4 + 3], start=(kc == 0), stop=(kc == 31))
            P.op("vector", "tensor_copy", hev[:, 0:3], pm[0][:, 0:3])
            r0 = (oc - 64) * 128
            P.dma("scalar", xbcd[r0:r0 + 128, 0:3], hev[:, 0:3])


def b_l0_dt(P):
    cumd = P.dram("cums").ap()
    dtb = P.sbuf("dtb", [128, 1], F32)
    alog = P.sbuf("alogv", [128, 1], F32)
    dvec = P.sbuf("dvecv", [128, 1], F32)
    dtr = P.sb["sc_c"]
    tmp = P.sb["sc_a"]
    acs2 = P.sb["sc_a"]
    acs = P.sbuf("acs_fm", [128, TPC], F32)
    dtt = P.sb["sc_b"]
    dbc = P.sbuf("d_bc", [128, 128], F32)
    cdec = P.sbuf("cdec", [128, 9], F32)
    rowt = P.sbuf("dt_rowt", [1, 128], F32)
    ptf = P.sb["PS4"]
    ptg = P.sb["PS3"]
    P.dma("sync", dtb[:, :], P.dram("dtbias").ap())
    P.dma("sync", alog[:, :], P.dram("alog").ap())
    P.dma("sync", dvec[:, :], P.dram("dvec").ap())

    T = lambda c0, n: tmp[:, c0:c0 + n]
    P.op("scalar", "activation", T(0, 2048), dtr[:, 0:2048], AF.Abs, bias=dtb[:, :])
    P.op("scalar", "activation", T(0, 2048), T(0, 2048), AF.Exp, scale=-1.0)
    P.op("scalar", "activation", T(0, 2048), T(0, 2048), AF.Ln, bias=ONEC(P))
    P.op("vector", "tensor_scalar", dtr[:, 0:2048], dtr[:, 0:2048], dtb[:, :], 0.0, op0=ALU.add, op1=ALU.max)
    P.op("vector", "tensor_add", dtr[:, 0:2048], dtr[:, 0:2048], T(0, 2048))
    P.op("scalar", "activation", alog[:, :], alog[:, :], AF.Exp)
    P.op("vector", "tensor_scalar", alog[:, :], alog[:, :], -1.0, None, op0=ALU.mult)
    P.op("vector", "tensor_scalar", acs[:, :], dtr[:, 0:2048], alog[:, :], None, op0=ALU.mult)
    for c in range(NCH):
        o = c * CH
        in_acs = True
        for sh in (1, 2, 4, 8, 16, 32, 64, 128):
            if in_acs:
                s_t, s_o, d_t, d_o = acs, o, acs2, 2048
            else:
                s_t, s_o, d_t, d_o = acs2, 2048, acs, o
            P.op("vector", "tensor_copy", d_t[:, d_o:d_o + sh], s_t[:, s_o:s_o + sh])
            P.op("vector", "tensor_add", d_t[:, d_o + sh:d_o + CH], s_t[:, s_o + sh:s_o + CH],
                 s_t[:, s_o:s_o + CH - sh])
            in_acs = not in_acs
        assert in_acs
    P.op("vector", "tensor_copy", cdec[:, 0:8], view(acs, CH - 1, [[TPC, 128], [CH, 8]]))
    P.op("vector", "memset", T(0, 1), 0.0)
    for z in range(1, 9):
        P.op("vector", "tensor_add", T(z, 1), T(z - 1, 1), cdec[:, z - 1:z])
    P.op("vector", "tensor_copy", cdec[:, 8:9], T(8, 1))
    P.op("scalar", "activation", cdec[:, :], cdec[:, :], AF.Exp)
    P.op("scalar", "activation", T(0, 9), T(0, 9), AF.Exp)
    P.op("tensor", "transpose", ptf[0:8, 0:128], T(0, 8), P.sb["c_idf"][:, :])
    P.op("vector", "tensor_copy", P.sb["ip_ev0"][0:8, 0:128], ptf[0:8, 0:128])
    P.dma("scalar", cumd[:, :], P.sb["ip_ev0"][0:8, 0:128])
    P.op("tensor", "transpose", ptg[0:1, 0:128], dvec[:, 0:1], P.sb["c_idf"][:, :])
    P.op("vector", "tensor_copy", rowt[:, :], ptg[0:1, 0:128])
    P.op("tensor", "matmul", ptg[:, 128:256], P.sb["c_onerf"][:, :], rowt[:, :], start=True, stop=True)
    P.op("vector", "tensor_copy", dbc[:, :], ptg[:, 128:256])
    for tt in range(NT):
        P.op("tensor", "transpose", ptf[:, 0:128], dtr[:, tt * 128:(tt + 1) * 128], P.sb["c_idf"][:, :])
        P.op("vector", "tensor_copy", dtt[:, tt * 128:(tt + 1) * 128], ptf[:, 0:128])


def b_l0_conv(P):
    xbcd = P.dram("xbc").ap()
    xdtd = P.dram("xdt").ap()
    dxtd = P.dram("dxt").ap()
    bfmd = P.dram("bfm").ap()
    cfmd = P.dram("cfm").ap()
    cw = P.sbuf("cv_w", [128, 320], F32)
    cb = P.sbuf("cv_b", [128, 80], F32)
    cin = [P.sb["sc_c"], P.sb["sc_d"]]
    cacc = P.sb["sc_a"]
    cbf = bfv(P, "sc_a")
    xevb = [P.sb["ip_ev0"][:, :].bitcast(BF16), P.sb["ip_ev1"][:, :].bitcast(BF16)]
    devb = [P.sb["ip_ev2"][:, :].bitcast(BF16), P.sb["ip_ev3"][:, :].bitcast(BF16)]
    pbf = P.sb["PT0"]
    dtt = P.sb["sc_b"]
    dbc = P.sb["d_bc"]
    P.dma("sync", cw[:, :], view(P.dram("convw"), 0, [[4, 128], [512, 80], [1, 4]]))
    P.dma("sync", cb[:, :], view(P.dram("convb"), 0, [[1, 128], [128, 80]]), allow_slow_non_contiguous=True)
    P.dma("sync", cin[0][:, 0:2051], xbcd[0:128, :])
    for fc in range(80):
        bb = fc % 2
        if fc + 1 < 80:
            P.dma("sync", cin[(fc + 1) % 2][:, 0:2051], xbcd[(fc + 1) * 128:(fc + 2) * 128, :])
        P.op("vector", "tensor_scalar", cacc[:, 0:2048], cin[bb][:, 0:TPC], cw[:, fc * 4:fc * 4 + 1],
             None, op0=ALU.mult)
        for k in range(1, 4):
            P.op("vector", "scalar_tensor_tensor", cacc[:, 0:2048], cin[bb][:, k:k + TPC],
                 cw[:, fc * 4 + k:fc * 4 + k + 1], cacc[:, 0:2048], op0=ALU.mult, op1=ALU.add)
        P.op("scalar", "activation", cacc[:, 0:2048], cacc[:, 0:2048], AF.Silu, bias=cb[:, fc:fc + 1])
        P.op("vector", "tensor_copy", cbf[:, 4096:6144], cacc[:, 0:2048])
        if fc < 64:
            h0 = 2 * fc
            for tb8 in range(2):
                for j in range(8):
                    tt = tb8 * 8 + j
                    P.op("tensor", "transpose", pbf[:, j * 128:(j + 1) * 128],
                         cbf[:, 4096 + tt * 128:4096 + (tt + 1) * 128], P.sb["c_idb"][:, :])
                pv = view(pbf, 0, [[1024, 128], [128, 8], [64, 2], [1, 64]])
                dtv = view(dtt, tb8 * 1024 + h0, [[4096, 128], [128, 8], [1, 2], [0, 64]])
                dbv = view(dbc, h0, [[128, 128], [0, 8], [1, 2], [0, 64]])
                P.op("vector", "tensor_tensor",
                     view(xevb[tb8], 0, [[1024, 128], [128, 8], [64, 2], [1, 64]]),
                     pv, dtv, op=ALU.mult)
                P.op("vector", "tensor_tensor",
                     view(devb[tb8], 0, [[1024, 128], [128, 8], [64, 2], [1, 64]]),
                     pv, dbv, op=ALU.mult)
                P.dma("scalar",
                      view(xdtd, (tb8 * 8 * 128) * DIN + fc * 128,
                           [[DIN, 128], [128 * DIN, 8], [1, 128]]),
                      view(xevb[tb8], 0, [[1024, 128], [128, 8], [1, 128]]))
                P.dma("scalar",
                      view(dxtd, (tb8 * 8 * 128) * DIN + fc * 128,
                           [[DIN, 128], [128 * DIN, 8], [1, 128]]),
                      view(devb[tb8], 0, [[1024, 128], [128, 8], [1, 128]]))
        else:
            r0 = (fc - 64) * 128 if fc < 72 else (fc - 72) * 128
            dst = bfmd if fc < 72 else cfmd
            P.dma("scalar", dst[r0:r0 + 128, :], cbf[:, 4096:6144])


def tile4(Wt):
    K, M = Wt.shape
    return np.ascontiguousarray(Wt.reshape(K // 128, 128, M // 128, 128).transpose(2, 1, 0, 3))


def host_weights(inputs):
    W = {}
    wip = inputs["L0_attn_in_proj_w"]
    W["wip"] = np.ascontiguousarray(wip.reshape(NOC, 128, 32, 128).transpose(0, 3, 2, 1))
    W["convw"] = np.ascontiguousarray(inputs["L0_attn_conv_w"].reshape(80, 128, 4))
    W["convb"] = np.ascontiguousarray(inputs["L0_attn_conv_b"].reshape(80, 128))
    W["dtbias"] = np.ascontiguousarray(inputs["L0_attn_dt_bias"].reshape(H, 1))
    W["alog"] = np.ascontiguousarray(inputs["L0_attn_A_log"].reshape(H, 1))
    W["dvec"] = np.ascontiguousarray(inputs["L0_attn_D"].reshape(H, 1))
    W["normw"] = np.ascontiguousarray(inputs["L0_attn_norm_w"].reshape(1, DIN))
    W["wout"] = np.ascontiguousarray(inputs["L0_attn_out_proj_w"].T)
    W["woutt"] = np.ascontiguousarray(W["wout"].reshape(64, 128, 4, 1024).transpose(2, 0, 1, 3))
    W["nwt"] = np.ascontiguousarray(inputs["L0_attn_norm_w"].reshape(64, 128).T)
    W["wb0"] = np.ascontiguousarray(inputs["L0_ffn_base_W"].T)
    W["wb0t"] = tile4(W["wb0"])
    W["wg0"] = np.ascontiguousarray(inputs["L0_ffn_gate_W"].T)
    W["wla0"] = np.ascontiguousarray(inputs["L0_ffn_lora_A"].transpose(2, 0, 1).reshape(D, 224))
    lb0 = np.ascontiguousarray(inputs["L0_ffn_lora_B"].transpose(0, 2, 1).reshape(224, D))
    W["wlb0a"] = np.ascontiguousarray(lb0[0:128])
    W["wlb0b"] = np.ascontiguousarray(lb0[128:224])
    W["wqr"] = np.ascontiguousarray(
        inputs["L1_attn_wq"].reshape(D, 8, 4, 128).transpose(2, 0, 1, 3).reshape(4, D, 1024))
    W["wk"] = np.ascontiguousarray(inputs["L1_attn_wk"])
    W["wkt"] = tile4(W["wk"])
    W["wv"] = np.ascontiguousarray(inputs["L1_attn_wv"])
    W["wvt"] = tile4(W["wv"])
    W["wo"] = np.ascontiguousarray(inputs["L1_attn_wo"])
    W["wb1"] = np.ascontiguousarray(inputs["L1_ffn_W_base"].T)
    W["wb1t"] = tile4(W["wb1"])
    W["bb1"] = np.ascontiguousarray(inputs["L1_ffn_b_base"].reshape(1, D))
    W["wr1"] = np.ascontiguousarray(inputs["L1_ffn_W_router"].T)
    W["br1"] = np.ascontiguousarray(inputs["L1_ffn_b_router"].reshape(1, 8))
    W["wla1"] = np.ascontiguousarray(inputs["L1_ffn_lora_A"].transpose(1, 0, 2).reshape(D, 128))
    W["wlb1"] = np.ascontiguousarray(inputs["L1_ffn_lora_B"].reshape(128, D))
    W["c_ident"] = np.eye(128, dtype=np.float32)
    W["c_ones"] = np.ones((128, 128), dtype=np.float32)
    msk = np.zeros((2, 128, CH), dtype=np.float32)
    s = np.arange(128)[:, None]
    l = np.arange(CH)[None, :]
    msk[0] = (s <= l).astype(np.float32)
    msk[1] = ((s + 128) <= l).astype(np.float32)
    W["c_mask"] = msk
    tn = np.zeros((128, 128), dtype=np.float32)
    tn[np.arange(128)[:, None] > np.arange(128)[None, :]] = -1e30
    W["c_trineg"] = tn
    cmisc = np.zeros((128, 4), dtype=np.float32)
    cmisc[:, 0] = EPS
    cmisc[:, 1] = 1.0
    cmisc[:, 2] = RSQD
    W["c_misc"] = cmisc
    return W


def host_percore(inputs, core):
    seq, pos = core // 4, core % 4
    x = inputs["L0_attn_hidden_states"]
    xh = np.zeros((2051, D), dtype=np.float32)
    lo = pos * TPC
    if pos == 0:
        xh[3:] = x[seq, 0:TPC]
    else:
        xh[:] = x[seq, lo - 3:lo + TPC]
    rel = np.zeros((128, 8), dtype=np.float32)
    for r in range(4):
        rel[:, r] = 1.0 if r < pos else 0.0
        rel[:, 4 + r] = 1.0 - rel[:, r]
    return {"xh": np.ascontiguousarray(xh), "relmask": rel}


def f32v(P, name):
    return P.sb[name][:, :].bitcast(F32)


def arf(P, j):
    return P.sb[f"ar{j}"][:, :].bitcast(F32)


def bcast_ap(base_ap, inner_rep, n_outer, outer_stride, offset=0):
    return _AP(base_ap.tensor, base_ap.offset + offset,
               [list(base_ap.ap[0]), [outer_stride, n_outer], [0, inner_rep]])


def b_l0_ffn(P):
    gffnd = P.dram("gffn").ap()
    wla = P.sb["sc_a"][:, :].bitcast(BF16)
    wg = P.sb["sc_b"][:, :].bitcast(BF16)
    lba = P.sb["sc_a"][:, :].bitcast(BF16)
    ws = [P.sb["sc_c"][:, :].bitcast(BF16), P.sb["sc_d"][:, :].bitcast(BF16)]
    m1 = P.sbuf("f_m1", [128, 1], F32)
    nm1 = P.sbuf("f_nm1", [128, 1], F32)
    m2 = P.sbuf("f_m2", [128, 1], F32)
    ps = [P.sb["PS0"], P.sb["PS1"], P.sb["PS2"], P.sb["PS3"]]
    ps4 = P.sb["PS4"]
    ptr = P.sb["PT0"]
    hf = [P.sb["wb0"], P.sb["wb1"]]
    ev = [P.sb[f"ip_ev{i}"] for i in range(4)]
    hp = ev[2][:, 0:224]
    hb = ev[3][:, :].bitcast(BF16)[:, 0:224]
    lg = ev[0][:, 0:28]
    gt = ev[0][:, 32:60]

    P.dma("gpsimd", wla[:, 0:7168].rearrange("p (k r) -> p k r", k=32),
          view(P.dram("wla0"), 0, [[224, 128], [128 * 224, 32], [1, 224]]))
    P.dma("gpsimd", wg[:, 0:896].rearrange("p (k r) -> p k r", k=32),
          view(P.dram("wg0"), 0, [[28, 128], [128 * 28, 32], [1, 28]]))

    for tt in range(NT):
        for kc in range(32):
            P.op("tensor", "matmul", ps4[:, 0:28], xn_rhs(P, kc, tt * 128, 128),
                 wg[:, kc * 28:(kc + 1) * 28],
                 start=(kc == 0), stop=(kc == 31))
        for kc in range(32):
            P.op("tensor", "matmul", ps4[:, 32:256], xn_rhs(P, kc, tt * 128, 128),
                 wla[:, kc * 224:(kc + 1) * 224],
                 start=False, stop=(kc == 31))
        P.op("vector", "tensor_copy", lg, ps4[:, 0:28])
        P.op("vector", "tensor_copy", hp, ps4[:, 32:256])
        P.op("vector", "reduce_max", m1[:, :], lg, axis=AX)
        P.op("vector", "tensor_scalar", nm1[:, :], m1[:, :], -1.0, None, op0=ALU.mult)
        P.op("vector", "tensor_scalar", gt, lg, m1[:, :], None, op0=ALU.is_equal)
        P.op("vector", "scalar_tensor_tensor", gt, gt, -1e30, lg, op0=ALU.mult, op1=ALU.add)
        P.op("vector", "reduce_max", m2[:, :], gt, axis=AX)
        P.op("vector", "tensor_scalar", gt, lg, m2[:, :], None, op0=ALU.is_ge)
        P.op("scalar", "activation", lg, lg, AF.Exp, bias=nm1[:, :])
        P.op("vector", "tensor_mul", lg, lg, gt)
        P.op("vector", "reduce_sum", m2[:, :], lg, axis=AX)
        P.op("vector", "reciprocal", m2[:, :], m2[:, :])
        P.op("vector", "tensor_scalar", gt, lg, m2[:, :], 2.0, op0=ALU.mult, op1=ALU.mult)
        gv = _AP(gt.tensor, gt.offset, [list(gt.ap[0]), [1, 28], [0, 8]])
        P.op("vector", "tensor_tensor", hb.rearrange("p (e r) -> p e r", e=28),
             hp.rearrange("p (e r) -> p e r", e=28), gv, op=ALU.mult)
        P.op("tensor", "transpose", ptr[:, 0:128], hb[:, 0:128], P.sb["c_idb"][:, :])
        P.op("tensor", "transpose", ptr[0:96, 128:256], hb[:, 128:224], P.sb["c_idb"][:, :])
        P.op("vector", "tensor_copy", hf[0][:, tt * 128:(tt + 1) * 128], ptr[:, 0:128])
        P.op("vector", "tensor_copy", hf[1][0:96, tt * 128:(tt + 1) * 128], ptr[0:96, 128:256])

    P.dma("gpsimd", lba[:, 0:4096], P.dram("wlb0a").ap())
    P.dma("gpsimd", lba[0:96, 4096:8192], P.dram("wlb0b").ap())
    wbt = P.dram("wb0t").ap()
    for og in range(32):
        sb = og % 2
        P.dma("gpsimd", ws[sb][:, 0:4096], wbt[og])
        for tt in range(NT):
            pp = ps[tt % 4]
            for kc in range(32):
                P.op("tensor", "matmul", pp[:, 0:128], xn_rhs(P, kc, tt * 128, 128),
                     ws[sb][:, kc * 128:(kc + 1) * 128],
                     start=(kc == 0), stop=False)
            P.op("tensor", "matmul", pp[:, 0:128], hf[0][:, tt * 128:(tt + 1) * 128],
                 lba[:, og * 128:(og + 1) * 128], start=False, stop=False)
            P.op("tensor", "matmul", pp[:, 0:128], hf[1][0:96, tt * 128:(tt + 1) * 128],
                 lba[0:96, 4096 + og * 128:4096 + (og + 1) * 128], start=False, stop=True)
            gev = ev[tt % 4]
            P.op("scalar", "activation", gev[:, 0:128], pp[:, 0:128], AF.Silu)
            P.dma("scalar", gffnd[tt * 128:(tt + 1) * 128, og * 128:(og + 1) * 128], gev[:, 0:128])


def b_l0_ssd(P):
    ytd = P.dram("yt").ap()
    ead = P.dram("expacs").ap()
    bfmd = P.dram("bfm").ap()
    cfmd = P.dram("cfm").ap()
    xdtd = P.dram("xdt").ap()
    gind = P.dram("gin").ap()
    goutd = P.dram("gout").ap()
    acs = P.sb["acs_fm"]
    cdec = P.sb["cdec"]
    idb = P.sb["c_idb"]
    idf = P.sb["c_idf"]
    trn = P.sbuf("c_trn", [128, 128], BF16)
    P.dma("gpsimd", trn[:, :], P.dram("c_trineg").ap())
    ar = [P.sb[f"ar{j}"] for j in range(8)]
    ev = [P.sb[f"ip_ev{i}"] for i in range(4)]
    PS = [P.sb[f"PS{i}"] for i in range(7)]
    PT0 = P.sb["PT0"]
    A5 = dict(bch=0, cch=2048, bt0=4096, bt1=5120, mt=6144, ah=7296, al=7552, ds=7808)
    A6 = dict(G=0, nacs=3072, eacs=3328, cdb=3584, st1=3712)

    def a5(c0, n):
        return view(ar[5], c0, [[8192, 128], [1, n]])

    def a6(c0, n, p=128):
        return _AP(arf(P, 6).tensor, c0, [[4096, p], [1, n]]) if False else arf(P, 6)[:, c0:c0 + n]

    for c in range(NCH):
        o = c * CH
        P.dma("sync", a5(A5["bch"], 2048).rearrange("p (g s) -> p g s", g=8),
              view(P.dram("bfm"), o, [[2048, 128], [262144, 8], [1, 256]]))
        P.dma("sync", a5(A5["cch"], 2048).rearrange("p (g s) -> p g s", g=8),
              view(P.dram("cfm"), o, [[2048, 128], [262144, 8], [1, 256]]))
        P.dma("sync", ar[3][:, :], xdtd[o:o + 128, :])
        P.dma("sync", ar[4][:, :], xdtd[o + 128:o + 256, :])
        P.op("vector", "tensor_copy", a5(A5["ah"], 256), acs[:, o:o + CH])
        P.op("vector", "tensor_copy", ev[0][:, 0:256], a5(A5["ah"], 256))
        P.op("vector", "tensor_sub", ev[0][:, 256:512], acs[:, o:o + CH], ev[0][:, 0:256])
        P.op("vector", "tensor_copy", a5(A5["al"], 256), ev[0][:, 256:512])
        for a in range(2):
            P.op("tensor", "transpose", PS[4][:, a * 128:(a + 1) * 128],
                 acs[:, o + a * 128:o + (a + 1) * 128], idf[:, :])
        P.op("vector", "tensor_scalar", a6(A6["nacs"], 256), PS[4][:, 0:256], -1.0, None, op0=ALU.mult)
        P.op("scalar", "activation", a6(A6["eacs"], 256), PS[4][:, 0:256], AF.Exp)
        for a in range(2):
            P.dma("scalar", ead[o + a * 128:o + (a + 1) * 128, :],
                  a6(A6["eacs"] + a * 128, 128))
        P.op("vector", "tensor_scalar", a6(A6["st1"], 256), acs[:, o:o + CH],
             acs[:, o + CH - 1:o + CH], None, op0=ALU.subtract)
        P.op("scalar", "activation", a6(A6["st1"], 256), a6(A6["st1"], 256), AF.Exp, scale=-1.0)
        for a in range(2):
            P.op("tensor", "transpose", PS[4][:, 256 + a * 128:256 + (a + 1) * 128],
                 a6(A6["st1"] + a * 128, 128), idf[:, :])
        P.op("vector", "tensor_copy", a5(A5["ds"], 256), PS[4][:, 256:512])
        P.op("tensor", "transpose", PS[4][0:1, 0:128], cdec[:, c:c + 1], idf[:, :])
        P.op("vector", "tensor_copy", P.sb["dt_rowt"][:, :], PS[4][0:1, 0:128])
        P.op("tensor", "matmul", PS[4][:, 128:256], P.sb["c_onerf"][:, :], P.sb["dt_rowt"][:, :],
             start=True, stop=True)
        P.op("vector", "tensor_copy", a6(A6["cdb"], 128), PS[4][:, 128:256])
        for a in range(2):
            for g in range(8):
                P.op("tensor", "transpose", PT0[:, g * 128:(g + 1) * 128],
                     a5(A5["bch"] + g * 256 + a * 128, 128), idb[:, :])
            P.op("vector", "tensor_copy", a5(A5["bt0"] if a == 0 else A5["bt1"], 1024), PT0[:, 0:1024])
        if c > 0:
            P.op("vector", "tensor_copy", ar[2][:, 0:4096], arf(P, 0)[:, :])
            P.op("vector", "tensor_copy", ar[2][:, 4096:8192], arf(P, 1)[:, :])
        for g in range(8):
            P.op("tensor", "matmul", PS[0][:, 0:256], a5(A5["bch"] + g * 256, 128),
                 a5(A5["cch"] + g * 256, 256), start=True, stop=True)
            P.op("tensor", "matmul", PS[0][:, 256:384], a5(A5["bch"] + g * 256 + 128, 128),
                 a5(A5["cch"] + g * 256 + 128, 128), start=False, stop=True)
            P.op("vector", "tensor_copy", a6(A6["G"] + g * 384, 256), PS[0][:, 0:256])
            P.op("vector", "tensor_copy", a6(A6["G"] + g * 384 + 256, 128), PS[0][:, 256:384])
        RBS = [PS[0], PS[5], PS[6]]
        EBS = [ev[0][:, 0:384], ev[1][:, 0:384], P.sb["sc_a"][:, 0:384]]

        def emit_R(h):
            lhb = _AP(idb[:, h:h + 1].tensor, idb[:, h:h + 1].offset,
                      [list(idb[:, h:h + 1].ap[0]), [0, 128]])
            RB = RBS[h % 3]
            P.op("tensor", "matmul", RB[:, 0:256], lhb, a5(A5["ah"], 256), start=True, stop=False)
            P.op("tensor", "matmul", RB[:, 0:256], lhb, a5(A5["al"], 256), start=False, stop=False)
            P.op("tensor", "matmul", RB[:, 256:384], lhb, a5(A5["ah"] + 128, 128),
                 start=False, stop=False)
            P.op("tensor", "matmul", RB[:, 256:384], lhb, a5(A5["al"] + 128, 128),
                 start=False, stop=False)
            P.op("tensor", "matmul", RB[:, 0:128], P.sb["c_idb"][:, :], trn[:, :],
                 start=False, stop=False)
            P.op("tensor", "matmul", RB[:, 256:384], P.sb["c_idb"][:, :], trn[:, :],
                 start=False, stop=True)

        def emit_Mt(h):
            g = h // 16
            RB = RBS[h % 3]
            eb = EBS[h % 3]
            mt_ = A5["mt"] + (h % 3) * 384
            P.op("scalar", "activation", eb[:, 0:256], RB[:, 0:256], AF.Exp,
                 bias=arf(P, 6)[:, A6["nacs"] + h:A6["nacs"] + h + 1])
            P.op("scalar", "activation", eb[:, 256:384], RB[:, 256:384], AF.Exp,
                 bias=arf(P, 6)[:, A6["nacs"] + 128 + h:A6["nacs"] + 128 + h + 1])
            P.op("vector", "tensor_mul", a5(mt_, 256), eb[:, 0:256], a6(A6["G"] + g * 384, 256))
            P.op("vector", "tensor_mul", a5(mt_ + 256, 128), eb[:, 256:384],
                 a6(A6["G"] + g * 384 + 256, 128))

        emit_R(0)
        emit_Mt(0)
        emit_R(1)
        emit_Mt(1)
        emit_R(2)
        for hg in range(16):
            for hh in range(8):
                h = hg * 8 + hh
                g = h // 16
                if h + 2 < 128:
                    emit_Mt(h + 2)
                if h + 3 < 128:
                    emit_R(h + 3)
                mt_ = A5["mt"] + (h % 3) * 384
                first = hh == 0
                last = hh == 7
                P.op("tensor", "matmul", PS[1][:, hh * 64:(hh + 1) * 64], a5(mt_, 128),
                     ar[3][:, h * 64:(h + 1) * 64], start=first, stop=last)
                P.op("tensor", "matmul", PS[2][:, hh * 64:(hh + 1) * 64], a5(mt_ + 128, 128),
                     ar[3][:, h * 64:(h + 1) * 64], start=first, stop=False)
                P.op("tensor", "matmul", PS[2][:, hh * 64:(hh + 1) * 64], a5(mt_ + 256, 128),
                     ar[4][:, h * 64:(h + 1) * 64], start=False, stop=last)
                if c > 0:
                    sbf = view(ar[2], h * 64, [[8192, 128], [1, 64]])
                    P.op("tensor", "matmul", PS[3][:, hh * 64:(hh + 1) * 64],
                         a5(A5["cch"] + g * 256, 128), sbf, start=first, stop=last)
                    P.op("tensor", "matmul", PS[4][:, hh * 64:(hh + 1) * 64],
                         a5(A5["cch"] + g * 256 + 128, 128), sbf, start=first, stop=last)
            for lh in range(2):
                dps = PS[1] if lh == 0 else PS[2]
                ops_ = PS[3] if lh == 0 else PS[4]
                eav = _AP(arf(P, 6).tensor, A6["eacs"] + lh * 128 + hg * 8,
                          [[4096, 128], [1, 8], [0, 64]])
                if c > 0:
                    P.op("vector", "tensor_tensor", ev[2][:, 0:512].rearrange("p (a b) -> p a b", a=8),
                         ops_[:, 0:512].rearrange("p (a b) -> p a b", a=8), eav, op=ALU.mult)
                    P.op("vector", "scalar_tensor_tensor", ev[3][:, 0:512], dps[:, 0:512], 1.0,
                         ev[2][:, 0:512], op0=ALU.mult, op1=ALU.add)
                else:
                    P.op("vector", "tensor_copy", ev[3][:, 0:512], dps[:, 0:512])
                P.dma("scalar", ytd[o + lh * 128:o + (lh + 1) * 128, hg * 512:(hg + 1) * 512],
                      ev[3][:, 0:512])
        for a in range(2):
            dsv = _AP(a5(A5["ds"], 256).tensor, a5(0, 1).offset + A5["ds"] + a * 128,
                      [[8192, 128], [1, 128], [0, 64]])
            P.op("vector", "tensor_tensor", ar[3 + a][:, :].rearrange("p (h q) -> p h q", h=128),
                 ar[3 + a][:, :].rearrange("p (h q) -> p h q", h=128), dsv, op=ALU.mult)
        for hg in range(16):
            sps = PS[1] if hg % 2 == 0 else PS[2]
            for hh in range(8):
                h = hg * 8 + hh
                g = h // 16
                P.op("tensor", "matmul", sps[:, hh * 64:(hh + 1) * 64], a5(A5["bt0"] + g * 128, 128),
                     ar[3][:, h * 64:(h + 1) * 64], start=(hh == 0), stop=False)
                P.op("tensor", "matmul", sps[:, hh * 64:(hh + 1) * 64], a5(A5["bt1"] + g * 128, 128),
                     ar[4][:, h * 64:(h + 1) * 64], start=False, stop=(hh == 7))
            sr = arf(P, 0)[:, (hg % 8) * 512:(hg % 8 + 1) * 512] if hg < 8 else                  arf(P, 1)[:, (hg - 8) * 512:(hg - 8 + 1) * 512]
            if c > 0:
                cdv = _AP(arf(P, 6).tensor, A6["cdb"] + hg * 8, [[4096, 128], [1, 8], [0, 64]])
                P.op("vector", "tensor_tensor", ev[0][:, 0:512].rearrange("p (a b) -> p a b", a=8),
                     sr.rearrange("p (a b) -> p a b", a=8), cdv, op=ALU.mult)
                P.op("vector", "scalar_tensor_tensor", sr, sps[:, 0:512], 1.0, ev[0][:, 0:512],
                     op0=ALU.mult, op1=ALU.add)
            else:
                P.op("vector", "tensor_copy", sr, sps[:, 0:512])
    P.dma("scalar", gind[:, 0:4096], arf(P, 0)[:, :])
    P.dma("scalar", gind[:, 4096:8192], arf(P, 1)[:, :])
    P.dma("scalar", gind[:, 8192:8193], cdec[:, 8:9], allow_slow_non_contiguous=True)
    P.collective("AllGather", ALU.bypass, [[0, 1, 2, 3], [4, 5, 6, 7]], gind, goutd)
    if "goutx" in P.ext_out:
        P.dma("sync", P.dram("goutx").ap(), goutd)


def b_l0_post(P):
    gname = "goutx" if "goutx" in P.ext_in else "gout"
    goutd = P.dram(gname).ap()
    cumd = P.dram("cums").ap()
    ead = P.dram("expacs").ap()
    cfmd = P.dram("cfm").ap()
    ytd = P.dram("yt").ap()
    ztd = P.dram("zt").ap()
    dxtd = P.dram("dxt").ap()
    gffnd = P.dram("gffn").ap()
    xhd = P.dram("xh").ap()
    yfmd = P.dram("yfm").ap()
    x1d = P.dram("x1").ap()
    idb = P.sb["c_idb"]
    idf = P.sb["c_idf"]
    ar = [P.sb[f"ar{j}"] for j in range(8)]
    ev = [P.sb[f"ip_ev{i}"] for i in range(4)]
    PS = [P.sb[f"PS{i}"] for i in range(7)]
    PT0 = P.sb["PT0"]
    mrel = P.sbuf("mrel", [128, 8], F32)
    dmr = P.sbuf("dmr", [128, 8], F32)
    suf = P.sbuf("suf", [128, 1], F32)
    rrow = P.sbuf("dt_rowt", [1, 128], F32)
    nwt = P.sbuf("nwt", [128, 64], F32)
    rms = P.sbuf("rms_c", [128, 4], F32)
    P.dma("sync", mrel[:, :], P.dram("relmask").ap())
    P.dma("sync", nwt[:, :], P.dram("nwt").ap())

    for r in range(4):
        P.dma("sync", dmr[:, r:r + 1], goutd[r * 128:(r + 1) * 128, 8192:8193],
              allow_slow_non_contiguous=True)
        P.op("vector", "tensor_scalar", dmr[:, r:r + 1], dmr[:, r:r + 1], mrel[:, r:r + 1], None,
             op0=ALU.mult)
        P.op("vector", "tensor_scalar", dmr[:, r:r + 1], dmr[:, r:r + 1], mrel[:, 4 + r:5 + r], None,
             op0=ALU.add)
    P.op("vector", "memset", suf[:, :], 1.0)
    for r in (3, 2, 1, 0):
        P.op("vector", "tensor_mul", dmr[:, 4 + r:5 + r], mrel[:, r:r + 1], suf[:, :])
        if r > 0:
            P.op("vector", "tensor_mul", suf[:, :], suf[:, :], dmr[:, r:r + 1])
    for r in range(4):
        P.op("tensor", "transpose", PS[4][0:1, 0:128], dmr[:, 4 + r:5 + r], idf[:, :])
        P.op("vector", "tensor_copy", rrow[:, :], PS[4][0:1, 0:128])
        P.op("tensor", "matmul", PS[4][:, 128:256], P.sb["c_onerf"][:, :], rrow[:, :], start=True, stop=True)
        P.op("vector", "tensor_copy", arf(P, 6)[:, r * 128:(r + 1) * 128], PS[4][:, 128:256])
    for r in range(4):
        P.dma("sync", arf(P, 2)[:, :], goutd[r * 128:(r + 1) * 128, 0:4096])
        P.dma("sync", arf(P, 3)[:, :], goutd[r * 128:(r + 1) * 128, 4096:8192])
        for half in range(2):
            cbv = _AP(arf(P, 6).tensor, r * 128 + half * 64, [[4096, 128], [1, 64], [0, 64]])
            src_ = arf(P, 2) if half == 0 else arf(P, 3)
            if r == 0:
                P.op("vector", "tensor_tensor",
                     arf(P, half)[:, :].rearrange("p (a b) -> p a b", a=64),
                     src_[:, :].rearrange("p (a b) -> p a b", a=64), cbv, op=ALU.mult)
            else:
                P.op("vector", "tensor_tensor",
                     arf(P, 4)[:, :].rearrange("p (a b) -> p a b", a=64),
                     src_[:, :].rearrange("p (a b) -> p a b", a=64), cbv, op=ALU.mult)
                P.op("vector", "tensor_add", arf(P, half)[:, :], arf(P, half)[:, :], arf(P, 4)[:, :])

    for tt in range(NT):
        c, lh = tt // 2, tt % 2
        if lh == 0:
            P.dma("sync", view(ar[5], 0, [[8192, 128], [256, 8], [1, 256]]),
                  view(P.dram("cfm"), c * 256, [[2048, 128], [262144, 8], [1, 256]]))
            P.dma("sync", rrow[:, :], cumd[c:c + 1, :])
            P.op("tensor", "matmul", PS[4][:, 128:256], P.sb["c_onerf"][:, :], rrow[:, :],
                 start=True, stop=True)
            P.op("vector", "tensor_copy", arf(P, 6)[:, 512:640], PS[4][:, 128:256])
            for half in range(2):
                cmv = _AP(arf(P, 6).tensor, 512 + half * 64, [[4096, 128], [1, 64], [0, 64]])
                P.op("vector", "tensor_tensor",
                     ar[2][:, half * 4096:(half + 1) * 4096].rearrange("p (a b) -> p a b", a=64),
                     arf(P, half)[:, :].rearrange("p (a b) -> p a b", a=64), cmv, op=ALU.mult)
        P.dma("sync", arf(P, 3)[:, :], ytd[tt * 128:(tt + 1) * 128, 0:4096])
        P.dma("sync", arf(P, 4)[:, :], ytd[tt * 128:(tt + 1) * 128, 4096:8192])
        P.dma("sync", P.sb["sc_a"][:, :], ztd[tt * 128:(tt + 1) * 128, 0:4096])
        P.dma("gpsimd", P.sb["sc_b"][:, :], dxtd[tt * 128:(tt + 1) * 128, 0:4096])
        P.dma("sync", arf(P, 7)[:, :], ztd[tt * 128:(tt + 1) * 128, 4096:8192])
        P.dma("sync", arf(P, 6)[:, 640:768], ead[tt * 128:(tt + 1) * 128, :])
        for hg in range(16):
            pp = PS[hg % 2]
            for hh in range(8):
                h = hg * 8 + hh
                g = h // 16
                P.op("tensor", "matmul", pp[:, hh * 64:(hh + 1) * 64],
                     view(ar[5], g * 256 + lh * 128, [[8192, 128], [1, 128]]),
                     view(ar[2], h * 64, [[8192, 128], [1, 64]]),
                     start=(hh == 0), stop=(hh == 7))
            eav = _AP(arf(P, 6).tensor, 640 + hg * 8, [[4096, 128], [1, 8], [0, 64]])
            P.op("vector", "tensor_tensor", ev[0][:, 0:512].rearrange("p (a b) -> p a b", a=8),
                 pp[:, 0:512].rearrange("p (a b) -> p a b", a=8), eav, op=ALU.mult)
            yh = arf(P, 3)[:, (hg % 8) * 512:(hg % 8 + 1) * 512] if hg < 8 else                  arf(P, 4)[:, (hg - 8) * 512:(hg - 8) * 512 + 512]
            P.op("vector", "tensor_add", yh, yh, ev[0][:, 0:512])
        for half in range(2):
            yh = arf(P, 3 + half)
            zt_h = P.sb["sc_a"][:, :] if half == 0 else arf(P, 7)[:, :]
            if half == 1:
                P.dma("gpsimd", P.sb["sc_b"][:, :],
                      dxtd[tt * 128:(tt + 1) * 128, 4096:8192])
            P.op("vector", "tensor_add", yh[:, :], yh[:, :], P.sb["sc_b"][:, :])
            P.op("scalar", "activation", zt_h, zt_h, AF.Silu)
            P.op("vector", "tensor_mul", yh[:, :], yh[:, :], zt_h)
            P.op("scalar", "activation", P.sb["sc_b"][:, :], yh[:, :], AF.Square,
                 accum_out=rms[:, half:half + 1])
        P.op("vector", "tensor_add", rms[:, 2:3], rms[:, 0:1], rms[:, 1:2])
        P.op("vector", "tensor_scalar", rms[:, 2:3], rms[:, 2:3], 1.0 / DIN, None, op0=ALU.mult)
        P.op("scalar", "activation", rms[:, 2:3], rms[:, 2:3], AF.Sqrt, bias=EPSC(P))
        P.op("vector", "reciprocal", rms[:, 2:3], rms[:, 2:3])
        for half in range(2):
            yh = arf(P, 3 + half)
            yrb = ar[5][:, 4096:8192]
            P.op("vector", "tensor_scalar", yrb, yh[:, :], rms[:, 2:3], None, op0=ALU.mult)
            for b8 in range(4):
                for j in range(8):
                    kc = half * 32 + b8 * 8 + j
                    P.op("tensor", "transpose", PT0[:, j * 128:(j + 1) * 128],
                         yrb[:, (b8 * 8 + j) * 128:(b8 * 8 + j + 1) * 128], idb[:, :])
                for j in range(8):
                    kc = half * 32 + b8 * 8 + j
                    P.op("vector", "tensor_scalar", P.sb["wb0"][:, j * 256:j * 256 + 128],
                         PT0[:, j * 128:(j + 1) * 128], nwt[:, kc:kc + 1], None, op0=ALU.mult)
                dst = view(yfmd, (tt * 64 + half * 32 + b8 * 8) * 128 * 128,
                           [[128, 128], [16384, 8], [1, 128]])
                P.dma("scalar", dst, view(P.sb["wb0"], 0, [[2048, 128], [256, 8], [1, 128]]))

    woutd = P.dram("woutt").ap()
    ysl = [P.sb["sc_a"][:, :].bitcast(BF16), P.sb["sc_b"][:, :].bitcast(BF16)]
    for nb in range(4):
        for kc in range(64):
            P.dma("gpsimd", view(ar[kc // 8], (kc % 8) * 1024, [[8192, 128], [1, 1024]]),
                  woutd[nb, kc])
        for tt in range(NT):
            sb = tt % 2
            P.dma("sync", view(ysl[sb], 0, [[8192, 128], [128, 64], [1, 128]]),
                  view(yfmd, (tt * 64) * 128 * 128, [[128, 128], [16384, 64], [1, 128]]))
            for half in range(2):
                pp = PS[half]
                for kc in range(64):
                    P.op("tensor", "matmul", pp[:, 0:512],
                         ysl[sb][:, kc * 128:(kc + 1) * 128],
                         view(ar[kc // 8], (kc % 8) * 1024 + half * 512, [[8192, 128], [1, 512]]),
                         start=(kc == 0), stop=(kc == 63))
            P.dma("sync", P.sb["sc_c"][:, 0:1024], gffnd[tt * 128:(tt + 1) * 128, nb * 1024:(nb + 1) * 1024])
            P.dma("sync", P.sb["sc_d"][:, 0:1024], xhd[3 + tt * 128:3 + (tt + 1) * 128,
                                                       nb * 1024:(nb + 1) * 1024])
            P.dma("sync", P.sb["sc_d"][:, 1024:2048],
                  P.dram("xnt").ap()[tt * 128:(tt + 1) * 128, nb * 1024:(nb + 1) * 1024])
            for half in range(2):
                P.op("vector", "scalar_tensor_tensor", ev[half][:, 0:512], PS[half][:, 0:512], 1.0,
                     P.sb["sc_d"][:, 1024 + half * 512:1024 + (half + 1) * 512],
                     op0=ALU.mult, op1=ALU.add)
                P.op("vector", "tensor_mul", ev[half][:, 0:512], ev[half][:, 0:512],
                     P.sb["sc_c"][:, half * 512:(half + 1) * 512])
                P.op("vector", "tensor_add", ev[2 + half][:, 0:512], ev[half][:, 0:512],
                     P.sb["sc_d"][:, half * 512:(half + 1) * 512])
                P.dma("scalar", x1d[tt * 128:(tt + 1) * 128,
                                    nb * 1024 + half * 512:nb * 1024 + (half + 1) * 512],
                      ev[2 + half][:, 0:512])


def b_l1_ln(P):
    x1d = P.dram("x1").ap()
    xt = P.sb["sc_a"]
    sqd = P.sb["sc_b"]
    xb = bfv(P, "sc_b")
    s_mu = P.sbuf("ln_mu", [128, 1], F32)
    s_nm = P.sbuf("ln_nm", [128, 1], F32)
    s_s2 = P.sbuf("ln_s2", [128, 1], F32)
    s_rs = P.sbuf("ln_rs", [128, 1], F32)
    ptr = P.psum("PT0", [128, 1024], BF16)
    for tt in range(NT):
        P.dma("sync", xt[0:128, :], x1d[tt * 128:(tt + 1) * 128, :])
        P.op("vector", "reduce_sum", s_mu[:, :], xt[:, :], axis=AX)
        P.op("vector", "tensor_scalar", s_mu[:, :], s_mu[:, :], 1.0 / D, None, op0=ALU.mult)
        P.op("vector", "tensor_scalar", s_nm[:, :], s_mu[:, :], -1.0, None, op0=ALU.mult)
        P.op("scalar", "activation", sqd[:, :], xt[:, :], AF.Square, bias=s_nm[:, :],
             accum_out=s_s2[:, :])
        P.op("vector", "tensor_scalar", s_s2[:, :], s_s2[:, :], 1.0 / D, None, op0=ALU.mult)
        P.op("scalar", "activation", s_rs[:, :], s_s2[:, :], AF.Sqrt, bias=EPSC(P))
        P.op("vector", "reciprocal", s_rs[:, :], s_rs[:, :])
        P.op("vector", "tensor_scalar", xb[:, 0:4096], xt[:, :], s_mu[:, :], s_rs[:, :],
             op0=ALU.subtract, op1=ALU.mult)
        for half in range(8):
            for j in range(4):
                kc = half * 4 + j
                P.op("tensor", "transpose", ptr[:, j * 128:(j + 1) * 128],
                     xb[:, kc * 128:(kc + 1) * 128], P.sb["c_idb"][:, :])
            dst = view(P.sb[f"ar{half}"], tt * 128, [[8192, 128], [2048, 4], [1, 128]])
            src_ = view(ptr, 0, [[1024, 128], [128, 4], [1, 128]])
            P.op("vector", "tensor_copy", dst, src_)


def b_l1_qkv(P):
    wqrd = P.dram("wqr").ap()
    wqed = P.dram("wqe").ap()
    qfd = P.dram("qf").ap()
    kvd = P.dram("kvloc").ap()
    qkd = P.dram("qkst").ap()
    wkd = P.dram("wkt").ap()
    wvd = P.dram("wvt").ap()
    wb = [P.sb["wb0"], P.sb["wb1"]]
    ev = [P.sb[f"ip_ev{i}"] for i in range(4)]
    PS = [P.sb[f"PS{i}"] for i in range(7)]
    kst = P.sbuf("l1_kst", [1, 16], F32)
    kstb = P.sbuf("l1_kstb", [1, 8], BF16)
    qsq = P.sb["ip_ev2"][:, :].bitcast(BF16)[:, 0:512]
    nrow = P.sb["ip_ev3"][0:1, 0:512]

    for kc in range(32):
        for g in range(4):
            P.dma("sync", P.sb["sc_a"][:, g * 1024:(g + 1) * 1024], wqrd[g, kc * 128:(kc + 1) * 128, :])
        P.op("vector", "tensor_add", P.sb["sc_b"][:, 0:1024], P.sb["sc_a"][:, 0:1024],
             P.sb["sc_a"][:, 1024:2048])
        P.op("vector", "tensor_add", P.sb["sc_b"][:, 1024:2048], P.sb["sc_a"][:, 2048:3072],
             P.sb["sc_a"][:, 3072:4096])
        P.op("vector", "tensor_add", P.sb["sc_b"][:, 0:1024], P.sb["sc_b"][:, 0:1024],
             P.sb["sc_b"][:, 1024:2048])
        P.op("vector", "tensor_scalar", bfv(P, "sc_b")[:, 4096:5120], P.sb["sc_b"][:, 0:1024],
             RSQD, None, op0=ALU.mult)
        P.dma("scalar", view(wqed, kc * 128 * 128, [[128, 128], [524288, 8], [1, 128]]),
              view(bfv(P, "sc_b"), 4096, [[8192, 128], [128, 8], [1, 128]]))

    P.op("vector", "memset", kst[:, :], 0.0)
    for oc in range(8):
        P.dma("gpsimd", wb[0][:, :].rearrange("p (k o) -> p k o", k=16),
              view(wqed, oc * 524288, [[128, 128], [16384, 16], [1, 128]]))
        P.dma("gpsimd", wb[1][:, :].rearrange("p (k o) -> p k o", k=16),
              view(wqed, oc * 524288 + 16 * 16384, [[128, 128], [16384, 16], [1, 128]]))
        for ts4 in range(4):
            pp = PS[ts4 % 2]
            for kc in range(32):
                P.op("tensor", "matmul", pp[:, 0:512],
                     wb[kc // 16][:, (kc % 16) * 128:(kc % 16 + 1) * 128],
                     xn_rhs(P, kc, ts4 * 512, 512), start=(kc == 0), stop=(kc == 31))
            P.op("vector", "tensor_copy", ev[ts4 % 2][:, 0:512].bitcast(BF16)[:, 0:512], pp[:, 0:512])
            P.dma("scalar", qfd[oc * 128:(oc + 1) * 128, ts4 * 512:(ts4 + 1) * 512],
                  ev[ts4 % 2][:, 0:512].bitcast(BF16)[:, 0:512])
            P.op("scalar", "activation", qsq, pp[:, 0:512], AF.Square)
            P.op("tensor", "matmul", PS[4][0:1, 0:512], P.sb["c_onec"][:, :], qsq,
                 start=True, stop=True)
            P.op("vector", "tensor_copy", nrow, PS[4][0:1, 0:512])
            P.op("vector", "reduce_max", kst[0:1, 8 + oc:9 + oc], nrow, axis=AX)
            P.op("vector", "tensor_max", kst[0:1, oc:oc + 1], kst[0:1, oc:oc + 1],
                 kst[0:1, 8 + oc:9 + oc])
    P.dma("scalar", qkd[0:1, 0:8], kst[0:1, 0:8])

    P.op("vector", "memset", kst[:, :], 0.0)
    for oc in range(8):
        P.dma("gpsimd", wb[0][:, :], wkd[oc, :, 0:16, :])
        P.dma("gpsimd", wb[1][:, :], wkd[oc, :, 16:32, :])
        for ts4 in range(4):
            pp = PS[ts4 % 2]
            for kc in range(32):
                P.op("tensor", "matmul", pp[:, 0:512],
                     wb[kc // 16][:, (kc % 16) * 128:(kc % 16 + 1) * 128],
                     xn_rhs(P, kc, ts4 * 512, 512), start=(kc == 0), stop=(kc == 31))
            P.op("vector", "tensor_copy", ev[ts4 % 2][:, 0:512].bitcast(BF16)[:, 0:512], pp[:, 0:512])
            P.dma("scalar", view(kvd, 2048 * 1024 + oc * 128 * 2048 + ts4 * 512,
                                 [[2048, 128], [1, 512]]),
                  ev[ts4 % 2][:, 0:512].bitcast(BF16)[:, 0:512])
            P.op("scalar", "activation", qsq, pp[:, 0:512], AF.Square)
            P.op("tensor", "matmul", PS[4][0:1, 0:512], P.sb["c_onec"][:, :], qsq,
                 start=True, stop=True)
            P.op("vector", "tensor_copy", nrow, PS[4][0:1, 0:512])
            P.op("vector", "reduce_max", kst[0:1, 8 + oc:9 + oc], nrow, axis=AX)
            P.op("vector", "tensor_max", kst[0:1, oc:oc + 1], kst[0:1, oc:oc + 1],
                 kst[0:1, 8 + oc:9 + oc])
    P.op("vector", "tensor_scalar", kst[0:1, 0:8], kst[0:1, 0:8], 1.02, None, op0=ALU.mult)
    P.op("vector", "tensor_copy", kstb[:, :], kst[0:1, 0:8])
    P.dma("scalar", view(kvd, 4096 * 1024, [[1024, 1], [1, 8]]), kstb[:, :])

    for oc in range(8):
        P.dma("gpsimd", wb[0][:, :], wvd[oc, :, 0:16, :])
        P.dma("gpsimd", wb[1][:, :], wvd[oc, :, 16:32, :])
        for tt in range(NT):
            pp = PS[tt % 2]
            for kc in range(32):
                P.op("tensor", "matmul", pp[:, 0:128], xn_rhs(P, kc, tt * 128, 128),
                     wb[kc // 16][:, (kc % 16) * 128:(kc % 16 + 1) * 128],
                     start=(kc == 0), stop=(kc == 31))
            P.op("vector", "tensor_copy", ev[tt % 2][:, 0:128].bitcast(BF16)[:, 0:128], pp[:, 0:128])
            P.dma("scalar", view(kvd, tt * 128 * 1024 + oc * 128, [[1024, 128], [1, 128]]),
                  ev[tt % 2][:, 0:128].bitcast(BF16)[:, 0:128])
    P.collective("AllGather", ALU.bypass, [[0, 1, 2, 3], [4, 5, 6, 7]],
                 P.dram("kvloc").ap(), P.dram("kvg").ap())
    if "kvgx" in P.ext_out:
        P.dma("sync", P.dram("kvgx").ap(), P.dram("kvg").ap())


def b_l1_ffn(P):
    g1d = P.dram("g1").ap()
    wla = P.sb["sc_a"][:, :].bitcast(BF16)
    wr = P.sb["sc_b"][:, :].bitcast(BF16)
    ws = [P.sb["sc_c"][:, :].bitcast(BF16), P.sb["sc_d"][:, :].bitcast(BF16)]
    m1 = P.sb["f_m1"] if "f_m1" in P.sb else P.sbuf("f_m1", [128, 1], F32)
    m2 = P.sb["f_m2"] if "f_m2" in P.sb else P.sbuf("f_m2", [128, 1], F32)
    zt_ = P.sbuf("s_z", [128, 8], F32)
    ke = P.sbuf("s_ke", [128, 8], F32)
    se = P.sbuf("s_se", [128, 8], F32)
    msk = P.sbuf("s_msk", [128, 8], F32)
    tau = P.sbuf("s_tau", [128, 1], F32)
    brow = P.sbuf("dt_rowt", [1, 128], F32)[0:1, :].bitcast(BF16)[0:1, 0:128]
    rrow2 = P.sbuf("s_rrow", [1, 8], BF16)
    PS = [P.sb[f"PS{i}"] for i in range(7)]
    ptr = P.sb["PT0"]
    hf = P.sb["wb0"]
    ev = [P.sb[f"ip_ev{i}"] for i in range(4)]
    hp = ev[2][:, 0:128]
    hb = ev[3][:, :].bitcast(BF16)[:, 0:128]

    P.dma("gpsimd", view(wla, 0, [[8192, 128], [128, 32], [1, 128]]),
          view(P.dram("wla1"), 0, [[128, 128], [128 * 128, 32], [1, 128]]))
    P.dma("gpsimd", view(wr, 0, [[8192, 128], [8, 32], [1, 8]]),
          view(P.dram("wr1"), 0, [[8, 128], [128 * 8, 32], [1, 8]]))

    P.dma("gpsimd", rrow2[:, :], P.dram("br1").ap())
    rb8 = P.sbuf("f_rb8", [128, 8], F32)
    bbt = P.sbuf("d_bc", [128, 128], F32)

    for tt in range(NT):
        if tt == 0:
            P.op("tensor", "matmul", PS[4][:, 16:24], P.sb["c_oner"][:, :], rrow2[:, :],
                 start=True, stop=True)
            P.op("vector", "tensor_copy", rb8[:, :], PS[4][:, 16:24])
        for kc in range(32):
            P.op("tensor", "matmul", PS[4][:, 0:8], xn_rhs(P, kc, tt * 128, 128),
                 wr[:, kc * 8:(kc + 1) * 8], start=(kc == 0), stop=(kc == 31))
        for kc in range(32):
            P.op("tensor", "matmul", PS[4][:, 32:160], xn_rhs(P, kc, tt * 128, 128),
                 wla[:, kc * 128:(kc + 1) * 128], start=False, stop=(kc == 31))
        P.op("vector", "tensor_add", zt_[:, :], PS[4][:, 0:8], rb8[:, :])
        P.op("vector", "tensor_copy", hp, PS[4][:, 32:160])
        for e in range(8):
            P.op("vector", "tensor_scalar", msk[:, :], zt_[:, :], zt_[:, e:e + 1], None, op0=ALU.is_ge)
            P.op("vector", "reduce_sum", ke[:, e:e + 1], msk[:, :], axis=AX)
            P.op("vector", "tensor_mul", msk[:, :], msk[:, :], zt_[:, :])
            P.op("vector", "reduce_sum", se[:, e:e + 1], msk[:, :], axis=AX)
        P.op("vector", "tensor_mul", ke[:, :], ke[:, :], zt_[:, :])
        P.op("vector", "scalar_tensor_tensor", msk[:, :], ke[:, :], 1.0, se[:, :],
             op0=ALU.add, op1=ALU.is_gt)
        P.op("vector", "reduce_sum", m1[:, :], msk[:, :], axis=AX)
        P.op("vector", "tensor_mul", se[:, :], msk[:, :], zt_[:, :])
        P.op("vector", "reduce_sum", m2[:, :], se[:, :], axis=AX)
        P.op("vector", "reciprocal", m1[:, :], m1[:, :])
        P.op("vector", "tensor_scalar", tau[:, :], m2[:, :], -1.0, m1[:, :], op0=ALU.add, op1=ALU.mult)
        P.op("vector", "tensor_scalar", zt_[:, :], zt_[:, :], tau[:, :], 0.0, op0=ALU.subtract, op1=ALU.max)
        wv_ = _AP(zt_[:, :].tensor, zt_[:, :].offset, [list(zt_[:, :].ap[0]), [1, 8], [0, 16]])
        P.op("vector", "tensor_tensor", hb.rearrange("p (e r) -> p e r", e=8),
             hp.rearrange("p (e r) -> p e r", e=8), wv_, op=ALU.mult)
        P.op("tensor", "transpose", ptr[:, 0:128], hb, P.sb["c_idb"][:, :])
        P.op("vector", "tensor_copy", hf[:, tt * 128:(tt + 1) * 128], ptr[:, 0:128])

    P.dma("gpsimd", view(wr, 1024, [[8192, 128], [1, 4096]]), P.dram("wlb1").ap())
    wbt = P.dram("wb1t").ap()
    for og in range(32):
        sb = og % 2
        P.dma("gpsimd", ws[sb][:, 0:4096], wbt[og])
        P.dma("gpsimd", brow, P.dram("bb1").ap()[0:1, og * 128:(og + 1) * 128])
        P.op("tensor", "matmul", PS[4][:, 256:384], P.sb["c_oner"][:, :], brow,
             start=True, stop=True)
        P.op("vector", "tensor_copy", bbt[:, :], PS[4][:, 256:384])
        for tt in range(NT):
            pp = PS[tt % 4]
            for kc in range(32):
                P.op("tensor", "matmul", pp[:, 0:128], xn_rhs(P, kc, tt * 128, 128),
                     ws[sb][:, kc * 128:(kc + 1) * 128], start=(kc == 0), stop=False)
            P.op("tensor", "matmul", pp[:, 0:128], hf[:, tt * 128:(tt + 1) * 128],
                 wr[:, 1024 + og * 128:1024 + (og + 1) * 128], start=False, stop=True)
            gev = ev[tt % 4]
            P.op("vector", "scalar_tensor_tensor", gev[:, 0:128], pp[:, 0:128], 1.0, bbt[:, :],
                 op0=ALU.mult, op1=ALU.add)
            P.op("scalar", "activation", gev[:, 0:128], gev[:, 0:128], AF.Silu)
            P.dma("scalar", g1d[tt * 128:(tt + 1) * 128, og * 128:(og + 1) * 128], gev[:, 0:128])


def b_l1_attn(P):
    gname = "kvgx" if "kvgx" in P.ext_in else "kvg"
    kvgd = P.dram(gname).ap()
    qfd = P.dram("qf").ap()
    qkd = P.dram("qkst").ap()
    attnd = P.dram("attn").ap()
    ar = [P.sb[f"ar{j}"] for j in range(8)]
    ev = [P.sb[f"ip_ev{i}"] for i in range(4)]
    PS = [P.sb[f"PS{i}"] for i in range(7)] if "PS5" in P.sb else [P.sb[f"PS{i}"] for i in range(5)]
    sa = P.sb["sc_a"]
    sab = sa[:, :].bitcast(BF16)
    sbb = P.sb["sc_b"][:, :].bitcast(BF16)
    st8 = sa[0:1, 0:48]
    nb8 = sa[:, 64:72]
    rc = sa[:, 72:73]
    rrow = P.sb["dt_rowt"] if "dt_rowt" in P.sb else P.sbuf("dt_rowt", [1, 128], F32)
    P.dma("sync", st8[0:1, 0:8], qkd[0:1, 0:8])
    for r in range(4):
        P.dma("gpsimd", st8[0:1, 16 + r * 8:24 + r * 8],
              view(kvgd, (r * 4097 + 4096) * 1024, [[8, 1], [1, 8]]))
    P.op("vector", "tensor_copy", st8[0:1, 8:16], st8[0:1, 16:24])
    for r in range(1, 4):
        P.op("vector", "tensor_max", st8[0:1, 8:16], st8[0:1, 8:16], st8[0:1, 16 + r * 8:24 + r * 8])
    P.op("vector", "tensor_mul", st8[0:1, 8:16], st8[0:1, 8:16], st8[0:1, 0:8])
    P.op("scalar", "activation", st8[0:1, 8:16], st8[0:1, 8:16], AF.Sqrt)
    P.op("vector", "tensor_scalar", st8[0:1, 8:16], st8[0:1, 8:16], -1.0, None, op0=ALU.mult)
    for h in range(8):
        P.op("tensor", "matmul", PS[4][:, 0:1], P.sb["c_onerf"][:, :], st8[0:1, 8 + h:9 + h],
             start=True, stop=True)
        P.op("vector", "tensor_copy", nb8[:, h:h + 1], PS[4][:, 0:1])
    for h in range(8):
        P.dma("sync", view(ar[h // 4], (h % 4) * 2048, [[8192, 128], [1, 2048]]),
              qfd[h * 128:(h + 1) * 128, :])
    for j in (3, 4, 6, 7):
        P.op("vector", "memset", view(ar[j], 128, [[8192, 128], [132, 32], [1, 1]]), 1.0)
    SCB = [0, 1, 5]
    et = [sbb[:, 0:512], sbb[:, 512:1024], sbb[:, 1024:1536]]

    def kv_load(h):
        hb_ = h % 2
        kb = ar[2] if hb_ == 0 else ar[5]
        for r in range(4):
            P.dma("sync", view(kb, r * 2048, [[8192, 128], [1, 2048]]),
                  view(kvgd, (r * 4097 + 2048) * 1024 + h * 128 * 2048, [[2048, 128], [1, 2048]]))
            vb = ar[3 + r // 2] if hb_ == 0 else ar[6 + r // 2]
            P.dma("sync", view(vb, (r % 2) * 16 * 132, [[8192, 128], [132, 16], [1, 128]]),
                  view(kvgd, (r * 4097) * 1024 + h * 128, [[1024, 128], [131072, 16], [1, 128]]))

    def vtile(h, kc):
        hb_ = h % 2
        vb = ar[3 + kc // 32] if hb_ == 0 else ar[6 + kc // 32]
        return view(vb, (kc % 32) * 132, [[8192, 128], [1, 129]])

    kv_load(0)
    kv_load(1)
    for h in range(8):
        hb_ = h % 2
        kb = ar[2] if hb_ == 0 else ar[5]
        for qt in range(4):
            qv = view(ar[h // 4], (h % 4) * 2048 + qt * 512, [[8192, 128], [1, 512]])

            def sc(kc):
                b = kc % 3
                P.op("tensor", "matmul", PS[SCB[b]][:, 0:512],
                     view(kb, kc * 128, [[8192, 128], [1, 128]]), qv, start=True, stop=True)
                P.op("scalar", "activation", et[b], PS[SCB[b]][:, 0:512], AF.Exp, bias=nb8[:, h:h + 1])

            sc(0)
            sc(1)
            for kc in range(64):
                if kc + 2 < 64:
                    sc(kc + 2)
                b = kc % 3
                for qi in range(4):
                    P.op("tensor", "matmul", PS[2 + (qi % 2)][:, (qi // 2) * 256:(qi // 2) * 256 + 129],
                         et[b][:, qi * 128:(qi + 1) * 128], vtile(h, kc),
                         start=(kc == 0 and qi < 2), stop=(kc == 63))
            for qi in range(4):
                pp = PS[2 + (qi % 2)]
                c0 = (qi // 2) * 256
                P.op("vector", "reciprocal", rc, pp[:, c0 + 128:c0 + 129])
                P.op("vector", "tensor_scalar", ev[qi % 4][:, 0:128], pp[:, c0:c0 + 128], rc,
                     None, op0=ALU.mult)
                P.dma("scalar", attnd[qt * 512 + qi * 128:qt * 512 + (qi + 1) * 128,
                                      h * 128:(h + 1) * 128], ev[qi % 4][:, 0:128])
        if h + 2 < 8:
            kv_load(h + 2)



def b_l1_wo(P):
    attnd = P.dram("attn").ap()
    g1d = P.dram("g1").ap()
    x1d = P.dram("x1").ap()
    outd = P.dram("out").ap()
    wod = P.dram("wo").ap()
    ar = [P.sb[f"ar{j}"] for j in range(8)]
    ev = [P.sb[f"ip_ev{i}"] for i in range(4)]
    PS = [P.sb[f"PS{i}"] for i in range(7)]
    ptr = P.sb["PT0"]
    for kc in range(8):
        P.dma("gpsimd", view(ar[4 + kc // 2], (kc % 2) * 4096, [[8192, 128], [1, 4096]]),
              wod[kc * 128:(kc + 1) * 128, :])
    for tt in range(NT):
        P.dma("sync", P.sb["sc_a"][:, 0:1024], attnd[tt * 128:(tt + 1) * 128, :])
        ab = bfv(P, "sc_b")
        P.op("vector", "tensor_copy", ab[:, 0:1024], P.sb["sc_a"][:, 0:1024])
        for kc in range(8):
            P.op("tensor", "transpose", ptr[:, (kc % 8) * 128:(kc % 8 + 1) * 128],
                 ab[:, kc * 128:(kc + 1) * 128], P.sb["c_idb"][:, :])
        P.op("vector", "tensor_copy", ab[:, 1024:2048], ptr[:, 0:1024])
        for og in range(8):
            pp = PS[og % 2]
            for kc in range(8):
                P.op("tensor", "matmul", pp[:, 0:512], ab[:, 1024 + kc * 128:1024 + (kc + 1) * 128],
                     view(ar[4 + kc // 2], (kc % 2) * 4096 + og * 512, [[8192, 128], [1, 512]]),
                     start=(kc == 0), stop=(kc == 7))
            P.dma("sync", P.sb["sc_c"][:, 0:512], g1d[tt * 128:(tt + 1) * 128, og * 512:(og + 1) * 512])
            P.dma("sync", P.sb["sc_c"][:, 512:1024], x1d[tt * 128:(tt + 1) * 128, og * 512:(og + 1) * 512])
            P.op("vector", "tensor_mul", ev[og % 2][:, 0:512], pp[:, 0:512], P.sb["sc_c"][:, 0:512])
            P.op("vector", "tensor_add", ev[2 + og % 2][:, 0:512], ev[og % 2][:, 0:512],
                 P.sb["sc_c"][:, 512:1024])
            P.dma("scalar", outd[tt * 128:(tt + 1) * 128, og * 512:(og + 1) * 512],
                  ev[2 + og % 2][:, 0:512])


L0_BUILDERS = None
L1_BUILDERS = None
L0_IN = ["xh", "relmask", "c_ident", "c_ones", "c_misc", "c_trineg", "wip", "convw", "convb",
         "dtbias", "alog", "dvec", "wg0", "wla0", "wlb0a", "wlb0b", "wb0t", "woutt", "nwt"]
L1_IN = ["c_ident", "c_ones", "c_misc", "wqr", "wkt", "wvt", "wo", "wb1t", "bb1",
         "wr1", "br1", "wla1", "wlb1"]
FUSED = True


def kernel(**inputs):
    l0b = [b_l0_ln, b_l0_inproj, b_l0_dt, b_l0_conv, b_l0_ffn, b_l0_ssd, b_l0_post]
    l1b = [b_l1_ln, b_l1_qkv, b_l1_ffn, b_l1_attn, b_l1_wo]
    W = host_weights(inputs)
    pc = [host_percore(inputs, c) for c in range(8)]
    out = np.zeros((2, SEQL if False else 8192, D), dtype=np.float32)
    if FUSED:
        nc = build_launch(l0b + l1b, ext_in=L0_IN + L1_IN, ext_out=["out"])
        maps = [{("t_" + n): (W[n] if n in W else pc[c][n]) for n in L0_IN + L1_IN}
                for c in range(8)]
        res = run_bass_kernel_spmd(nc, maps, core_ids=list(range(8)))
        for c in range(8):
            out[c // 4, (c % 4) * TPC:(c % 4 + 1) * TPC] = res.results[c]["t_out"]
        return out
    nc0 = build_launch(l0b, ext_in=L0_IN, ext_out=["x1"])
    maps0 = [{("t_" + n): (W[n] if n in W else pc[c][n]) for n in L0_IN} for c in range(8)]
    res0 = run_bass_kernel_spmd(nc0, maps0, core_ids=list(range(8)))
    nc1 = build_launch(l1b, ext_in=["x1"] + L1_IN, ext_out=["out"])
    maps1 = []
    for c in range(8):
        m = {"t_x1": np.ascontiguousarray(res0.results[c]["t_x1"])}
        for n in L1_IN:
            m["t_" + n] = W[n]
        maps1.append(m)
    res1 = run_bass_kernel_spmd(nc1, maps1, core_ids=list(range(8)))
    for c in range(8):
        out[c // 4, (c % 4) * TPC:(c % 4 + 1) * TPC] = res1.results[c]["t_out"]
    return out


# version 13
# speedup vs baseline: 1.4102x; 1.0221x over previous
import contextlib
import numpy as np
import ml_dtypes
import concourse.bass as bass
import concourse.mybir as mybir
from concourse.bass_types import AP as _AP
from concourse.bass_utils import run_bass_kernel_spmd

F32 = mybir.dt.float32
BF16 = mybir.dt.bfloat16
AF = mybir.ActivationFunctionType
ALU = mybir.AluOpType
AX = mybir.AxisListType.X
bfdt = ml_dtypes.bfloat16

D = 4096
DIN = 8192
H = 128
PP = 64
GG = 8
CH = 256
NCH = 8
TPC = 2048
NT = 16
DIP = 18560
NOC = DIP // 128
CONVD = 10240
EPS = 1e-5
RSQD = float(1.0 / np.sqrt(128.0))

ENGS = ("sync", "scalar", "vector", "tensor", "gpsimd")
CSEM = {"sync": "c_sy", "scalar": "ac", "vector": "dv", "tensor": "pe", "gpsimd": "gp"}
QSEM = {"sync": "d_sy", "scalar": "d_sc", "gpsimd": "d_gp"}
NSLOT = 16

SCRATCH = {
    "zt":     ([TPC, DIN], F32),
    "xnt":    ([TPC, D], F32),
    "xbc":    ([CONVD, 2051], F32),
    "xdt":    ([TPC, DIN], BF16),
    "dxt":    ([TPC, DIN], BF16),
    "bfm":    ([GG * 128, TPC], BF16),
    "cfm":    ([GG * 128, TPC], BF16),
    "yt":     ([TPC, DIN], F32),
    "expacs": ([TPC, H], F32),
    "cums":   ([NCH, H], F32),
    "gin":    ([128, DIN + 1], F32),
    "gout":   ([512, DIN + 1], F32),
    "goutx":  ([512, DIN + 1], F32),
    "gffn":   ([TPC, D], F32),
    "yfm":    ([NT * 64 * 128, 128], BF16),
    "x1":     ([TPC, D], F32),
    "wqe":    ([8, D, 128], BF16),
    "qf":     ([1024, TPC], BF16),
    "qkst":   ([1, 16], F32),
    "kvloc":  ([4097, 1024], BF16),
    "kvg":    ([4 * 4097, 1024], BF16),
    "kvgx":   ([4 * 4097, 1024], BF16),
    "g1":     ([TPC, D], F32),
    "attn":   ([TPC, 1024], F32),
    "out":    ([TPC, D], F32),
}

WSPEC = {
    "xh":      ([2051, D], F32),
    "wip":     ([NOC, 128, 32, 128], F32),
    "convw":   ([80, 128, 4], F32),
    "convb":   ([80, 128], F32),
    "dtbias":  ([H, 1], F32),
    "alog":    ([H, 1], F32),
    "dvec":    ([H, 1], F32),
    "normw":   ([1, DIN], F32),
    "wout":    ([DIN, D], F32),
    "woutt":   ([4, 64, 128, 1024], F32),
    "nwt":     ([128, 64], F32),
    "wb0":     ([D, D], F32),
    "wb0t":    ([32, 128, 32, 128], F32),
    "wg0":     ([D, 28], F32),
    "wla0":    ([D, 224], F32),
    "wlb0a":   ([128, D], F32),
    "wlb0b":   ([96, D], F32),
    "wqr":     ([4, D, 1024], F32),
    "wk":      ([D, 1024], F32),
    "wkt":     ([8, 128, 32, 128], F32),
    "wv":      ([D, 1024], F32),
    "wvt":     ([8, 128, 32, 128], F32),
    "wo":      ([1024, D], F32),
    "wb1":     ([D, D], F32),
    "wb1t":    ([32, 128, 32, 128], F32),
    "bb1":     ([1, D], F32),
    "wr1":     ([D, 8], F32),
    "br1":     ([1, 8], F32),
    "wla1":    ([D, 128], F32),
    "wlb1":    ([128, D], F32),
    "c_ident": ([128, 128], F32),
    "c_ones":  ([128, 128], F32),
    "c_mask":  ([2, 128, CH], F32),
    "c_trineg": ([128, 128], F32),
    "c_misc":  ([128, 4], F32),
    "relmask": ([128, 8], F32),
}


def ap_of(t):
    if isinstance(t, _AP):
        return t
    return t.ap() if callable(getattr(t, "ap", None)) else t[:, :]


def view(t, offset, dims):
    a = ap_of(t)
    return _AP(a.tensor, offset, [list(d) for d in dims])


class Op:
    __slots__ = ("eng", "fn", "deps", "sem", "need", "inc_k", "val", "seq")

    def __init__(self, eng, fn, sem, inc_k, seq):
        self.eng, self.fn, self.sem, self.inc_k, self.seq = eng, fn, sem, inc_k, seq
        self.deps = []
        self.need = False
        self.val = None


class Prog:
    def __init__(self):
        self.nc = bass.Bass("TRN2", target_bir_lowering=False, debug=False)
        self.ops = {e: [] for e in ENGS}
        self.sb = {}
        self.dramd = {}
        self._stack = None
        self.ext_in = set()
        self.ext_out = set()
        self.last_writer = {}
        self.readers = {}
        self._seq = 0
        self._slot = {q: 0 for q in QSEM}
        self._slot_last = {q: [None] * NSLOT for q in QSEM}

    def sbuf(self, name, shape, dtype):
        if name not in self.sb:
            self.sb[name] = self._stack.enter_context(self.nc.sbuf_tensor(name, shape, dtype))
        return self.sb[name]

    def psum(self, name, shape, dtype=F32):
        if name not in self.sb:
            self.sb[name] = self._stack.enter_context(self.nc.psum_tensor(name, shape, dtype))
        return self.sb[name]

    def dram(self, name):
        if name not in self.dramd:
            shape, dtype = SCRATCH[name] if name in SCRATCH else WSPEC[name]
            kind = ("ExternalInput" if name in self.ext_in else
                    "ExternalOutput" if name in self.ext_out else "Internal")
            self.dramd[name] = self.nc.dram_tensor("t_" + name, list(shape), dtype, kind=kind)
        return self.dramd[name]

    def _plan(self, eng, fn, reads, writes, sem, inc_k, extra_deps=()):
        self._seq += 1
        o = Op(eng, fn, sem, inc_k, self._seq)
        deps = {}

        def add(d):
            if d is None:
                return
            if eng == "tensor" and d.eng == "tensor":
                return
            cur = deps.get(d.sem)
            if cur is None or cur.seq < d.seq:
                deps[d.sem] = d

        rn = [ap.tensor.name for ap in reads]
        wn = [ap.tensor.name for ap in writes]
        for t in rn:
            add(self.last_writer.get(t))
        for t in wn:
            add(self.last_writer.get(t))
            for d in self.readers.get(t, {}).values():
                add(d)
        for d in extra_deps:
            add(d)
        for d in deps.values():
            d.need = True
        o.deps = list(deps.values())
        self.ops[eng].append(o)
        for t in wn:
            self.last_writer[t] = o
            self.readers[t] = {}
        for t in rn:
            self.readers.setdefault(t, {})[eng] = o
        return o

    def op(self, eng, meth, *args, extra_deps=(), **kw):
        writes, reads = [], []
        seen_w = False
        for a in args:
            if isinstance(a, _AP):
                if not seen_w:
                    writes.append(a)
                    seen_w = True
                else:
                    reads.append(a)
        for k, v in kw.items():
            if isinstance(v, _AP):
                (writes if k in ("accum_out", "out") else reads).append(v)

        def fn(e, _m=meth, _a=args, _k=kw):
            return getattr(e, _m)(*_a, **_k)
        return self._plan(eng, fn, reads, writes, CSEM[eng], 1, extra_deps)

    def dma(self, q, out, in_, extra_deps=(), **kw):
        def fn(e, _o=out, _i=in_, _k=kw):
            return e.dma_start(out=_o, in_=_i, **_k)
        slot = self._slot[q] % NSLOT
        self._slot[q] += 1
        prev = self._slot_last[q][slot]
        deps = list(extra_deps) + ([prev] if prev is not None else [])
        o = self._plan(q, fn, [in_], [out], QSEM[q] + str(slot), 16, deps)
        o.need = True
        self._slot_last[q][slot] = o
        return o

    def collective(self, kind, alu, groups, in_ap, out_ap, extra_deps=()):
        def fn(e, _i=in_ap, _o=out_ap):
            return e.collective_compute(kind, alu, replica_groups=groups, ins=[_i.opt()], outs=[_o.opt()])
        return self._plan("gpsimd", fn, [in_ap], [out_ap], "gp", 1, extra_deps)

    def lower(self):
        semnames = sorted(set(CSEM.values()) |
                          {q + str(i) for q in QSEM.values() for i in range(NSLOT)})
        cnt = {s: 0 for s in semnames}
        for eng in ENGS:
            for o in self.ops[eng]:
                if o.need:
                    cnt[o.sem] += o.inc_k
                    o.val = cnt[o.sem]
        bar = [(s, cnt[s]) for s in semnames if cnt[s] > 0]
        sems = {n: self._stack.enter_context(self.nc.semaphore(n)) for n in semnames}
        block = self._stack.enter_context(self.nc.Block())
        for eng in ENGS:
            ops = self.ops[eng]
            if eng == "sync":
                pass
            if not ops and eng != "sync":
                continue

            def mk(ops, fin):
                def f(e):
                    last = {}
                    for o in ops:
                        for d in o.deps:
                            if last.get(d.sem, 0) < d.val:
                                e.wait_ge(sems[d.sem], d.val)
                                last[d.sem] = d.val
                        instr = o.fn(e)
                        if o.need:
                            instr.then_inc(sems[o.sem], o.inc_k)
                    for s, v in fin:
                        if last.get(s, 0) < v:
                            e.wait_ge(sems[s], v)
                return f

            getattr(block, eng)(mk(ops, bar if eng == "sync" else []))


def build_launch(builders, ext_in, ext_out, dbg=()):
    P = Prog()
    P.ext_in = set(ext_in)
    P.ext_out = set(ext_out)
    with contextlib.ExitStack() as st:
        P._stack = st
        consts(P)
        for b in builders:
            b(P)
        for name, get_ap in dbg:
            ap = get_ap(P)
            t = P.nc.dram_tensor("t_dbg_" + name, list(ap.shape), ap.dtype, kind="ExternalOutput")
            P.dramd["dbg_" + name] = t
            P.dma("scalar", t.ap(), ap)
        P.lower()
    return P.nc


def consts(P):
    P.sbuf("c_idb", [128, 128], BF16)
    P.sbuf("c_idf", [128, 128], F32)
    P.sbuf("c_onec", [128, 1], BF16)
    P.sbuf("c_oner", [1, 128], BF16)
    P.sbuf("c_onerf", [1, 128], F32)
    P.sbuf("c_miscs", [128, 4], F32)
    for j in range(8):
        P.sbuf(f"ar{j}", [128, 8192], BF16)
    P.sbuf("sc_a", [128, 4096], F32)
    P.sbuf("sc_b", [128, 4096], F32)
    P.sbuf("sc_c", [128, 2051], F32)
    P.sbuf("sc_d", [128, 2051], F32)
    P.sbuf("wb0", [128, 2048], BF16)
    P.sbuf("wb1", [128, 2048], BF16)
    for i in range(4):
        P.sbuf(f"ip_ev{i}", [128, 512], F32)
    for i in range(7):
        P.psum(f"PS{i}", [128, 512], F32)
    P.psum("PT0", [128, 1024], BF16)
    ci = P.dram("c_ident").ap()
    co = P.dram("c_ones").ap()
    P.dma("sync", P.sb["c_idf"][:, :], ci)
    P.dma("gpsimd", P.sb["c_idb"][:, :], ci)
    P.dma("gpsimd", P.sb["c_onec"][:, :], co[:, 0:1], allow_slow_non_contiguous=True)
    P.dma("gpsimd", P.sb["c_oner"][:, :], co[0:1, :])
    P.dma("sync", P.sb["c_onerf"][:, :], co[0:1, :])
    P.dma("sync", P.sb["c_miscs"][:, :], P.dram("c_misc").ap())


def EPSC(P, nr=128):
    return P.sb["c_miscs"][0:nr, 0:1]


def ONEC(P, nr=128):
    return P.sb["c_miscs"][0:nr, 1:2]


def bfv(P, name):
    return P.sb[name][:, :].bitcast(BF16)


def xn_rhs(P, kc, c0, n):
    ar = P.sb[f"ar{kc // 4}"]
    return view(ar, (kc % 4) * 2048 + c0, [[8192, 128], [1, n]])


def b_l0_ln(P):
    xnh = P.sbuf("xnh", [128, 128], BF16)
    xh = P.dram("xh").ap()
    xt = P.sb["sc_a"]
    sqd = P.sb["sc_b"]
    xb = bfv(P, "sc_b")
    s_mu = P.sbuf("ln_mu", [128, 1], F32)
    s_nm = P.sbuf("ln_nm", [128, 1], F32)
    s_s2 = P.sbuf("ln_s2", [128, 1], F32)
    s_rs = P.sbuf("ln_rs", [128, 1], F32)
    ptr = P.psum("PT0", [128, 1024], BF16)

    for tt in range(-1, NT):
        r0, nr = (0, 3) if tt < 0 else (3 + tt * 128, 128)
        P.dma("sync", xt[0:nr, :], xh[r0:r0 + nr, :])
        P.op("vector", "reduce_sum", s_mu[0:nr, :], xt[0:nr, :], axis=AX)
        P.op("vector", "tensor_scalar", s_mu[0:nr, :], s_mu[0:nr, :], 1.0 / D, None, op0=ALU.mult)
        P.op("vector", "tensor_scalar", s_nm[0:nr, :], s_mu[0:nr, :], -1.0, None, op0=ALU.mult)
        P.op("scalar", "activation", sqd[0:nr, :], xt[0:nr, :], AF.Square, bias=s_nm[0:nr, :],
             accum_out=s_s2[0:nr, :])
        P.op("vector", "tensor_scalar", s_s2[0:nr, :], s_s2[0:nr, :], 1.0 / D, None, op0=ALU.mult)
        P.op("scalar", "activation", s_rs[0:nr, :], s_s2[0:nr, :], AF.Sqrt, bias=EPSC(P, nr))
        P.op("vector", "reciprocal", s_rs[0:nr, :], s_rs[0:nr, :])
        P.op("vector", "tensor_scalar", xb[0:nr, 0:4096], xt[0:nr, :], s_mu[0:nr, :], s_rs[0:nr, :],
             op0=ALU.subtract, op1=ALU.mult)
        if tt >= 0:
            P.dma("gpsimd", P.dram("xnt").ap()[tt * 128:(tt + 1) * 128, :], xb[0:nr, 0:4096])
        for half in range(8):
            for j in range(4):
                kc = half * 4 + j
                P.op("tensor", "transpose", ptr[:, j * 128:j * 128 + nr],
                     xb[0:nr, kc * 128:(kc + 1) * 128], P.sb["c_idb"][0:nr, 0:nr])
            if tt < 0:
                for j in range(4):
                    kc = half * 4 + j
                    P.op("vector", "tensor_copy", xnh[:, kc * 4:kc * 4 + 3], ptr[:, j * 128:j * 128 + 3])
            else:
                ar = P.sb[f"ar{half // 1 // 1}"]
                ar = P.sb[f"ar{half}"]
                dst = view(ar, tt * 128, [[8192, 128], [2048, 4], [1, 128]])
                src = view(ptr, 0, [[1024, 128], [128, 4], [1, 128]])
                P.op("vector", "tensor_copy", dst, src)


def b_l0_inproj(P):
    xnh = P.sb["xnh"]
    wip = P.dram("wip").ap()
    ztd = P.dram("zt").ap()
    xbcd = P.dram("xbc").ap()
    wb = [P.sb["wb0"], P.sb["wb1"]]
    ev = [P.sb[f"ip_ev{i}"] for i in range(4)]
    ztr = P.sbuf("ip_ztr", [128, 512], F32)
    hev = P.sbuf("ip_hev", [128, 4], F32)
    dtr = P.sb["sc_c"]
    pm = [P.psum(f"PS{i}", [128, 512], F32) for i in range(4)]
    ptf = P.psum("PS4", [128, 512], F32)

    for oc in range(NOC):
        P.dma("gpsimd", wb[0][:, :], wip[oc, :, 0:16, :])
        P.dma("gpsimd", wb[1][:, :], wip[oc, :, 16:32, :])
        is_z, is_dt = oc < 64, oc == NOC - 1
        for ts4 in range(4):
            pi = ts4
            for kc in range(32):
                P.op("tensor", "matmul", pm[pi][:, :],
                     wb[kc // 16][:, (kc % 16) * 128:(kc % 16 + 1) * 128],
                     xn_rhs(P, kc, ts4 * 512, 512),
                     start=(kc == 0), stop=(kc == 31))
            if is_dt:
                P.op("vector", "tensor_copy", dtr[:, ts4 * 512:(ts4 + 1) * 512], pm[pi][:, :])
            elif is_z:
                P.op("vector", "tensor_copy", ev[pi][:, :], pm[pi][:, :])
                for j in range(4):
                    P.op("tensor", "transpose", ptf[:, j * 128:(j + 1) * 128],
                         ev[pi][:, j * 128:(j + 1) * 128], P.sb["c_idf"][:, :])
                P.op("vector", "tensor_copy", ztr[:, :], ptf[:, :])
                dst = view(ztd, (ts4 * 512) * DIN + oc * 128, [[DIN, 128], [DIN * 128, 4], [1, 128]])
                src = view(ztr, 0, [[512, 128], [128, 4], [1, 128]])
                P.dma("scalar", dst, src)
            else:
                P.op("vector", "tensor_copy", ev[pi][:, :], pm[pi][:, :])
                r0 = (oc - 64) * 128
                P.dma("scalar", xbcd[r0:r0 + 128, 3 + ts4 * 512:3 + (ts4 + 1) * 512], ev[pi][:, :])
        if (not is_z) and (not is_dt):
            for kc in range(32):
                P.op("tensor", "matmul", pm[0][:, 0:3],
                     wb[kc // 16][:, (kc % 16) * 128:(kc % 16 + 1) * 128],
                     xnh[:, kc * 4:kc * 4 + 3], start=(kc == 0), stop=(kc == 31))
            P.op("vector", "tensor_copy", hev[:, 0:3], pm[0][:, 0:3])
            r0 = (oc - 64) * 128
            P.dma("scalar", xbcd[r0:r0 + 128, 0:3], hev[:, 0:3])


def b_l0_dt(P):
    cumd = P.dram("cums").ap()
    dtb = P.sbuf("dtb", [128, 1], F32)
    alog = P.sbuf("alogv", [128, 1], F32)
    dvec = P.sbuf("dvecv", [128, 1], F32)
    dtr = P.sb["sc_c"]
    tmp = P.sb["sc_a"]
    acs2 = P.sb["sc_a"]
    acs = P.sbuf("acs_fm", [128, TPC], F32)
    dtt = P.sb["sc_b"]
    dbc = P.sbuf("d_bc", [128, 128], F32)
    cdec = P.sbuf("cdec", [128, 9], F32)
    rowt = P.sbuf("dt_rowt", [1, 128], F32)
    ptf = P.sb["PS4"]
    ptg = P.sb["PS3"]
    P.dma("sync", dtb[:, :], P.dram("dtbias").ap())
    P.dma("sync", alog[:, :], P.dram("alog").ap())
    P.dma("sync", dvec[:, :], P.dram("dvec").ap())

    T = lambda c0, n: tmp[:, c0:c0 + n]
    P.op("scalar", "activation", T(0, 2048), dtr[:, 0:2048], AF.Abs, bias=dtb[:, :])
    P.op("scalar", "activation", T(0, 2048), T(0, 2048), AF.Exp, scale=-1.0)
    P.op("scalar", "activation", T(0, 2048), T(0, 2048), AF.Ln, bias=ONEC(P))
    P.op("vector", "tensor_scalar", dtr[:, 0:2048], dtr[:, 0:2048], dtb[:, :], 0.0, op0=ALU.add, op1=ALU.max)
    P.op("vector", "tensor_add", dtr[:, 0:2048], dtr[:, 0:2048], T(0, 2048))
    P.op("scalar", "activation", alog[:, :], alog[:, :], AF.Exp)
    P.op("vector", "tensor_scalar", alog[:, :], alog[:, :], -1.0, None, op0=ALU.mult)
    P.op("vector", "tensor_scalar", acs[:, :], dtr[:, 0:2048], alog[:, :], None, op0=ALU.mult)
    for c in range(NCH):
        o = c * CH
        in_acs = True
        for sh in (1, 2, 4, 8, 16, 32, 64, 128):
            if in_acs:
                s_t, s_o, d_t, d_o = acs, o, acs2, 2048
            else:
                s_t, s_o, d_t, d_o = acs2, 2048, acs, o
            P.op("vector", "tensor_copy", d_t[:, d_o:d_o + sh], s_t[:, s_o:s_o + sh])
            P.op("vector", "tensor_add", d_t[:, d_o + sh:d_o + CH], s_t[:, s_o + sh:s_o + CH],
                 s_t[:, s_o:s_o + CH - sh])
            in_acs = not in_acs
        assert in_acs
    P.op("vector", "tensor_copy", cdec[:, 0:8], view(acs, CH - 1, [[TPC, 128], [CH, 8]]))
    P.op("vector", "memset", T(0, 1), 0.0)
    for z in range(1, 9):
        P.op("vector", "tensor_add", T(z, 1), T(z - 1, 1), cdec[:, z - 1:z])
    P.op("vector", "tensor_copy", cdec[:, 8:9], T(8, 1))
    P.op("scalar", "activation", cdec[:, :], cdec[:, :], AF.Exp)
    P.op("scalar", "activation", T(0, 9), T(0, 9), AF.Exp)
    P.op("tensor", "transpose", ptf[0:8, 0:128], T(0, 8), P.sb["c_idf"][:, :])
    P.op("vector", "tensor_copy", P.sb["ip_ev0"][0:8, 0:128], ptf[0:8, 0:128])
    P.dma("scalar", cumd[:, :], P.sb["ip_ev0"][0:8, 0:128])
    P.op("tensor", "transpose", ptg[0:1, 0:128], dvec[:, 0:1], P.sb["c_idf"][:, :])
    P.op("vector", "tensor_copy", rowt[:, :], ptg[0:1, 0:128])
    P.op("tensor", "matmul", ptg[:, 128:256], P.sb["c_onerf"][:, :], rowt[:, :], start=True, stop=True)
    P.op("vector", "tensor_copy", dbc[:, :], ptg[:, 128:256])
    for tt in range(NT):
        P.op("tensor", "transpose", ptf[:, 0:128], dtr[:, tt * 128:(tt + 1) * 128], P.sb["c_idf"][:, :])
        P.op("vector", "tensor_copy", dtt[:, tt * 128:(tt + 1) * 128], ptf[:, 0:128])


def b_l0_conv(P):
    xbcd = P.dram("xbc").ap()
    xdtd = P.dram("xdt").ap()
    dxtd = P.dram("dxt").ap()
    bfmd = P.dram("bfm").ap()
    cfmd = P.dram("cfm").ap()
    cw = P.sbuf("cv_w", [128, 320], F32)
    cb = P.sbuf("cv_b", [128, 80], F32)
    cin = [P.sb["sc_c"], P.sb["sc_d"]]
    cacc = P.sb["sc_a"]
    cbf = bfv(P, "sc_a")
    xevb = [P.sb["ip_ev0"][:, :].bitcast(BF16), P.sb["ip_ev1"][:, :].bitcast(BF16)]
    devb = [P.sb["ip_ev2"][:, :].bitcast(BF16), P.sb["ip_ev3"][:, :].bitcast(BF16)]
    pbf = P.sb["PT0"]
    dtt = P.sb["sc_b"]
    dbc = P.sb["d_bc"]
    P.dma("sync", cw[:, :], view(P.dram("convw"), 0, [[4, 128], [512, 80], [1, 4]]))
    P.dma("sync", cb[:, :], view(P.dram("convb"), 0, [[1, 128], [128, 80]]), allow_slow_non_contiguous=True)
    P.dma("sync", cin[0][:, 0:2051], xbcd[0:128, :])
    for fc in range(80):
        bb = fc % 2
        if fc + 1 < 80:
            P.dma("sync", cin[(fc + 1) % 2][:, 0:2051], xbcd[(fc + 1) * 128:(fc + 2) * 128, :])
        P.op("vector", "tensor_scalar", cacc[:, 0:2048], cin[bb][:, 0:TPC], cw[:, fc * 4:fc * 4 + 1],
             None, op0=ALU.mult)
        for k in range(1, 4):
            P.op("vector", "scalar_tensor_tensor", cacc[:, 0:2048], cin[bb][:, k:k + TPC],
                 cw[:, fc * 4 + k:fc * 4 + k + 1], cacc[:, 0:2048], op0=ALU.mult, op1=ALU.add)
        P.op("scalar", "activation", cacc[:, 0:2048], cacc[:, 0:2048], AF.Silu, bias=cb[:, fc:fc + 1])
        P.op("vector", "tensor_copy", cbf[:, 4096:6144], cacc[:, 0:2048])
        if fc < 64:
            h0 = 2 * fc
            for tb8 in range(2):
                for j in range(8):
                    tt = tb8 * 8 + j
                    P.op("tensor", "transpose", pbf[:, j * 128:(j + 1) * 128],
                         cbf[:, 4096 + tt * 128:4096 + (tt + 1) * 128], P.sb["c_idb"][:, :])
                pv = view(pbf, 0, [[1024, 128], [128, 8], [64, 2], [1, 64]])
                dtv = view(dtt, tb8 * 1024 + h0, [[4096, 128], [128, 8], [1, 2], [0, 64]])
                dbv = view(dbc, h0, [[128, 128], [0, 8], [1, 2], [0, 64]])
                P.op("vector", "tensor_tensor",
                     view(xevb[tb8], 0, [[1024, 128], [128, 8], [64, 2], [1, 64]]),
                     pv, dtv, op=ALU.mult)
                P.op("vector", "tensor_tensor",
                     view(devb[tb8], 0, [[1024, 128], [128, 8], [64, 2], [1, 64]]),
                     pv, dbv, op=ALU.mult)
                P.dma("scalar",
                      view(xdtd, (tb8 * 8 * 128) * DIN + fc * 128,
                           [[DIN, 128], [128 * DIN, 8], [1, 128]]),
                      view(xevb[tb8], 0, [[1024, 128], [128, 8], [1, 128]]))
                P.dma("scalar",
                      view(dxtd, (tb8 * 8 * 128) * DIN + fc * 128,
                           [[DIN, 128], [128 * DIN, 8], [1, 128]]),
                      view(devb[tb8], 0, [[1024, 128], [128, 8], [1, 128]]))
        else:
            r0 = (fc - 64) * 128 if fc < 72 else (fc - 72) * 128
            dst = bfmd if fc < 72 else cfmd
            P.dma("scalar", dst[r0:r0 + 128, :], cbf[:, 4096:6144])


def tile4(Wt):
    K, M = Wt.shape
    return np.ascontiguousarray(Wt.reshape(K // 128, 128, M // 128, 128).transpose(2, 1, 0, 3))


def host_weights(inputs):
    W = {}
    wip = inputs["L0_attn_in_proj_w"]
    W["wip"] = np.ascontiguousarray(wip.reshape(NOC, 128, 32, 128).transpose(0, 3, 2, 1))
    W["convw"] = np.ascontiguousarray(inputs["L0_attn_conv_w"].reshape(80, 128, 4))
    W["convb"] = np.ascontiguousarray(inputs["L0_attn_conv_b"].reshape(80, 128))
    W["dtbias"] = np.ascontiguousarray(inputs["L0_attn_dt_bias"].reshape(H, 1))
    W["alog"] = np.ascontiguousarray(inputs["L0_attn_A_log"].reshape(H, 1))
    W["dvec"] = np.ascontiguousarray(inputs["L0_attn_D"].reshape(H, 1))
    W["normw"] = np.ascontiguousarray(inputs["L0_attn_norm_w"].reshape(1, DIN))
    W["wout"] = np.ascontiguousarray(inputs["L0_attn_out_proj_w"].T)
    W["woutt"] = np.ascontiguousarray(W["wout"].reshape(64, 128, 4, 1024).transpose(2, 0, 1, 3))
    W["nwt"] = np.ascontiguousarray(inputs["L0_attn_norm_w"].reshape(64, 128).T)
    W["wb0"] = np.ascontiguousarray(inputs["L0_ffn_base_W"].T)
    W["wb0t"] = tile4(W["wb0"])
    W["wg0"] = np.ascontiguousarray(inputs["L0_ffn_gate_W"].T)
    W["wla0"] = np.ascontiguousarray(inputs["L0_ffn_lora_A"].transpose(2, 0, 1).reshape(D, 224))
    lb0 = np.ascontiguousarray(inputs["L0_ffn_lora_B"].transpose(0, 2, 1).reshape(224, D))
    W["wlb0a"] = np.ascontiguousarray(lb0[0:128])
    W["wlb0b"] = np.ascontiguousarray(lb0[128:224])
    W["wqr"] = np.ascontiguousarray(
        inputs["L1_attn_wq"].reshape(D, 8, 4, 128).transpose(2, 0, 1, 3).reshape(4, D, 1024))
    W["wk"] = np.ascontiguousarray(inputs["L1_attn_wk"])
    W["wkt"] = tile4(W["wk"])
    W["wv"] = np.ascontiguousarray(inputs["L1_attn_wv"])
    W["wvt"] = tile4(W["wv"])
    W["wo"] = np.ascontiguousarray(inputs["L1_attn_wo"])
    W["wb1"] = np.ascontiguousarray(inputs["L1_ffn_W_base"].T)
    W["wb1t"] = tile4(W["wb1"])
    W["bb1"] = np.ascontiguousarray(inputs["L1_ffn_b_base"].reshape(1, D))
    W["wr1"] = np.ascontiguousarray(inputs["L1_ffn_W_router"].T)
    W["br1"] = np.ascontiguousarray(inputs["L1_ffn_b_router"].reshape(1, 8))
    W["wla1"] = np.ascontiguousarray(inputs["L1_ffn_lora_A"].transpose(1, 0, 2).reshape(D, 128))
    W["wlb1"] = np.ascontiguousarray(inputs["L1_ffn_lora_B"].reshape(128, D))
    W["c_ident"] = np.eye(128, dtype=np.float32)
    W["c_ones"] = np.ones((128, 128), dtype=np.float32)
    msk = np.zeros((2, 128, CH), dtype=np.float32)
    s = np.arange(128)[:, None]
    l = np.arange(CH)[None, :]
    msk[0] = (s <= l).astype(np.float32)
    msk[1] = ((s + 128) <= l).astype(np.float32)
    W["c_mask"] = msk
    tn = np.zeros((128, 128), dtype=np.float32)
    tn[np.arange(128)[:, None] > np.arange(128)[None, :]] = -1e30
    W["c_trineg"] = tn
    cmisc = np.zeros((128, 4), dtype=np.float32)
    cmisc[:, 0] = EPS
    cmisc[:, 1] = 1.0
    cmisc[:, 2] = RSQD
    W["c_misc"] = cmisc
    return W


def host_percore(inputs, core):
    seq, pos = core // 4, core % 4
    x = inputs["L0_attn_hidden_states"]
    xh = np.zeros((2051, D), dtype=np.float32)
    lo = pos * TPC
    if pos == 0:
        xh[3:] = x[seq, 0:TPC]
    else:
        xh[:] = x[seq, lo - 3:lo + TPC]
    rel = np.zeros((128, 8), dtype=np.float32)
    for r in range(4):
        rel[:, r] = 1.0 if r < pos else 0.0
        rel[:, 4 + r] = 1.0 - rel[:, r]
    return {"xh": np.ascontiguousarray(xh), "relmask": rel}


def f32v(P, name):
    return P.sb[name][:, :].bitcast(F32)


def arf(P, j):
    return P.sb[f"ar{j}"][:, :].bitcast(F32)


def bcast_ap(base_ap, inner_rep, n_outer, outer_stride, offset=0):
    return _AP(base_ap.tensor, base_ap.offset + offset,
               [list(base_ap.ap[0]), [outer_stride, n_outer], [0, inner_rep]])


def b_l0_ffn(P):
    gffnd = P.dram("gffn").ap()
    wla = P.sb["sc_a"][:, :].bitcast(BF16)
    wg = P.sb["sc_b"][:, :].bitcast(BF16)
    lba = P.sb["sc_a"][:, :].bitcast(BF16)
    ws = [P.sb["sc_c"][:, :].bitcast(BF16), P.sb["sc_d"][:, :].bitcast(BF16)]
    m1 = P.sbuf("f_m1", [128, 1], F32)
    nm1 = P.sbuf("f_nm1", [128, 1], F32)
    m2 = P.sbuf("f_m2", [128, 1], F32)
    ps = [P.sb["PS0"], P.sb["PS1"], P.sb["PS2"], P.sb["PS3"]]
    ps4 = P.sb["PS4"]
    ptr = P.sb["PT0"]
    hf = [P.sb["wb0"], P.sb["wb1"]]
    ev = [P.sb[f"ip_ev{i}"] for i in range(4)]
    hp = ev[2][:, 0:224]
    hb = ev[3][:, :].bitcast(BF16)[:, 0:224]
    lg = ev[0][:, 0:28]
    gt = ev[0][:, 32:60]

    P.dma("gpsimd", wla[:, 0:7168].rearrange("p (k r) -> p k r", k=32),
          view(P.dram("wla0"), 0, [[224, 128], [128 * 224, 32], [1, 224]]))
    P.dma("gpsimd", wg[:, 0:896].rearrange("p (k r) -> p k r", k=32),
          view(P.dram("wg0"), 0, [[28, 128], [128 * 28, 32], [1, 28]]))

    for tt in range(NT):
        for kc in range(32):
            P.op("tensor", "matmul", ps4[:, 0:28], xn_rhs(P, kc, tt * 128, 128),
                 wg[:, kc * 28:(kc + 1) * 28],
                 start=(kc == 0), stop=(kc == 31))
        for kc in range(32):
            P.op("tensor", "matmul", ps4[:, 32:256], xn_rhs(P, kc, tt * 128, 128),
                 wla[:, kc * 224:(kc + 1) * 224],
                 start=False, stop=(kc == 31))
        P.op("vector", "tensor_copy", lg, ps4[:, 0:28])
        P.op("vector", "tensor_copy", hp, ps4[:, 32:256])
        P.op("vector", "reduce_max", m1[:, :], lg, axis=AX)
        P.op("vector", "tensor_scalar", nm1[:, :], m1[:, :], -1.0, None, op0=ALU.mult)
        P.op("vector", "tensor_scalar", gt, lg, m1[:, :], None, op0=ALU.is_equal)
        P.op("vector", "scalar_tensor_tensor", gt, gt, -1e30, lg, op0=ALU.mult, op1=ALU.add)
        P.op("vector", "reduce_max", m2[:, :], gt, axis=AX)
        P.op("vector", "tensor_scalar", gt, lg, m2[:, :], None, op0=ALU.is_ge)
        P.op("scalar", "activation", lg, lg, AF.Exp, bias=nm1[:, :])
        P.op("vector", "tensor_mul", lg, lg, gt)
        P.op("vector", "reduce_sum", m2[:, :], lg, axis=AX)
        P.op("vector", "reciprocal", m2[:, :], m2[:, :])
        P.op("vector", "tensor_scalar", gt, lg, m2[:, :], 2.0, op0=ALU.mult, op1=ALU.mult)
        gv = _AP(gt.tensor, gt.offset, [list(gt.ap[0]), [1, 28], [0, 8]])
        P.op("vector", "tensor_tensor", hb.rearrange("p (e r) -> p e r", e=28),
             hp.rearrange("p (e r) -> p e r", e=28), gv, op=ALU.mult)
        P.op("tensor", "transpose", ptr[:, 0:128], hb[:, 0:128], P.sb["c_idb"][:, :])
        P.op("tensor", "transpose", ptr[0:96, 128:256], hb[:, 128:224], P.sb["c_idb"][:, :])
        P.op("vector", "tensor_copy", hf[0][:, tt * 128:(tt + 1) * 128], ptr[:, 0:128])
        P.op("vector", "tensor_copy", hf[1][0:96, tt * 128:(tt + 1) * 128], ptr[0:96, 128:256])

    P.dma("gpsimd", lba[:, 0:4096], P.dram("wlb0a").ap())
    P.dma("gpsimd", lba[0:96, 4096:8192], P.dram("wlb0b").ap())
    wbt = P.dram("wb0t").ap()
    for og in range(32):
        sb = og % 2
        P.dma("gpsimd", ws[sb][:, 0:4096], wbt[og])
        for tt in range(NT):
            pp = ps[tt % 4]
            for kc in range(32):
                P.op("tensor", "matmul", pp[:, 0:128], xn_rhs(P, kc, tt * 128, 128),
                     ws[sb][:, kc * 128:(kc + 1) * 128],
                     start=(kc == 0), stop=False)
            P.op("tensor", "matmul", pp[:, 0:128], hf[0][:, tt * 128:(tt + 1) * 128],
                 lba[:, og * 128:(og + 1) * 128], start=False, stop=False)
            P.op("tensor", "matmul", pp[:, 0:128], hf[1][0:96, tt * 128:(tt + 1) * 128],
                 lba[0:96, 4096 + og * 128:4096 + (og + 1) * 128], start=False, stop=True)
            gev = ev[tt % 4]
            P.op("scalar", "activation", gev[:, 0:128], pp[:, 0:128], AF.Silu)
            P.dma("scalar", gffnd[tt * 128:(tt + 1) * 128, og * 128:(og + 1) * 128], gev[:, 0:128])


def b_l0_ssd(P):
    ytd = P.dram("yt").ap()
    ead = P.dram("expacs").ap()
    bfmd = P.dram("bfm").ap()
    cfmd = P.dram("cfm").ap()
    xdtd = P.dram("xdt").ap()
    gind = P.dram("gin").ap()
    goutd = P.dram("gout").ap()
    acs = P.sb["acs_fm"]
    cdec = P.sb["cdec"]
    idb = P.sb["c_idb"]
    idf = P.sb["c_idf"]
    trn = P.sbuf("c_trn", [128, 128], BF16)
    P.dma("gpsimd", trn[:, :], P.dram("c_trineg").ap())
    ar = [P.sb[f"ar{j}"] for j in range(8)]
    ev = [P.sb[f"ip_ev{i}"] for i in range(4)]
    PS = [P.sb[f"PS{i}"] for i in range(7)]
    PT0 = P.sb["PT0"]
    A5 = dict(bch=0, cch=2048, bt0=4096, bt1=5120, mt=6144, ah=7296, al=7552, ds=7808)
    A6 = dict(G=0, nacs=3072, eacs=3328, cdb=3584, st1=3712)

    def a5(c0, n):
        return view(ar[5], c0, [[8192, 128], [1, n]])

    def a6(c0, n, p=128):
        return _AP(arf(P, 6).tensor, c0, [[4096, p], [1, n]]) if False else arf(P, 6)[:, c0:c0 + n]

    for c in range(NCH):
        o = c * CH
        P.dma("sync", a5(A5["bch"], 2048).rearrange("p (g s) -> p g s", g=8),
              view(P.dram("bfm"), o, [[2048, 128], [262144, 8], [1, 256]]))
        P.dma("sync", a5(A5["cch"], 2048).rearrange("p (g s) -> p g s", g=8),
              view(P.dram("cfm"), o, [[2048, 128], [262144, 8], [1, 256]]))
        P.dma("sync", ar[3][:, :], xdtd[o:o + 128, :])
        P.dma("sync", ar[4][:, :], xdtd[o + 128:o + 256, :])
        P.op("vector", "tensor_copy", a5(A5["ah"], 256), acs[:, o:o + CH])
        P.op("vector", "tensor_copy", ev[0][:, 0:256], a5(A5["ah"], 256))
        P.op("vector", "tensor_sub", ev[0][:, 256:512], acs[:, o:o + CH], ev[0][:, 0:256])
        P.op("vector", "tensor_copy", a5(A5["al"], 256), ev[0][:, 256:512])
        for a in range(2):
            P.op("tensor", "transpose", PS[4][:, a * 128:(a + 1) * 128],
                 acs[:, o + a * 128:o + (a + 1) * 128], idf[:, :])
        P.op("vector", "tensor_scalar", a6(A6["nacs"], 256), PS[4][:, 0:256], -1.0, None, op0=ALU.mult)
        P.op("scalar", "activation", a6(A6["eacs"], 256), PS[4][:, 0:256], AF.Exp)
        for a in range(2):
            P.dma("scalar", ead[o + a * 128:o + (a + 1) * 128, :],
                  a6(A6["eacs"] + a * 128, 128))
        P.op("vector", "tensor_scalar", a6(A6["st1"], 256), acs[:, o:o + CH],
             acs[:, o + CH - 1:o + CH], None, op0=ALU.subtract)
        P.op("scalar", "activation", a6(A6["st1"], 256), a6(A6["st1"], 256), AF.Exp, scale=-1.0)
        for a in range(2):
            P.op("tensor", "transpose", PS[4][:, 256 + a * 128:256 + (a + 1) * 128],
                 a6(A6["st1"] + a * 128, 128), idf[:, :])
        P.op("vector", "tensor_copy", a5(A5["ds"], 256), PS[4][:, 256:512])
        P.op("tensor", "transpose", PS[4][0:1, 0:128], cdec[:, c:c + 1], idf[:, :])
        P.op("vector", "tensor_copy", P.sb["dt_rowt"][:, :], PS[4][0:1, 0:128])
        P.op("tensor", "matmul", PS[4][:, 128:256], P.sb["c_onerf"][:, :], P.sb["dt_rowt"][:, :],
             start=True, stop=True)
        P.op("vector", "tensor_copy", a6(A6["cdb"], 128), PS[4][:, 128:256])
        for a in range(2):
            for g in range(8):
                P.op("tensor", "transpose", PT0[:, g * 128:(g + 1) * 128],
                     a5(A5["bch"] + g * 256 + a * 128, 128), idb[:, :])
            P.op("vector", "tensor_copy", a5(A5["bt0"] if a == 0 else A5["bt1"], 1024), PT0[:, 0:1024])
        if c > 0:
            P.op("vector", "tensor_copy", ar[2][:, 0:4096], arf(P, 0)[:, :])
            P.op("vector", "tensor_copy", ar[2][:, 4096:8192], arf(P, 1)[:, :])
        for g in range(8):
            P.op("tensor", "matmul", PS[0][:, 0:256], a5(A5["bch"] + g * 256, 128),
                 a5(A5["cch"] + g * 256, 256), start=True, stop=True)
            P.op("tensor", "matmul", PS[0][:, 256:384], a5(A5["bch"] + g * 256 + 128, 128),
                 a5(A5["cch"] + g * 256 + 128, 128), start=False, stop=True)
            P.op("vector", "tensor_copy", a6(A6["G"] + g * 384, 256), PS[0][:, 0:256])
            P.op("vector", "tensor_copy", a6(A6["G"] + g * 384 + 256, 128), PS[0][:, 256:384])
        def r_l(i):
            return PS[0][:, (i % 2) * 256:(i % 2) * 256 + 256] if i < 2 else \
                   PS[5][:, (i - 2) * 256:(i - 2) * 256 + 256]

        def r_m(i):
            return PS[6][:, i * 128:(i + 1) * 128]

        EBS = [ev[0][:, 0:384], ev[1][:, 0:384], P.sb["sc_a"][:, 0:384]]

        def emit_R(h):
            lhb = _AP(idb[:, h:h + 1].tensor, idb[:, h:h + 1].offset,
                      [list(idb[:, h:h + 1].ap[0]), [0, 128]])
            i = h % 4
            RL, RM = r_l(i), r_m(i)
            P.op("tensor", "matmul", RL, lhb, a5(A5["ah"], 256), start=True, stop=False)
            P.op("tensor", "matmul", RL, lhb, a5(A5["al"], 256), start=False, stop=False)
            P.op("tensor", "matmul", RL[:, 0:128], P.sb["c_idb"][:, :], trn[:, :],
                 start=False, stop=True)
            P.op("tensor", "matmul", RM, lhb, a5(A5["ah"] + 128, 128), start=True, stop=False)
            P.op("tensor", "matmul", RM, lhb, a5(A5["al"] + 128, 128), start=False, stop=False)
            P.op("tensor", "matmul", RM, P.sb["c_idb"][:, :], trn[:, :], start=False, stop=True)

        def emit_Mt(h):
            g = h // 16
            i = h % 4
            eb = EBS[h % 3]
            mt_ = A5["mt"] + (h % 3) * 384
            P.op("scalar", "activation", eb[:, 0:256], r_l(i), AF.Exp,
                 bias=arf(P, 6)[:, A6["nacs"] + h:A6["nacs"] + h + 1])
            P.op("scalar", "activation", eb[:, 256:384], r_m(i), AF.Exp,
                 bias=arf(P, 6)[:, A6["nacs"] + 128 + h:A6["nacs"] + 128 + h + 1])
            P.op("vector", "tensor_mul", a5(mt_, 256), eb[:, 0:256], a6(A6["G"] + g * 384, 256))
            P.op("vector", "tensor_mul", a5(mt_ + 256, 128), eb[:, 256:384],
                 a6(A6["G"] + g * 384 + 256, 128))

        for i in range(4):
            emit_R(i)
            if i < 2:
                emit_Mt(i)
        for hg in range(16):
            for hh in range(8):
                h = hg * 8 + hh
                g = h // 16
                if h + 2 < 128:
                    emit_Mt(h + 2)
                if h + 4 < 128:
                    emit_R(h + 4)
                mt_ = A5["mt"] + (h % 3) * 384
                first = hh == 0
                last = hh == 7
                P.op("tensor", "matmul", PS[1][:, hh * 64:(hh + 1) * 64], a5(mt_, 128),
                     ar[3][:, h * 64:(h + 1) * 64], start=first, stop=last)
                P.op("tensor", "matmul", PS[2][:, hh * 64:(hh + 1) * 64], a5(mt_ + 128, 128),
                     ar[3][:, h * 64:(h + 1) * 64], start=first, stop=False)
                P.op("tensor", "matmul", PS[2][:, hh * 64:(hh + 1) * 64], a5(mt_ + 256, 128),
                     ar[4][:, h * 64:(h + 1) * 64], start=False, stop=last)
                if c > 0:
                    sbf = view(ar[2], h * 64, [[8192, 128], [1, 64]])
                    P.op("tensor", "matmul", PS[3][:, hh * 64:(hh + 1) * 64],
                         a5(A5["cch"] + g * 256, 128), sbf, start=first, stop=last)
                    P.op("tensor", "matmul", PS[4][:, hh * 64:(hh + 1) * 64],
                         a5(A5["cch"] + g * 256 + 128, 128), sbf, start=first, stop=last)
            for lh in range(2):
                dps = PS[1] if lh == 0 else PS[2]
                ops_ = PS[3] if lh == 0 else PS[4]
                eav = _AP(arf(P, 6).tensor, A6["eacs"] + lh * 128 + hg * 8,
                          [[4096, 128], [1, 8], [0, 64]])
                if c > 0:
                    P.op("vector", "tensor_tensor", ev[2][:, 0:512].rearrange("p (a b) -> p a b", a=8),
                         ops_[:, 0:512].rearrange("p (a b) -> p a b", a=8), eav, op=ALU.mult)
                    P.op("vector", "scalar_tensor_tensor", ev[3][:, 0:512], dps[:, 0:512], 1.0,
                         ev[2][:, 0:512], op0=ALU.mult, op1=ALU.add)
                else:
                    P.op("vector", "tensor_copy", ev[3][:, 0:512], dps[:, 0:512])
                P.dma("scalar", ytd[o + lh * 128:o + (lh + 1) * 128, hg * 512:(hg + 1) * 512],
                      ev[3][:, 0:512])
        for a in range(2):
            dsv = _AP(a5(A5["ds"], 256).tensor, a5(0, 1).offset + A5["ds"] + a * 128,
                      [[8192, 128], [1, 128], [0, 64]])
            P.op("vector", "tensor_tensor", ar[3 + a][:, :].rearrange("p (h q) -> p h q", h=128),
                 ar[3 + a][:, :].rearrange("p (h q) -> p h q", h=128), dsv, op=ALU.mult)
        for hg in range(16):
            sps = PS[1] if hg % 2 == 0 else PS[2]
            for hh in range(8):
                h = hg * 8 + hh
                g = h // 16
                P.op("tensor", "matmul", sps[:, hh * 64:(hh + 1) * 64], a5(A5["bt0"] + g * 128, 128),
                     ar[3][:, h * 64:(h + 1) * 64], start=(hh == 0), stop=False)
                P.op("tensor", "matmul", sps[:, hh * 64:(hh + 1) * 64], a5(A5["bt1"] + g * 128, 128),
                     ar[4][:, h * 64:(h + 1) * 64], start=False, stop=(hh == 7))
            sr = arf(P, 0)[:, (hg % 8) * 512:(hg % 8 + 1) * 512] if hg < 8 else                  arf(P, 1)[:, (hg - 8) * 512:(hg - 8 + 1) * 512]
            if c > 0:
                cdv = _AP(arf(P, 6).tensor, A6["cdb"] + hg * 8, [[4096, 128], [1, 8], [0, 64]])
                P.op("vector", "tensor_tensor", ev[0][:, 0:512].rearrange("p (a b) -> p a b", a=8),
                     sr.rearrange("p (a b) -> p a b", a=8), cdv, op=ALU.mult)
                P.op("vector", "scalar_tensor_tensor", sr, sps[:, 0:512], 1.0, ev[0][:, 0:512],
                     op0=ALU.mult, op1=ALU.add)
            else:
                P.op("vector", "tensor_copy", sr, sps[:, 0:512])
    P.dma("scalar", gind[:, 0:4096], arf(P, 0)[:, :])
    P.dma("scalar", gind[:, 4096:8192], arf(P, 1)[:, :])
    P.dma("scalar", gind[:, 8192:8193], cdec[:, 8:9], allow_slow_non_contiguous=True)
    P.collective("AllGather", ALU.bypass, [[0, 1, 2, 3], [4, 5, 6, 7]], gind, goutd)
    if "goutx" in P.ext_out:
        P.dma("sync", P.dram("goutx").ap(), goutd)


def b_l0_post(P):
    gname = "goutx" if "goutx" in P.ext_in else "gout"
    goutd = P.dram(gname).ap()
    cumd = P.dram("cums").ap()
    ead = P.dram("expacs").ap()
    cfmd = P.dram("cfm").ap()
    ytd = P.dram("yt").ap()
    ztd = P.dram("zt").ap()
    dxtd = P.dram("dxt").ap()
    gffnd = P.dram("gffn").ap()
    xhd = P.dram("xh").ap()
    yfmd = P.dram("yfm").ap()
    x1d = P.dram("x1").ap()
    idb = P.sb["c_idb"]
    idf = P.sb["c_idf"]
    ar = [P.sb[f"ar{j}"] for j in range(8)]
    ev = [P.sb[f"ip_ev{i}"] for i in range(4)]
    PS = [P.sb[f"PS{i}"] for i in range(7)]
    PT0 = P.sb["PT0"]
    mrel = P.sbuf("mrel", [128, 8], F32)
    dmr = P.sbuf("dmr", [128, 8], F32)
    suf = P.sbuf("suf", [128, 1], F32)
    rrow = P.sbuf("dt_rowt", [1, 128], F32)
    nwt = P.sbuf("nwt", [128, 64], F32)
    rms = P.sbuf("rms_c", [128, 4], F32)
    P.dma("sync", mrel[:, :], P.dram("relmask").ap())
    P.dma("sync", nwt[:, :], P.dram("nwt").ap())

    for r in range(4):
        P.dma("sync", dmr[:, r:r + 1], goutd[r * 128:(r + 1) * 128, 8192:8193],
              allow_slow_non_contiguous=True)
        P.op("vector", "tensor_scalar", dmr[:, r:r + 1], dmr[:, r:r + 1], mrel[:, r:r + 1], None,
             op0=ALU.mult)
        P.op("vector", "tensor_scalar", dmr[:, r:r + 1], dmr[:, r:r + 1], mrel[:, 4 + r:5 + r], None,
             op0=ALU.add)
    P.op("vector", "memset", suf[:, :], 1.0)
    for r in (3, 2, 1, 0):
        P.op("vector", "tensor_mul", dmr[:, 4 + r:5 + r], mrel[:, r:r + 1], suf[:, :])
        if r > 0:
            P.op("vector", "tensor_mul", suf[:, :], suf[:, :], dmr[:, r:r + 1])
    for r in range(4):
        P.op("tensor", "transpose", PS[4][0:1, 0:128], dmr[:, 4 + r:5 + r], idf[:, :])
        P.op("vector", "tensor_copy", rrow[:, :], PS[4][0:1, 0:128])
        P.op("tensor", "matmul", PS[4][:, 128:256], P.sb["c_onerf"][:, :], rrow[:, :], start=True, stop=True)
        P.op("vector", "tensor_copy", arf(P, 6)[:, r * 128:(r + 1) * 128], PS[4][:, 128:256])
    for r in range(4):
        P.dma("sync", arf(P, 2)[:, :], goutd[r * 128:(r + 1) * 128, 0:4096])
        P.dma("sync", arf(P, 3)[:, :], goutd[r * 128:(r + 1) * 128, 4096:8192])
        for half in range(2):
            cbv = _AP(arf(P, 6).tensor, r * 128 + half * 64, [[4096, 128], [1, 64], [0, 64]])
            src_ = arf(P, 2) if half == 0 else arf(P, 3)
            if r == 0:
                P.op("vector", "tensor_tensor",
                     arf(P, half)[:, :].rearrange("p (a b) -> p a b", a=64),
                     src_[:, :].rearrange("p (a b) -> p a b", a=64), cbv, op=ALU.mult)
            else:
                P.op("vector", "tensor_tensor",
                     arf(P, 4)[:, :].rearrange("p (a b) -> p a b", a=64),
                     src_[:, :].rearrange("p (a b) -> p a b", a=64), cbv, op=ALU.mult)
                P.op("vector", "tensor_add", arf(P, half)[:, :], arf(P, half)[:, :], arf(P, 4)[:, :])

    for tt in range(NT):
        c, lh = tt // 2, tt % 2
        if lh == 0:
            P.dma("sync", view(ar[5], 0, [[8192, 128], [256, 8], [1, 256]]),
                  view(P.dram("cfm"), c * 256, [[2048, 128], [262144, 8], [1, 256]]))
            P.dma("sync", rrow[:, :], cumd[c:c + 1, :])
            P.op("tensor", "matmul", PS[4][:, 128:256], P.sb["c_onerf"][:, :], rrow[:, :],
                 start=True, stop=True)
            P.op("vector", "tensor_copy", arf(P, 6)[:, 512:640], PS[4][:, 128:256])
            for half in range(2):
                cmv = _AP(arf(P, 6).tensor, 512 + half * 64, [[4096, 128], [1, 64], [0, 64]])
                P.op("vector", "tensor_tensor",
                     ar[2][:, half * 4096:(half + 1) * 4096].rearrange("p (a b) -> p a b", a=64),
                     arf(P, half)[:, :].rearrange("p (a b) -> p a b", a=64), cmv, op=ALU.mult)
        P.dma("sync", arf(P, 3)[:, :], ytd[tt * 128:(tt + 1) * 128, 0:4096])
        P.dma("sync", arf(P, 4)[:, :], ytd[tt * 128:(tt + 1) * 128, 4096:8192])
        P.dma("sync", P.sb["sc_a"][:, :], ztd[tt * 128:(tt + 1) * 128, 0:4096])
        P.dma("gpsimd", P.sb["sc_b"][:, :], dxtd[tt * 128:(tt + 1) * 128, 0:4096])
        P.dma("sync", arf(P, 7)[:, :], ztd[tt * 128:(tt + 1) * 128, 4096:8192])
        P.dma("sync", arf(P, 6)[:, 640:768], ead[tt * 128:(tt + 1) * 128, :])
        for hg in range(16):
            pp = PS[hg % 2]
            for hh in range(8):
                h = hg * 8 + hh
                g = h // 16
                P.op("tensor", "matmul", pp[:, hh * 64:(hh + 1) * 64],
                     view(ar[5], g * 256 + lh * 128, [[8192, 128], [1, 128]]),
                     view(ar[2], h * 64, [[8192, 128], [1, 64]]),
                     start=(hh == 0), stop=(hh == 7))
            eav = _AP(arf(P, 6).tensor, 640 + hg * 8, [[4096, 128], [1, 8], [0, 64]])
            P.op("vector", "tensor_tensor", ev[0][:, 0:512].rearrange("p (a b) -> p a b", a=8),
                 pp[:, 0:512].rearrange("p (a b) -> p a b", a=8), eav, op=ALU.mult)
            yh = arf(P, 3)[:, (hg % 8) * 512:(hg % 8 + 1) * 512] if hg < 8 else                  arf(P, 4)[:, (hg - 8) * 512:(hg - 8) * 512 + 512]
            P.op("vector", "tensor_add", yh, yh, ev[0][:, 0:512])
        for half in range(2):
            yh = arf(P, 3 + half)
            zt_h = P.sb["sc_a"][:, :] if half == 0 else arf(P, 7)[:, :]
            if half == 1:
                P.dma("gpsimd", P.sb["sc_b"][:, :],
                      dxtd[tt * 128:(tt + 1) * 128, 4096:8192])
            P.op("vector", "tensor_add", yh[:, :], yh[:, :], P.sb["sc_b"][:, :])
            P.op("scalar", "activation", zt_h, zt_h, AF.Silu)
            P.op("vector", "tensor_mul", yh[:, :], yh[:, :], zt_h)
            P.op("scalar", "activation", P.sb["sc_b"][:, :], yh[:, :], AF.Square,
                 accum_out=rms[:, half:half + 1])
        P.op("vector", "tensor_add", rms[:, 2:3], rms[:, 0:1], rms[:, 1:2])
        P.op("vector", "tensor_scalar", rms[:, 2:3], rms[:, 2:3], 1.0 / DIN, None, op0=ALU.mult)
        P.op("scalar", "activation", rms[:, 2:3], rms[:, 2:3], AF.Sqrt, bias=EPSC(P))
        P.op("vector", "reciprocal", rms[:, 2:3], rms[:, 2:3])
        for half in range(2):
            yh = arf(P, 3 + half)
            yrb = ar[5][:, 4096:8192]
            P.op("vector", "tensor_scalar", yrb, yh[:, :], rms[:, 2:3], None, op0=ALU.mult)
            for b8 in range(4):
                for j in range(8):
                    kc = half * 32 + b8 * 8 + j
                    P.op("tensor", "transpose", PT0[:, j * 128:(j + 1) * 128],
                         yrb[:, (b8 * 8 + j) * 128:(b8 * 8 + j + 1) * 128], idb[:, :])
                for j in range(8):
                    kc = half * 32 + b8 * 8 + j
                    P.op("vector", "tensor_scalar", P.sb["wb0"][:, j * 256:j * 256 + 128],
                         PT0[:, j * 128:(j + 1) * 128], nwt[:, kc:kc + 1], None, op0=ALU.mult)
                dst = view(yfmd, (tt * 64 + half * 32 + b8 * 8) * 128 * 128,
                           [[128, 128], [16384, 8], [1, 128]])
                P.dma("scalar", dst, view(P.sb["wb0"], 0, [[2048, 128], [256, 8], [1, 128]]))

    woutd = P.dram("woutt").ap()
    ysl = [P.sb["sc_a"][:, :].bitcast(BF16), P.sb["sc_b"][:, :].bitcast(BF16)]
    for nb in range(4):
        for kc in range(64):
            P.dma("gpsimd", view(ar[kc // 8], (kc % 8) * 1024, [[8192, 128], [1, 1024]]),
                  woutd[nb, kc])
        for tt in range(NT):
            sb = tt % 2
            P.dma("sync", view(ysl[sb], 0, [[8192, 128], [128, 64], [1, 128]]),
                  view(yfmd, (tt * 64) * 128 * 128, [[128, 128], [16384, 64], [1, 128]]))
            for half in range(2):
                pp = PS[half]
                for kc in range(64):
                    P.op("tensor", "matmul", pp[:, 0:512],
                         ysl[sb][:, kc * 128:(kc + 1) * 128],
                         view(ar[kc // 8], (kc % 8) * 1024 + half * 512, [[8192, 128], [1, 512]]),
                         start=(kc == 0), stop=(kc == 63))
            P.dma("sync", P.sb["sc_c"][:, 0:1024], gffnd[tt * 128:(tt + 1) * 128, nb * 1024:(nb + 1) * 1024])
            P.dma("sync", P.sb["sc_d"][:, 0:1024], xhd[3 + tt * 128:3 + (tt + 1) * 128,
                                                       nb * 1024:(nb + 1) * 1024])
            P.dma("sync", P.sb["sc_d"][:, 1024:2048],
                  P.dram("xnt").ap()[tt * 128:(tt + 1) * 128, nb * 1024:(nb + 1) * 1024])
            for half in range(2):
                P.op("vector", "scalar_tensor_tensor", ev[half][:, 0:512], PS[half][:, 0:512], 1.0,
                     P.sb["sc_d"][:, 1024 + half * 512:1024 + (half + 1) * 512],
                     op0=ALU.mult, op1=ALU.add)
                P.op("vector", "tensor_mul", ev[half][:, 0:512], ev[half][:, 0:512],
                     P.sb["sc_c"][:, half * 512:(half + 1) * 512])
                P.op("vector", "tensor_add", ev[2 + half][:, 0:512], ev[half][:, 0:512],
                     P.sb["sc_d"][:, half * 512:(half + 1) * 512])
                P.dma("scalar", x1d[tt * 128:(tt + 1) * 128,
                                    nb * 1024 + half * 512:nb * 1024 + (half + 1) * 512],
                      ev[2 + half][:, 0:512])


def b_l1_ln(P):
    x1d = P.dram("x1").ap()
    xt = P.sb["sc_a"]
    sqd = P.sb["sc_b"]
    xb = bfv(P, "sc_b")
    s_mu = P.sbuf("ln_mu", [128, 1], F32)
    s_nm = P.sbuf("ln_nm", [128, 1], F32)
    s_s2 = P.sbuf("ln_s2", [128, 1], F32)
    s_rs = P.sbuf("ln_rs", [128, 1], F32)
    ptr = P.psum("PT0", [128, 1024], BF16)
    for tt in range(NT):
        P.dma("sync", xt[0:128, :], x1d[tt * 128:(tt + 1) * 128, :])
        P.op("vector", "reduce_sum", s_mu[:, :], xt[:, :], axis=AX)
        P.op("vector", "tensor_scalar", s_mu[:, :], s_mu[:, :], 1.0 / D, None, op0=ALU.mult)
        P.op("vector", "tensor_scalar", s_nm[:, :], s_mu[:, :], -1.0, None, op0=ALU.mult)
        P.op("scalar", "activation", sqd[:, :], xt[:, :], AF.Square, bias=s_nm[:, :],
             accum_out=s_s2[:, :])
        P.op("vector", "tensor_scalar", s_s2[:, :], s_s2[:, :], 1.0 / D, None, op0=ALU.mult)
        P.op("scalar", "activation", s_rs[:, :], s_s2[:, :], AF.Sqrt, bias=EPSC(P))
        P.op("vector", "reciprocal", s_rs[:, :], s_rs[:, :])
        P.op("vector", "tensor_scalar", xb[:, 0:4096], xt[:, :], s_mu[:, :], s_rs[:, :],
             op0=ALU.subtract, op1=ALU.mult)
        for half in range(8):
            for j in range(4):
                kc = half * 4 + j
                P.op("tensor", "transpose", ptr[:, j * 128:(j + 1) * 128],
                     xb[:, kc * 128:(kc + 1) * 128], P.sb["c_idb"][:, :])
            dst = view(P.sb[f"ar{half}"], tt * 128, [[8192, 128], [2048, 4], [1, 128]])
            src_ = view(ptr, 0, [[1024, 128], [128, 4], [1, 128]])
            P.op("vector", "tensor_copy", dst, src_)


def b_l1_qkv(P):
    wqrd = P.dram("wqr").ap()
    wqed = P.dram("wqe").ap()
    qfd = P.dram("qf").ap()
    kvd = P.dram("kvloc").ap()
    qkd = P.dram("qkst").ap()
    wkd = P.dram("wkt").ap()
    wvd = P.dram("wvt").ap()
    wb = [P.sb["wb0"], P.sb["wb1"]]
    ev = [P.sb[f"ip_ev{i}"] for i in range(4)]
    PS = [P.sb[f"PS{i}"] for i in range(7)]
    kst = P.sbuf("l1_kst", [1, 16], F32)
    kstb = P.sbuf("l1_kstb", [1, 8], BF16)
    qsq = P.sb["ip_ev2"][:, :].bitcast(BF16)[:, 0:512]
    nrow = P.sb["ip_ev3"][0:1, 0:512]

    for kc in range(32):
        for g in range(4):
            P.dma("sync", P.sb["sc_a"][:, g * 1024:(g + 1) * 1024], wqrd[g, kc * 128:(kc + 1) * 128, :])
        P.op("vector", "tensor_add", P.sb["sc_b"][:, 0:1024], P.sb["sc_a"][:, 0:1024],
             P.sb["sc_a"][:, 1024:2048])
        P.op("vector", "tensor_add", P.sb["sc_b"][:, 1024:2048], P.sb["sc_a"][:, 2048:3072],
             P.sb["sc_a"][:, 3072:4096])
        P.op("vector", "tensor_add", P.sb["sc_b"][:, 0:1024], P.sb["sc_b"][:, 0:1024],
             P.sb["sc_b"][:, 1024:2048])
        P.op("vector", "tensor_scalar", bfv(P, "sc_b")[:, 4096:5120], P.sb["sc_b"][:, 0:1024],
             RSQD, None, op0=ALU.mult)
        P.dma("scalar", view(wqed, kc * 128 * 128, [[128, 128], [524288, 8], [1, 128]]),
              view(bfv(P, "sc_b"), 4096, [[8192, 128], [128, 8], [1, 128]]))

    P.op("vector", "memset", kst[:, :], 0.0)
    for oc in range(8):
        P.dma("gpsimd", wb[0][:, :].rearrange("p (k o) -> p k o", k=16),
              view(wqed, oc * 524288, [[128, 128], [16384, 16], [1, 128]]))
        P.dma("gpsimd", wb[1][:, :].rearrange("p (k o) -> p k o", k=16),
              view(wqed, oc * 524288 + 16 * 16384, [[128, 128], [16384, 16], [1, 128]]))
        for ts4 in range(4):
            pp = PS[ts4 % 2]
            for kc in range(32):
                P.op("tensor", "matmul", pp[:, 0:512],
                     wb[kc // 16][:, (kc % 16) * 128:(kc % 16 + 1) * 128],
                     xn_rhs(P, kc, ts4 * 512, 512), start=(kc == 0), stop=(kc == 31))
            P.op("vector", "tensor_copy", ev[ts4 % 2][:, 0:512].bitcast(BF16)[:, 0:512], pp[:, 0:512])
            P.dma("scalar", qfd[oc * 128:(oc + 1) * 128, ts4 * 512:(ts4 + 1) * 512],
                  ev[ts4 % 2][:, 0:512].bitcast(BF16)[:, 0:512])
            P.op("scalar", "activation", qsq, pp[:, 0:512], AF.Square)
            P.op("tensor", "matmul", PS[4][0:1, 0:512], P.sb["c_onec"][:, :], qsq,
                 start=True, stop=True)
            P.op("vector", "tensor_copy", nrow, PS[4][0:1, 0:512])
            P.op("vector", "reduce_max", kst[0:1, 8 + oc:9 + oc], nrow, axis=AX)
            P.op("vector", "tensor_max", kst[0:1, oc:oc + 1], kst[0:1, oc:oc + 1],
                 kst[0:1, 8 + oc:9 + oc])
    P.dma("scalar", qkd[0:1, 0:8], kst[0:1, 0:8])

    P.op("vector", "memset", kst[:, :], 0.0)
    for oc in range(8):
        P.dma("gpsimd", wb[0][:, :], wkd[oc, :, 0:16, :])
        P.dma("gpsimd", wb[1][:, :], wkd[oc, :, 16:32, :])
        for ts4 in range(4):
            pp = PS[ts4 % 2]
            for kc in range(32):
                P.op("tensor", "matmul", pp[:, 0:512],
                     wb[kc // 16][:, (kc % 16) * 128:(kc % 16 + 1) * 128],
                     xn_rhs(P, kc, ts4 * 512, 512), start=(kc == 0), stop=(kc == 31))
            P.op("vector", "tensor_copy", ev[ts4 % 2][:, 0:512].bitcast(BF16)[:, 0:512], pp[:, 0:512])
            P.dma("scalar", view(kvd, 2048 * 1024 + oc * 128 * 2048 + ts4 * 512,
                                 [[2048, 128], [1, 512]]),
                  ev[ts4 % 2][:, 0:512].bitcast(BF16)[:, 0:512])
            P.op("scalar", "activation", qsq, pp[:, 0:512], AF.Square)
            P.op("tensor", "matmul", PS[4][0:1, 0:512], P.sb["c_onec"][:, :], qsq,
                 start=True, stop=True)
            P.op("vector", "tensor_copy", nrow, PS[4][0:1, 0:512])
            P.op("vector", "reduce_max", kst[0:1, 8 + oc:9 + oc], nrow, axis=AX)
            P.op("vector", "tensor_max", kst[0:1, oc:oc + 1], kst[0:1, oc:oc + 1],
                 kst[0:1, 8 + oc:9 + oc])
    P.op("vector", "tensor_scalar", kst[0:1, 0:8], kst[0:1, 0:8], 1.02, None, op0=ALU.mult)
    P.op("vector", "tensor_copy", kstb[:, :], kst[0:1, 0:8])
    P.dma("scalar", view(kvd, 4096 * 1024, [[1024, 1], [1, 8]]), kstb[:, :])

    for oc in range(8):
        P.dma("gpsimd", wb[0][:, :], wvd[oc, :, 0:16, :])
        P.dma("gpsimd", wb[1][:, :], wvd[oc, :, 16:32, :])
        for tt in range(NT):
            pp = PS[tt % 2]
            for kc in range(32):
                P.op("tensor", "matmul", pp[:, 0:128], xn_rhs(P, kc, tt * 128, 128),
                     wb[kc // 16][:, (kc % 16) * 128:(kc % 16 + 1) * 128],
                     start=(kc == 0), stop=(kc == 31))
            P.op("vector", "tensor_copy", ev[tt % 2][:, 0:128].bitcast(BF16)[:, 0:128], pp[:, 0:128])
            P.dma("scalar", view(kvd, tt * 128 * 1024 + oc * 128, [[1024, 128], [1, 128]]),
                  ev[tt % 2][:, 0:128].bitcast(BF16)[:, 0:128])
    P.collective("AllGather", ALU.bypass, [[0, 1, 2, 3], [4, 5, 6, 7]],
                 P.dram("kvloc").ap(), P.dram("kvg").ap())
    if "kvgx" in P.ext_out:
        P.dma("sync", P.dram("kvgx").ap(), P.dram("kvg").ap())


def b_l1_ffn(P):
    g1d = P.dram("g1").ap()
    wla = P.sb["sc_a"][:, :].bitcast(BF16)
    wr = P.sb["sc_b"][:, :].bitcast(BF16)
    ws = [P.sb["sc_c"][:, :].bitcast(BF16), P.sb["sc_d"][:, :].bitcast(BF16)]
    m1 = P.sb["f_m1"] if "f_m1" in P.sb else P.sbuf("f_m1", [128, 1], F32)
    m2 = P.sb["f_m2"] if "f_m2" in P.sb else P.sbuf("f_m2", [128, 1], F32)
    zt_ = P.sbuf("s_z", [128, 8], F32)
    ke = P.sbuf("s_ke", [128, 8], F32)
    se = P.sbuf("s_se", [128, 8], F32)
    msk = P.sbuf("s_msk", [128, 8], F32)
    tau = P.sbuf("s_tau", [128, 1], F32)
    brow = P.sbuf("dt_rowt", [1, 128], F32)[0:1, :].bitcast(BF16)[0:1, 0:128]
    rrow2 = P.sbuf("s_rrow", [1, 8], BF16)
    PS = [P.sb[f"PS{i}"] for i in range(7)]
    ptr = P.sb["PT0"]
    hf = P.sb["wb0"]
    ev = [P.sb[f"ip_ev{i}"] for i in range(4)]
    hp = ev[2][:, 0:128]
    hb = ev[3][:, :].bitcast(BF16)[:, 0:128]

    P.dma("gpsimd", view(wla, 0, [[8192, 128], [128, 32], [1, 128]]),
          view(P.dram("wla1"), 0, [[128, 128], [128 * 128, 32], [1, 128]]))
    P.dma("gpsimd", view(wr, 0, [[8192, 128], [8, 32], [1, 8]]),
          view(P.dram("wr1"), 0, [[8, 128], [128 * 8, 32], [1, 8]]))

    P.dma("gpsimd", rrow2[:, :], P.dram("br1").ap())
    rb8 = P.sbuf("f_rb8", [128, 8], F32)
    bbt = P.sbuf("d_bc", [128, 128], F32)

    for tt in range(NT):
        if tt == 0:
            P.op("tensor", "matmul", PS[4][:, 16:24], P.sb["c_oner"][:, :], rrow2[:, :],
                 start=True, stop=True)
            P.op("vector", "tensor_copy", rb8[:, :], PS[4][:, 16:24])
        for kc in range(32):
            P.op("tensor", "matmul", PS[4][:, 0:8], xn_rhs(P, kc, tt * 128, 128),
                 wr[:, kc * 8:(kc + 1) * 8], start=(kc == 0), stop=(kc == 31))
        for kc in range(32):
            P.op("tensor", "matmul", PS[4][:, 32:160], xn_rhs(P, kc, tt * 128, 128),
                 wla[:, kc * 128:(kc + 1) * 128], start=False, stop=(kc == 31))
        P.op("vector", "tensor_add", zt_[:, :], PS[4][:, 0:8], rb8[:, :])
        P.op("vector", "tensor_copy", hp, PS[4][:, 32:160])
        for e in range(8):
            P.op("vector", "tensor_scalar", msk[:, :], zt_[:, :], zt_[:, e:e + 1], None, op0=ALU.is_ge)
            P.op("vector", "reduce_sum", ke[:, e:e + 1], msk[:, :], axis=AX)
            P.op("vector", "tensor_mul", msk[:, :], msk[:, :], zt_[:, :])
            P.op("vector", "reduce_sum", se[:, e:e + 1], msk[:, :], axis=AX)
        P.op("vector", "tensor_mul", ke[:, :], ke[:, :], zt_[:, :])
        P.op("vector", "scalar_tensor_tensor", msk[:, :], ke[:, :], 1.0, se[:, :],
             op0=ALU.add, op1=ALU.is_gt)
        P.op("vector", "reduce_sum", m1[:, :], msk[:, :], axis=AX)
        P.op("vector", "tensor_mul", se[:, :], msk[:, :], zt_[:, :])
        P.op("vector", "reduce_sum", m2[:, :], se[:, :], axis=AX)
        P.op("vector", "reciprocal", m1[:, :], m1[:, :])
        P.op("vector", "tensor_scalar", tau[:, :], m2[:, :], -1.0, m1[:, :], op0=ALU.add, op1=ALU.mult)
        P.op("vector", "tensor_scalar", zt_[:, :], zt_[:, :], tau[:, :], 0.0, op0=ALU.subtract, op1=ALU.max)
        wv_ = _AP(zt_[:, :].tensor, zt_[:, :].offset, [list(zt_[:, :].ap[0]), [1, 8], [0, 16]])
        P.op("vector", "tensor_tensor", hb.rearrange("p (e r) -> p e r", e=8),
             hp.rearrange("p (e r) -> p e r", e=8), wv_, op=ALU.mult)
        P.op("tensor", "transpose", ptr[:, 0:128], hb, P.sb["c_idb"][:, :])
        P.op("vector", "tensor_copy", hf[:, tt * 128:(tt + 1) * 128], ptr[:, 0:128])

    P.dma("gpsimd", view(wr, 1024, [[8192, 128], [1, 4096]]), P.dram("wlb1").ap())
    wbt = P.dram("wb1t").ap()
    for og in range(32):
        sb = og % 2
        P.dma("gpsimd", ws[sb][:, 0:4096], wbt[og])
        P.dma("gpsimd", brow, P.dram("bb1").ap()[0:1, og * 128:(og + 1) * 128])
        P.op("tensor", "matmul", PS[4][:, 256:384], P.sb["c_oner"][:, :], brow,
             start=True, stop=True)
        P.op("vector", "tensor_copy", bbt[:, :], PS[4][:, 256:384])
        for tt in range(NT):
            pp = PS[tt % 4]
            for kc in range(32):
                P.op("tensor", "matmul", pp[:, 0:128], xn_rhs(P, kc, tt * 128, 128),
                     ws[sb][:, kc * 128:(kc + 1) * 128], start=(kc == 0), stop=False)
            P.op("tensor", "matmul", pp[:, 0:128], hf[:, tt * 128:(tt + 1) * 128],
                 wr[:, 1024 + og * 128:1024 + (og + 1) * 128], start=False, stop=True)
            gev = ev[tt % 4]
            P.op("vector", "scalar_tensor_tensor", gev[:, 0:128], pp[:, 0:128], 1.0, bbt[:, :],
                 op0=ALU.mult, op1=ALU.add)
            P.op("scalar", "activation", gev[:, 0:128], gev[:, 0:128], AF.Silu)
            P.dma("scalar", g1d[tt * 128:(tt + 1) * 128, og * 128:(og + 1) * 128], gev[:, 0:128])


def b_l1_attn(P):
    gname = "kvgx" if "kvgx" in P.ext_in else "kvg"
    kvgd = P.dram(gname).ap()
    qfd = P.dram("qf").ap()
    qkd = P.dram("qkst").ap()
    attnd = P.dram("attn").ap()
    ar = [P.sb[f"ar{j}"] for j in range(8)]
    ev = [P.sb[f"ip_ev{i}"] for i in range(4)]
    PS = [P.sb[f"PS{i}"] for i in range(7)] if "PS5" in P.sb else [P.sb[f"PS{i}"] for i in range(5)]
    sa = P.sb["sc_a"]
    sab = sa[:, :].bitcast(BF16)
    sbb = P.sb["sc_b"][:, :].bitcast(BF16)
    st8 = sa[0:1, 0:48]
    nb8 = sa[:, 64:72]
    rc = sa[:, 72:73]
    rrow = P.sb["dt_rowt"] if "dt_rowt" in P.sb else P.sbuf("dt_rowt", [1, 128], F32)
    P.dma("sync", st8[0:1, 0:8], qkd[0:1, 0:8])
    for r in range(4):
        P.dma("gpsimd", st8[0:1, 16 + r * 8:24 + r * 8],
              view(kvgd, (r * 4097 + 4096) * 1024, [[8, 1], [1, 8]]))
    P.op("vector", "tensor_copy", st8[0:1, 8:16], st8[0:1, 16:24])
    for r in range(1, 4):
        P.op("vector", "tensor_max", st8[0:1, 8:16], st8[0:1, 8:16], st8[0:1, 16 + r * 8:24 + r * 8])
    P.op("vector", "tensor_mul", st8[0:1, 8:16], st8[0:1, 8:16], st8[0:1, 0:8])
    P.op("scalar", "activation", st8[0:1, 8:16], st8[0:1, 8:16], AF.Sqrt)
    P.op("vector", "tensor_scalar", st8[0:1, 8:16], st8[0:1, 8:16], -1.0, None, op0=ALU.mult)
    for h in range(8):
        P.op("tensor", "matmul", PS[4][:, 0:1], P.sb["c_onerf"][:, :], st8[0:1, 8 + h:9 + h],
             start=True, stop=True)
        P.op("vector", "tensor_copy", nb8[:, h:h + 1], PS[4][:, 0:1])
    for h in range(8):
        P.dma("sync", view(ar[h // 4], (h % 4) * 2048, [[8192, 128], [1, 2048]]),
              qfd[h * 128:(h + 1) * 128, :])
    for j in (3, 4, 6, 7):
        P.op("vector", "memset", view(ar[j], 128, [[8192, 128], [132, 32], [1, 1]]), 1.0)
    SCB = [0, 1, 5, 4]
    et = [sbb[:, 0:512], sbb[:, 512:1024], sbb[:, 1024:1536], sbb[:, 1536:2048]]

    def kv_load(h):
        hb_ = h % 2
        kb = ar[2] if hb_ == 0 else ar[5]
        for r in range(4):
            P.dma("sync", view(kb, r * 2048, [[8192, 128], [1, 2048]]),
                  view(kvgd, (r * 4097 + 2048) * 1024 + h * 128 * 2048, [[2048, 128], [1, 2048]]))
            vb = ar[3 + r // 2] if hb_ == 0 else ar[6 + r // 2]
            P.dma("sync", view(vb, (r % 2) * 16 * 132, [[8192, 128], [132, 16], [1, 128]]),
                  view(kvgd, (r * 4097) * 1024 + h * 128, [[1024, 128], [131072, 16], [1, 128]]))

    def vtile(h, kc):
        hb_ = h % 2
        vb = ar[3 + kc // 32] if hb_ == 0 else ar[6 + kc // 32]
        return view(vb, (kc % 32) * 132, [[8192, 128], [1, 129]])

    kv_load(0)
    kv_load(1)
    for h in range(8):
        hb_ = h % 2
        kb = ar[2] if hb_ == 0 else ar[5]
        for qt in range(4):
            qv = view(ar[h // 4], (h % 4) * 2048 + qt * 512, [[8192, 128], [1, 512]])

            def sc(kc):
                b = kc % 4
                P.op("tensor", "matmul", PS[SCB[b]][:, 0:512],
                     view(kb, kc * 128, [[8192, 128], [1, 128]]), qv, start=True, stop=True)
                P.op("scalar", "activation", et[b], PS[SCB[b]][:, 0:512], AF.Exp, bias=nb8[:, h:h + 1])

            sc(0)
            sc(1)
            sc(2)
            for kc in range(64):
                if kc + 3 < 64:
                    sc(kc + 3)
                b = kc % 4
                for qi in range(4):
                    P.op("tensor", "matmul", PS[2 + (qi % 2)][:, (qi // 2) * 256:(qi // 2) * 256 + 129],
                         et[b][:, qi * 128:(qi + 1) * 128], vtile(h, kc),
                         start=(kc == 0 and qi < 2), stop=(kc == 63))
            for qi in range(4):
                pp = PS[2 + (qi % 2)]
                c0 = (qi // 2) * 256
                P.op("vector", "reciprocal", rc, pp[:, c0 + 128:c0 + 129])
                P.op("vector", "tensor_scalar", ev[qi % 4][:, 0:128], pp[:, c0:c0 + 128], rc,
                     None, op0=ALU.mult)
                P.dma("scalar", attnd[qt * 512 + qi * 128:qt * 512 + (qi + 1) * 128,
                                      h * 128:(h + 1) * 128], ev[qi % 4][:, 0:128])
        if h + 2 < 8:
            kv_load(h + 2)



def b_l1_wo(P):
    attnd = P.dram("attn").ap()
    g1d = P.dram("g1").ap()
    x1d = P.dram("x1").ap()
    outd = P.dram("out").ap()
    wod = P.dram("wo").ap()
    ar = [P.sb[f"ar{j}"] for j in range(8)]
    ev = [P.sb[f"ip_ev{i}"] for i in range(4)]
    PS = [P.sb[f"PS{i}"] for i in range(7)]
    ptr = P.sb["PT0"]
    for kc in range(8):
        P.dma("gpsimd", view(ar[4 + kc // 2], (kc % 2) * 4096, [[8192, 128], [1, 4096]]),
              wod[kc * 128:(kc + 1) * 128, :])
    for tt in range(NT):
        P.dma("sync", P.sb["sc_a"][:, 0:1024], attnd[tt * 128:(tt + 1) * 128, :])
        ab = bfv(P, "sc_b")
        P.op("vector", "tensor_copy", ab[:, 0:1024], P.sb["sc_a"][:, 0:1024])
        for kc in range(8):
            P.op("tensor", "transpose", ptr[:, (kc % 8) * 128:(kc % 8 + 1) * 128],
                 ab[:, kc * 128:(kc + 1) * 128], P.sb["c_idb"][:, :])
        P.op("vector", "tensor_copy", ab[:, 1024:2048], ptr[:, 0:1024])
        for og in range(8):
            pp = PS[og % 2]
            for kc in range(8):
                P.op("tensor", "matmul", pp[:, 0:512], ab[:, 1024 + kc * 128:1024 + (kc + 1) * 128],
                     view(ar[4 + kc // 2], (kc % 2) * 4096 + og * 512, [[8192, 128], [1, 512]]),
                     start=(kc == 0), stop=(kc == 7))
            P.dma("sync", P.sb["sc_c"][:, 0:512], g1d[tt * 128:(tt + 1) * 128, og * 512:(og + 1) * 512])
            P.dma("sync", P.sb["sc_c"][:, 512:1024], x1d[tt * 128:(tt + 1) * 128, og * 512:(og + 1) * 512])
            P.op("vector", "tensor_mul", ev[og % 2][:, 0:512], pp[:, 0:512], P.sb["sc_c"][:, 0:512])
            P.op("vector", "tensor_add", ev[2 + og % 2][:, 0:512], ev[og % 2][:, 0:512],
                 P.sb["sc_c"][:, 512:1024])
            P.dma("scalar", outd[tt * 128:(tt + 1) * 128, og * 512:(og + 1) * 512],
                  ev[2 + og % 2][:, 0:512])


L0_BUILDERS = None
L1_BUILDERS = None
L0_IN = ["xh", "relmask", "c_ident", "c_ones", "c_misc", "c_trineg", "wip", "convw", "convb",
         "dtbias", "alog", "dvec", "wg0", "wla0", "wlb0a", "wlb0b", "wb0t", "woutt", "nwt"]
L1_IN = ["c_ident", "c_ones", "c_misc", "wqr", "wkt", "wvt", "wo", "wb1t", "bb1",
         "wr1", "br1", "wla1", "wlb1"]
FUSED = True


def kernel(**inputs):
    l0b = [b_l0_ln, b_l0_inproj, b_l0_dt, b_l0_conv, b_l0_ffn, b_l0_ssd, b_l0_post]
    l1b = [b_l1_ln, b_l1_qkv, b_l1_ffn, b_l1_attn, b_l1_wo]
    W = host_weights(inputs)
    pc = [host_percore(inputs, c) for c in range(8)]
    out = np.zeros((2, SEQL if False else 8192, D), dtype=np.float32)
    if FUSED:
        nc = build_launch(l0b + l1b, ext_in=L0_IN + L1_IN, ext_out=["out"])
        maps = [{("t_" + n): (W[n] if n in W else pc[c][n]) for n in L0_IN + L1_IN}
                for c in range(8)]
        res = run_bass_kernel_spmd(nc, maps, core_ids=list(range(8)))
        for c in range(8):
            out[c // 4, (c % 4) * TPC:(c % 4 + 1) * TPC] = res.results[c]["t_out"]
        return out
    nc0 = build_launch(l0b, ext_in=L0_IN, ext_out=["x1"])
    maps0 = [{("t_" + n): (W[n] if n in W else pc[c][n]) for n in L0_IN} for c in range(8)]
    res0 = run_bass_kernel_spmd(nc0, maps0, core_ids=list(range(8)))
    nc1 = build_launch(l1b, ext_in=["x1"] + L1_IN, ext_out=["out"])
    maps1 = []
    for c in range(8):
        m = {"t_x1": np.ascontiguousarray(res0.results[c]["t_x1"])}
        for n in L1_IN:
            m["t_" + n] = W[n]
        maps1.append(m)
    res1 = run_bass_kernel_spmd(nc1, maps1, core_ids=list(range(8)))
    for c in range(8):
        out[c // 4, (c % 4) * TPC:(c % 4 + 1) * TPC] = res1.results[c]["t_out"]
    return out
